# version 05
# baseline (speedup 1.0000x reference)
import contextlib
import numpy as np
import ml_dtypes

import concourse.bass as bass
import concourse.mybir as mybir
from concourse.bass_utils import run_bass_kernel_spmd

F32 = mybir.dt.float32
BF16 = mybir.dt.bfloat16
AF = mybir.ActivationFunctionType
ALU = mybir.AluOpType

D = 1024
DC = 8
DI = 2048
DIC = 16
NH = 32
DZ = 4384
HF = 1408
HFC = 11
NE = 8
EPS = 1e-5
CH = 128

ENGINES = ("sync", "scalar", "vector", "tensor", "gpsimd")
BF = ml_dtypes.bfloat16


class Builder:
    def __init__(self, nc):
        self.nc = nc
        self.ops = {e: [] for e in ENGINES}
        self.sems = {}
        self.waited = {e: {} for e in ENGINES}
        self.es = contextlib.ExitStack()
        self.stacks = []

    def sem(self, name, eng):
        h = self.es.enter_context(self.nc.semaphore(name))
        self.sems[name] = [h, 0, eng]

    UNSAFE_SEMS = ("sV", "sS", "sP", "sT")

    def op(self, eng, fn, inc=None, wait=()):
        need_delay = False
        for w in wait:
            if w is None:
                continue
            s, v = w[0], w[1]
            safe = len(w) > 2 and w[2] == "safe"
            if self.sems[s][2] == eng:
                continue
            if self.waited[eng].get(s, -1) >= v:
                continue
            self.waited[eng][s] = v
            h = self.sems[s][0]
            self.ops[eng].append(lambda e, h=h, v=v: e.wait_ge(h, v))
            if s in self.UNSAFE_SEMS and not safe:
                need_delay = True
        if need_delay:
            self.ops[eng].append(lambda e: e.nop(cycle_cnt=400))
        if inc is None:
            self.ops[eng].append(fn)
            return None
        s, k = inc
        rec = self.sems[s]
        rec[1] += k
        h = rec[0]
        self.ops[eng].append(lambda e, fn=fn, h=h, k=k: fn(e).then_inc(h, k))
        return (s, rec[1])

    def tok(self, s):
        return (s, self.sems[s][1])

    def barrier(self, engines=("scalar", "vector", "tensor", "gpsimd")):
        toks = []
        for e in engines:
            s = "bar_" + e
            rec = self.sems[s]
            rec[1] += 1
            h, v = rec[0], rec[1]
            self.ops[e].append(lambda eng, h=h: eng.sem_inc(h, 1))
            toks.append((s, v))
        for e in engines:
            for (s, v) in toks:
                if self.sems[s][2] == e:
                    continue
                self.waited[e][s] = v
                h = self.sems[s][0]
                self.ops[e].append(lambda eng, h=h, v=v: eng.wait_ge(h, v))

    def sb(self, name, shape, dtype, es=None, side=None):
        kw = {"side": side} if side else {}
        return (es or self.es).enter_context(
            self.nc.sbuf_tensor(name, shape, dtype, **kw))

    def psum(self, name, shape, dtype):
        return self.es.enter_context(self.nc.psum_tensor(name, shape, dtype))

    def finalize(self, block):
        for eng in ENGINES:
            opl = self.ops[eng]

            def make(opl):
                def _(e):
                    for f in opl:
                        f(e)
                return _
            getattr(block, eng)(make(opl))


class Bank:
    def __init__(self, t):
        self.t = t
        self.w = None
        self.r = []

    def ww(self):
        out = [x for x in self.r if x is not None]
        if self.w is not None:
            out.append(self.w)
        self.r = []
        return out

    def rw(self):
        return [self.w] if self.w is not None else []


def build_core(T=2048, taps=(), debug_stop=None):
    NTT = T // 512
    NCHK = T // CH
    TH = T // 2
    taps = set(taps)
    nc = bass.Bass("TRN2", target_bir_lowering=False)
    b = Builder(nc)

    def din(name, shape, dtype=F32):
        return nc.dram_tensor(name, shape, dtype, kind="ExternalInput").ap()

    xT = din("xT", [D, T])
    wqT = din("wqT", [D, D])
    wzT = din("wzT", [D, DZ])
    woT = din("woT", [DI, D])
    convw_d = din("convw_d", [128, 18, 4])
    convb_d = din("convb_d", [128, 18])
    dtb_d = din("dtb_d", [128, 1])
    alog_d = din("alog_d", [128, 1])
    drep_d = din("drep_d", [128, 16])
    npw_d = din("npw_d", [128, DC])
    ngw_d = din("ngw_d", [128, DIC])
    gw_d = din("gw_d", [D, NE])
    gb_d = din("gb_d", [128, 16 * NE])
    w1e = din("w1e", [NE, D, HF])
    w3e = din("w3e", [NE, D, HF])
    w2e = din("w2e", [NE, HF, D])
    b1_d = din("b1_d", [NE, 128, HFC])
    b3_d = din("b3_d", [NE, 128, HFC])
    b2_d = din("b2_d", [128, D])
    sw1_d = din("sw1_d", [D, HF])
    sw3_d = din("sw3_d", [D, HF])
    sw2_d = din("sw2_d", [HF, D])
    sb1_d = din("sb1_d", [128, HFC])
    sb3_d = din("sb3_d", [128, HFC])
    sb2_d = din("sb2_d", [128, DC])
    identb_d = din("identb_d", [128, 128], BF16)
    identf_d = din("identf_d", [128, 128], F32)
    ropeC_d = din("ropeC_d", [128, T])
    ropeS_d = din("ropeS_d", [128, T])
    maskneg_d = din("maskneg_d", [128, 128], BF16)
    selrw_d = din("selrw_d", [128, NE * 128], BF16)

    out_fm = nc.dram_tensor("out_fm", [D, T], F32, kind="ExternalOutput").ap()
    qrot_scr = nc.dram_tensor("qrot_scr", [D, T], BF16, kind="Internal").ap()
    x1_scr = nc.dram_tensor("x1_scr", [D, T], F32, kind="Internal").ap()
    outa_scr = nc.dram_tensor("outa_scr", [D, T], BF16, kind="Internal").ap()
    ymoe_scr = nc.dram_tensor("ymoe_scr", [D, T], F32, kind="Internal").ap()
    acs_scr = nc.dram_tensor("acs_scr", [NH, T], F32, kind="Internal").ap()

    tap_out = {}

    def mktap(name, shape, dt):
        if name in taps:
            tap_out[name] = nc.dram_tensor("dbg_" + name, shape, dt,
                                           kind="ExternalOutput").ap()

    mktap("xn", [128, DC, T], BF16)
    mktap("rows", [128, T], F32)
    mktap("qrot", [128, DC, T], BF16)
    mktap("hn", [128, DC, T], BF16)
    mktap("dt", [NH, T], F32)
    mktap("dtraw", [NH, T], F32)
    mktap("adt", [NH, T], F32)
    mktap("acol", [NH, 2], F32)
    mktap("convraw", [128, T], BF16)
    mktap("acs", [NH, T], F32)
    mktap("bfm", [128, T], BF16)
    mktap("cfm", [128, T], BF16)
    mktap("cbtm", [128, T], BF16)
    mktap("btok", [128, T], BF16)
    mktap("acstok", [128, NCHK * 32], F32)
    mktap("dttok", [128, NCHK * 32], BF16)
    mktap("dcytok", [128, NCHK * 32], BF16)
    mktap("earep", [128, NCHK * 32], F32)
    mktap("sz0", [128, T], BF16)
    mktap("xs0", [128, T], BF16)
    mktap("yg", [128, DIC, T], BF16)
    mktap("outa", [128, DC, T], BF16)
    mktap("rgb", [128, T], BF16)
    mktap("pout", [128, T], BF16)
    mktap("wdump", [128, 1024], BF16)
    mktap("oarall", [128, DC, T], BF16)
    mktap("qredump", [128, T], BF16)
    mktap("qscr", [128, T], BF16)
    mktap("qre0", [128, T], BF16)
    mktap("x1", [128, DC, T], F32)
    mktap("xf", [128, DC, TH], BF16)
    mktap("rw", [128, (TH // 128) * NE], F32)
    mktap("glog", [128, (TH // 128) * NE], F32)
    mktap("gexp", [128, (TH // 128) * NE], F32)
    mktap("gmx", [128, TH // 128], F32)
    mktap("canary", [128, 8], F32)
    mktap("he", [128, HFC, TH], BF16)
    mktap("ymoe", [128, DC, TH], F32)

    def tap(name, ap, wtok):
        if name in tap_out:
            b.op("scalar", lambda e, ap=ap, name=name: e.dma_start(
                out=tap_out[name], in_=ap), inc=("sQ", 16),
                wait=[wtok] if wtok else [])

    b.sem("sW", None)
    b.sem("sQ", None)
    b.sem("sG", None)
    for i in range(4):
        b.sem(f"sg{i}", None)
    b.sem("sge", None)
    b.sem("sO", None)
    b.sem("sV", "vector")
    b.sem("sS", "scalar")
    b.sem("sT", "tensor")
    b.sem("sP", "gpsimd")
    for e in ("scalar", "vector", "tensor", "gpsimd"):
        b.sem("bar_" + e, e)

    pb = [Bank(b.psum(f"pb{i}", [128, 512], F32)) for i in range(7)]
    pbT = Bank(b.psum("pbT", [128, 1024], BF16))

    cpool = b.sb("cpool", [128, 8], F32)
    CP_ONE, CP_NEG1, CP_NEGH, CP_EPS, CP_I1024, CP_I2048 = range(6)
    ones_b = b.sb("ones_b", [128, 1], BF16)
    ones_f = b.sb("ones_f", [1, 128], F32)
    onesF = b.sb("onesF", [128, 128], F32)
    identb = b.sb("identb", [128, 128], BF16)
    identf = b.sb("identf", [128, 128], F32)
    masks = b.sb("masks", [128, 128], BF16)
    rbB = b.sb("rbB", [128, T], F32)
    rbC = b.sb("rbC", [128, T], F32)
    rsc = b.sb("rsc", [1, 2048], F32)
    convw_s = b.sb("convw_s", [128, 72], F32)
    convb_s = b.sb("convb_s", [128, 18], F32)
    dtb_s = b.sb("dtb_s", [NH, 1], F32)
    alog_s = b.sb("alog_s", [NH, 1], F32)
    drep_s = b.sb("drep_s", [128, 16], F32)
    npw_s = b.sb("npw_s", [128, DC], F32)
    ngw_s = b.sb("ngw_s", [128, DIC], F32)
    canary = b.sb("canary", [128, 8], F32)
    wstg = [None, None]
    wbf = [None, None, None]
    sqb = [b.sb(f"sqb{i}", [128, 512], BF16) for i in range(4)]

    def realloc_wbufs(es, width, nbf=2):
        gv, gt = b.tok("sV"), b.tok("sT")
        for i in range(2):
            wstg[i] = b.sb(f"wstg{i}_{width}_{b.sems['sV'][1]}_{b.sems['sT'][1]}",
                           [128, width], F32, es)
            wstg_tok[i] = gv
        for i in range(3):
            wbf[i] = None
            wbf_tok[i] = gt
        for i in range(nbf):
            wbf[i] = b.sb(f"wbf{i}_{width}_{b.sems['sV'][1]}_{b.sems['sT'][1]}",
                          [128, width], BF16, es)

    for i, v in enumerate([1.0, -1.0, -0.5, EPS, 1.0 / 1024, 1.0 / 2048]):
        b.op("vector", lambda e, i=i, v=v: e.memset(cpool[:, i:i + 1], v))
    b.op("vector", lambda e: e.memset(ones_b[:], 1.0))
    b.op("vector", lambda e: e.memset(ones_f[:], 1.0))
    t_init = b.op("vector", lambda e: e.memset(onesF[:], 1.0), inc=("sV", 1))

    for dst, src in [(identb, identb_d), (identf, identf_d), (masks, maskneg_d),
                     (convb_s, convb_d), (dtb_s, dtb_d[0:NH]), (alog_s, alog_d[0:NH]),
                     (drep_s, drep_d), (npw_s, npw_d), (ngw_s, ngw_d)]:
        b.op("sync", lambda e, d=dst, s=src: e.dma_start(out=d[:], in_=s),
             inc=("sW", 16))
    b.op("sync", lambda e: e.dma_start(
        out=convw_s[:].rearrange("p (j k) -> p j k", j=18), in_=convw_d),
        inc=("sW", 16))
    t_consts = b.tok("sW")

    def row(rb, slot, lo=0, hi=T):
        return rb[slot:slot + 1, lo:hi]

    wstg_tok = [None, None]
    wbf_tok = [None, None, None]

    def stream_tile(a, sg, bf, kc, mj=128, scale_base=None):
        stg, wb = wstg[sg], wbf[bf]
        td = b.op("sync", lambda e, stg=stg, kc=kc, mj=mj, a=a: e.dma_start(
            out=stg[:, :kc * mj].rearrange("p (c m) -> p c m", c=kc),
            in_=a.rearrange("(c p) m -> p c m", p=128)),
            inc=("sW", 16), wait=[wstg_tok[sg]])
        if scale_base is None:
            tc = b.op("gpsimd", lambda e, stg=stg, wb=wb, kc=kc, mj=mj:
                      e.tensor_copy(
                out=wb[:, :kc * mj], in_=stg[:, :kc * mj]),
                inc=("sP", 1), wait=[td, wbf_tok[bf]])
            wstg_tok[sg] = tc
            return relay(tc)
        else:
            tc = None
            for c in range(kc):
                tc = b.op("gpsimd", lambda e, stg=stg, wb=wb, c=c, mj=mj:
                          e.tensor_scalar(
                    out=wb[:, c * mj:(c + 1) * mj],
                    in0=stg[:, c * mj:(c + 1) * mj],
                    scalar1=ngw_s[:, scale_base + c:scale_base + c + 1],
                    scalar2=None, op0=ALU.mult), inc=("sP", 1),
                    wait=[td, t_consts, wbf_tok[bf]] if c == 0 else [])
        wstg_tok[sg] = tc
        return relay(tc)

    def guard():
        return [b.tok("sV"), b.tok("sT"), b.tok("sS"), b.tok("sP")]

    def vdelay():
        b.op("vector", lambda e: e.nop(cycle_cnt=400))

    def relay(tok):
        t = b.op("gpsimd", lambda e: e.engine_nop() if hasattr(e, "engine_nop")
                 else e.nop(), inc=("sP", 1), wait=[tok])
        return (t[0], t[1], "safe")

    def mm_acc(bank, lhs_fn, rhs_fn, nk, mj=128, fw=()):
        ws = bank.ww()
        tok = None
        for c in range(nk):
            lh, rh = lhs_fn(c), rhs_fn(c)
            tok = b.op("tensor", lambda e, lh=lh, rh=rh, c=c, mj=mj, bank=bank:
                       e.matmul(
                out=bank.t[0:mj, 0:512], lhsT=lh, rhs=rh,
                start=(c == 0), stop=(c == nk - 1)),
                inc=("sT", 1) if c == nk - 1 else None,
                wait=(ws + [w for w in fw]) if c == 0 else [])
        bank.w = tok
        return tok

    def bcast_row(bank, src_row, wt):
        ws = bank.ww()
        bp = src_row.base_partition()
        lh = onesF[bp:bp + 1, :]
        t = b.op("tensor", lambda e, src_row=src_row, lh=lh: e.matmul(
            out=bank.t[:, 0:512], lhsT=lh, rhs=src_row,
            start=True, stop=True), inc=("sT", 1),
            wait=ws + [w for w in wt] + [t_init])
        bank.w = t
        return t

    sq_rd = [None] * 4

    def stats_tt(src3, nch, tt, sum_bank, ssq_bank, src_tok):
        ws = sum_bank.ww()
        wq = ssq_bank.ww()
        tok_s = tok_q = None
        for c0 in range(0, nch, 4):
            cb = min(4, nch - c0)
            tsq = None
            for ci in range(cb):
                c = c0 + ci
                sq = sqb[c % 4]
                tsq = b.op("vector", lambda e, c=c, tt=tt, sq=sq: e.tensor_tensor(
                    out=sq[:], in0=src3[:, c, tt * 512:(tt + 1) * 512],
                    in1=src3[:, c, tt * 512:(tt + 1) * 512], op=ALU.mult),
                    inc=("sV", 1), wait=[src_tok, sq_rd[c % 4]])
            for ci in range(cb):
                c = c0 + ci
                sq = sqb[c % 4]
                tok_s = b.op("tensor", lambda e, c=c, tt=tt: e.matmul(
                    out=sum_bank.t[0:1, 0:512], lhsT=ones_b[:],
                    rhs=src3[:, c, tt * 512:(tt + 1) * 512],
                    start=(c == 0), stop=(c == nch - 1), skip_group_check=True),
                    inc=("sT", 1) if c == nch - 1 else None,
                    wait=(ws + [src_tok, t_init]) if c == 0 else [])
                tok_q = b.op("tensor", lambda e, c=c, sq=sq: e.matmul(
                    out=ssq_bank.t[0:1, 0:512], lhsT=ones_b[:], rhs=sq[:],
                    start=(c == 0), stop=(c == nch - 1), skip_group_check=True),
                    inc=("sT", 1),
                    wait=(wq + [t_init, tsq]) if c == 0 else [tsq])
                sq_rd[c % 4] = tok_q
        sum_bank.w = tok_s
        ssq_bank.w = tok_q
        return tok_s, tok_q

    rb_war = {}
    RV = rsc[0:1, 0:512]
    RM2 = rsc[0:1, 512:1024]
    RM = rsc[0:1, 1024:1536]
    RI = rsc[0:1, 1536:2048]

    def rows_span(src_sum, src_ssq, w, inv_dst, minv_dst, scale_idx,
                  want_mean, wt=(), keep_ri=False):
        tm = None
        if want_mean:
            tm = b.op("vector", lambda e: e.tensor_scalar(
                out=RM[:, 0:w], in0=src_sum,
                scalar1=cpool[0:1, scale_idx:scale_idx + 1], scalar2=None,
                op0=ALU.mult), inc=("sV", 1), wait=list(wt))
        tv = b.op("vector", lambda e: e.tensor_scalar(
            out=RV[:, 0:w], in0=src_ssq,
            scalar1=cpool[0:1, scale_idx:scale_idx + 1], scalar2=None,
            op0=ALU.mult), inc=("sV", 1),
            wait=list(wt) + [rb_war.pop("RV", None)])
        if want_mean:
            b.op("vector", lambda e: e.tensor_tensor(
                out=RM2[:, 0:w], in0=RM[:, 0:w], in1=RM[:, 0:w],
                op=ALU.mult), inc=("sV", 1), wait=[rb_war.pop("RM2", None)])
            tv = b.op("vector", lambda e: e.tensor_tensor(
                out=RV[:, 0:w], in0=RV[:, 0:w], in1=RM2[:, 0:w],
                op=ALU.subtract), inc=("sV", 1))
        tl = b.op("scalar", lambda e: e.activation(
            out=RM2[:, 0:w], in_=RV[:, 0:w], func=AF.Ln,
            bias=cpool[0:1, CP_EPS:CP_EPS + 1], scale=1.0), inc=("sS", 1),
            wait=[tv])
        rb_war["RV"] = tl
        ti = b.op("scalar", lambda e: e.activation(
            out=RI[:, 0:w], in_=RM2[:, 0:w], func=AF.Exp,
            scale=cpool[0:1, CP_NEGH:CP_NEGH + 1]), inc=("sS", 1),
            wait=[rb_war.pop("RI", None)])
        rb_war["RM2"] = ti
        rb_, slot, lo, hi = inv_dst
        tic = b.op("vector", lambda e, rb_=rb_, slot=slot, lo=lo, hi=hi:
                   e.tensor_copy(out=rb_[slot:slot + 1, lo:hi],
                                 in_=RI[:, 0:w]), inc=("sV", 1),
            wait=[ti, rb_war.pop((id(rb_), slot), None)])
        tmi = None
        if want_mean:
            rb2, slot2, lo2, hi2 = minv_dst
            tmi = b.op("vector", lambda e, rb2=rb2, slot2=slot2, lo2=lo2,
                       hi2=hi2: e.tensor_tensor(
                out=rb2[slot2:slot2 + 1, lo2:hi2], in0=RM[:, 0:w],
                in1=RI[:, 0:w], op=ALU.mult), inc=("sV", 1),
                wait=[ti, rb_war.pop((id(rb2), slot2), None)])
        rb_war["RI"] = tmi if tmi is not None else tic
        return tic, tmi

    def mark_row_read(rb_id, slot, tok):
        rb_war[(rb_id, slot)] = tok

    pH = contextlib.ExitStack()
    b.stacks.append(pH)
    hn = b.sb("hn", [128, DC, T], BF16, pH, side="right")
    p0 = contextlib.ExitStack()
    b.stacks.append(p0)
    ropC = b.sb("ropC", [128, T], F32, p0)
    ropS = b.sb("ropS", [128, T], F32, p0)
    xn = b.sb("xn", [128, DC, T], BF16, p0)
    qrot = b.sb("qrot", [128, DC, T], BF16, p0)
    xstg = [b.sb(f"xstg{i}", [128, T], F32, p0) for i in range(2)]
    xtmp = b.sb("xtmp", [128, 512], F32, p0)
    rtA = b.sb("rtA", [128, 512], F32, p0)
    rtB = b.sb("rtB", [128, 512], F32, p0)
    realloc_wbufs(p0, 1024)
    xbf = xn

    b.op("sync", lambda e: e.dma_start(out=ropC[:], in_=ropeC_d), inc=("sW", 16))
    t_ropes = b.op("sync", lambda e: e.dma_start(out=ropS[:], in_=ropeS_d),
                   inc=("sW", 16))

    xstg_tok = [None, None]
    for c in range(DC):
        td = b.op("sync", lambda e, c=c: e.dma_start(
            out=xstg[c % 2][:], in_=xT[c * 128:(c + 1) * 128, :]),
            inc=("sW", 16), wait=[xstg_tok[c % 2]])
        tc = b.op("vector", lambda e, c=c: e.tensor_copy(
            out=xbf[:, c, :], in_=xstg[c % 2][:]), inc=("sV", 1), wait=[td])
        xstg_tok[c % 2] = tc
    t_xbf = b.tok("sV")

    for tt in range(NTT):
        lo, hi = tt * 512, (tt + 1) * 512
        stats_tt(xbf, DC, tt, pb[4], pb[5], t_xbf)
        ti, tmi = rows_span(pb[4].t[0:1, 0:512], pb[5].t[0:1, 0:512], 512,
                            (rbB, 0, lo, hi), (rbB, 32, lo, hi), CP_I1024,
                            True, wt=pb[4].rw() + pb[5].rw())
        pb[4].r.append(ti)
        pb[5].r.append(ti)
        tb0 = bcast_row(pb[0], row(rbB, 0, lo, hi), [ti])
        mark_row_read(id(rbB), 0, tb0)
        tb1 = bcast_row(pb[1], row(rbB, 32, lo, hi), [tmi])
        mark_row_read(id(rbB), 32, tb1)
        for c in range(DC):
            b.op("vector", lambda e, c=c, lo=lo, hi=hi: e.tensor_tensor(
                out=xtmp[:], in0=xbf[:, c, lo:hi], in1=pb[0].t[:, 0:512],
                op=ALU.mult), inc=("sV", 1), wait=pb[0].rw())
            pb[0].r.append(b.tok("sV"))
            b.op("vector", lambda e, c=c, lo=lo, hi=hi: e.tensor_tensor(
                out=xn[:, c, lo:hi], in0=xtmp[:], in1=pb[1].t[:, 0:512],
                op=ALU.subtract), inc=("sV", 1), wait=pb[1].rw())
            pb[1].r.append(b.tok("sV"))
    t_xn = b.tok("sV")
    tap("xn", xn[:], t_xn)
    tap("rows", rbB[:], t_xn)

    for j in range(DC):
        sg, bf = j % 2, j % 2
        tc = stream_tile(wqT[:, j * 128:(j + 1) * 128], sg, bf, DC)
        tq = None
        for tt in range(NTT):
            lo, hi = tt * 512, (tt + 1) * 512
            bank = pb[2 + tt % 2]
            mm_acc(bank, lambda c, bf=bf: wbf[bf][:, c * 128:(c + 1) * 128],
                   lambda c, lo=lo, hi=hi: xn[:, c, lo:hi], DC,
                   fw=[tc, t_xn])
            b.op("vector", lambda e, lo=lo, hi=hi, bank=bank: e.tensor_tensor(
                out=rtA[:], in0=bank.t[:, 0:512], in1=ropC[:, lo:hi],
                op=ALU.mult), inc=("sV", 1), wait=bank.rw() + [t_ropes])
            bank.r.append(b.tok("sV"))
            for blk in range(4):
                b.op("vector", lambda e, blk=blk, lo=lo, hi=hi, bank=bank:
                     e.tensor_tensor(
                    out=rtB[blk * 32:(blk + 1) * 32, :],
                    in0=bank.t[(blk ^ 1) * 32:((blk ^ 1) + 1) * 32, 0:512],
                    in1=ropS[blk * 32:(blk + 1) * 32, lo:hi], op=ALU.mult),
                    inc=("sV", 1) if blk == 3 else None)
            bank.r.append(b.tok("sV"))
            tq = b.op("vector", lambda e, j=j, lo=lo, hi=hi: e.tensor_tensor(
                out=qrot[:, j, lo:hi], in0=rtA[:], in1=rtB[:], op=ALU.add),
                inc=("sV", 1))
        wbf_tok[bf] = tq
        b.op("scalar", lambda e, j=j: e.dma_start(
            out=qrot_scr[j * 128:(j + 1) * 128, :], in_=qrot[:, j, :]),
            inc=("sQ", 16), wait=[tq])
    t_qrot = b.tok("sV")
    tap("qrot", qrot[:], t_qrot)

    for tt in range(NTT):
        lo, hi = tt * 512, (tt + 1) * 512
        stats_tt(qrot, DC, tt, pb[4], pb[5], t_qrot)
        ti, _ = rows_span(None, pb[5].t[0:1, 0:512], 512,
                          (rbB, 64, lo, hi), None, CP_I1024, False,
                          wt=pb[5].rw())
        pb[5].r.append(ti)
        pb[4].r.append(ti)
        tb0 = bcast_row(pb[0], row(rbB, 64, lo, hi), [ti])
        mark_row_read(id(rbB), 64, tb0)
        for c in range(DC):
            b.op("vector", lambda e, c=c, lo=lo, hi=hi: e.tensor_tensor(
                out=xtmp[:], in0=qrot[:, c, lo:hi], in1=pb[0].t[:, 0:512],
                op=ALU.mult), inc=("sV", 1), wait=pb[0].rw())
            pb[0].r.append(b.tok("sV"))
            b.op("vector", lambda e, c=c, lo=lo, hi=hi: e.tensor_scalar(
                out=hn[:, c, lo:hi], in0=xtmp[:], scalar1=npw_s[:, c:c + 1],
                scalar2=None, op0=ALU.mult), inc=("sV", 1), wait=[t_consts])
    t_hn = b.tok("sV")
    tap("hn", hn[:], t_hn)

    if debug_stop == "p0":
        return finish(b, nc)
    b.barrier()
    p0.close()

    pS = contextlib.ExitStack()
    b.stacks.append(pS)
    acs_tok = b.sb("acs_tok", [128, NCHK * 32], F32, pS, side="right")
    dt_tok = b.sb("dt_tok", [128, NCHK * 32], BF16, pS, side="right")
    dcy_tok = b.sb("dcy_tok", [128, NCHK * 32], BF16, pS, side="right")
    ea_rep = b.sb("ea_rep", [128, NCHK * 32], F32, pS, side="right")
    b_tok = b.sb("b_tok", [128, T], BF16, pS, side="right")
    c_fm = b.sb("c_fm", [128, T], BF16, pS, side="right")
    cbtm = b.sb("cbtm", [128, T], BF16, pS, side="right")

    p1 = contextlib.ExitStack()
    b.stacks.append(p1)
    acs_hm = b.sb("acs_hm", [NH, T], F32, p1)
    realloc_wbufs(p1, 1024)
    convbuf = b.sb("convbuf", [128, T + 4], BF16, p1)
    convac = [b.sb(f"convac{i}", [128, T], BF16, p1) for i in range(2)]
    bfm = b.sb("bfm", [128, T], BF16, p1)
    dt_f = b.sb("dt_f", [NH, T], F32, p1)
    adt_f = b.sb("adt_f", [NH, T], F32, p1)
    dtraw = b.sb("dtraw", [NH, T], F32, p1)
    dtt1 = b.sb("dtt1", [NH, T], F32, p1)
    zeros32 = b.sb("zeros32", [NH, CH], F32, p1)
    acolt = b.sb("acolt", [NH, 2], F32, p1)
    ea_all = b.sb("ea_all", [NH, NCHK], F32, p1)
    ea_t = b.sb("ea_t", [NCHK, NH], F32, p1)
    earow = [b.sb(f"earow{i}", [1, NH], F32, p1) for i in range(2)]
    dtmp = b.sb("dtmp", [NH, CH], F32, p1)

    b.op("vector", lambda e: e.memset(zeros32[:], 0.0))
    t_cb0 = b.op("vector", lambda e: e.memset(convbuf[:, 0:4], 0.0),
                 inc=("sV", 1))

    def mm_hn(bank, bf, tt, mj=128, fw=()):
        return mm_acc(bank,
                      lambda c, bf=bf, mj=mj: wbf[bf][:, c * mj:(c + 1) * mj],
                      lambda c, tt=tt: hn[:, c, tt * 512:(tt + 1) * 512],
                      DC, mj=mj, fw=list(fw) + [t_hn])

    def conv_silu(dst_ap, chblk, ev_tok, war_tok, cbuf, cac, cb0, war2=None):
        tcv = None
        for k in range(4):
            vk = cbuf[:, 1 + k:1 + k + T]
            wk = convw_s[:, chblk * 4 + k:chblk * 4 + k + 1]
            if k == 0:
                tcv = b.op("vector", lambda e, vk=vk, wk=wk, chblk=chblk:
                           e.tensor_scalar(
                    out=cac[0][:], in0=vk, scalar1=wk,
                    scalar2=convb_s[:, chblk:chblk + 1], op0=ALU.mult,
                    op1=ALU.add), inc=("sV", 1),
                    wait=[ev_tok, t_consts, cb0, war_tok])
            else:
                tcv = b.op("vector", lambda e, vk=vk, wk=wk, k=k:
                           e.scalar_tensor_tensor(
                    out=cac[k % 2][:], in0=vk, scalar=wk,
                    in1=cac[(k - 1) % 2][:], op0=ALU.mult, op1=ALU.add),
                    inc=("sV", 1), wait=[war2] if k == 1 else [])
        return b.op("scalar", lambda e, dst_ap=dst_ap: e.activation(
            out=dst_ap, in_=cac[1][:], func=AF.Silu), inc=("sS", 1),
            wait=[tcv, war_tok])

    bc_tok = {}
    for idx, jt in enumerate((32, 33)):
        sg, bf = idx % 2, idx % 2
        tc = stream_tile(wzT[:, jt * 128:(jt + 1) * 128], sg, bf, DC)
        tev = None
        for tt in range(NTT):
            bank = pb[2 + tt % 2]
            mm_hn(bank, bf, tt, fw=[tc])
            tev = b.op("vector", lambda e, tt=tt, bank=bank: e.tensor_copy(
                out=convbuf[:, 4 + tt * 512:4 + (tt + 1) * 512],
                in_=bank.t[:, 0:512]), inc=("sV", 1), wait=bank.rw())
            bank.r.append(tev)
        wbf_tok[bf] = tev
        dst = bfm if jt == 32 else c_fm
        bc_tok[jt] = conv_silu(dst[:], 16 + idx, tev, None, convbuf, convac,
                               t_cb0, war2=bc_tok.get(32) if jt == 33 else None)
        if jt == 32:
            tap("convraw", convac[1][:], bc_tok[32])
    tap("bfm", bfm[:], bc_tok[32])
    tap("cfm", c_fm[:], bc_tok[33])

    def transpose_blocks(dst_sb, src_fn, nblk, src_tok, blkw=128):
        last = None
        per_bank = 1024 // blkw
        for g0 in range(0, nblk, per_bank):
            gn = min(per_bank, nblk - g0)
            ws = pbT.ww()
            tok = None
            for i in range(gn):
                c = g0 + i
                tok = b.op("tensor", lambda e, c=c, i=i, blkw=blkw: e.transpose(
                    out=pbT.t[:, i * blkw:(i + 1) * blkw], in_=src_fn(c),
                    identity=identb[:]),
                    inc=("sT", 1) if i == gn - 1 else None,
                    wait=(ws + [src_tok, t_consts]) if i == 0 else [])
            pbT.w = tok
            last = b.op("vector", lambda e, g0=g0, gn=gn, blkw=blkw: e.tensor_copy(
                out=dst_sb[:, g0 * blkw:(g0 + gn) * blkw],
                in_=pbT.t[:, 0:gn * blkw]), inc=("sV", 1), wait=pbT.rw())
            pbT.r.append(last)
        return last

    t_btok = transpose_blocks(b_tok, lambda c: bfm[:, c * 128:(c + 1) * 128],
                              NCHK, bc_tok[32])
    tap("btok", b_tok[:], t_btok)

    t_cbtm = None
    for g0 in range(0, NCHK, 4):
        bank = pb[2 + (g0 // 4) % 2]
        ws = bank.ww()
        tok = None
        for i in range(4):
            c = g0 + i
            tok = b.op("tensor", lambda e, c=c, i=i, bank=bank: e.matmul(
                out=bank.t[:, i * 128:(i + 1) * 128],
                lhsT=bfm[:, c * 128:(c + 1) * 128],
                rhs=c_fm[:, c * 128:(c + 1) * 128], start=True, stop=True,
                skip_group_check=True),
                inc=("sT", 1) if i == 3 else None,
                wait=(ws + [bc_tok[32], bc_tok[33]]) if i == 0 else [])
        bank.w = tok
        t_cbtm = b.op("vector", lambda e, g0=g0, bank=bank: e.tensor_copy(
            out=cbtm[:, g0 * 128:(g0 + 4) * 128], in_=bank.t[:, 0:512]),
            inc=("sV", 1), wait=bank.rw())
        bank.r.append(t_cbtm)
    tap("cbtm", cbtm[:], t_cbtm)

    tc34 = stream_tile(wzT[:, 34 * 128:34 * 128 + 32], 0, 0, DC, mj=32)
    tdv = None
    for tt in range(NTT):
        bank = pb[2 + tt % 2]
        mm_hn(bank, 0, tt, mj=32, fw=[tc34])
        tdv = b.op("vector", lambda e, tt=tt, bank=bank: e.tensor_copy(
            out=dtraw[:, tt * 512:(tt + 1) * 512], in_=bank.t[0:NH, 0:512]),
            inc=("sV", 1), wait=bank.rw())
        bank.r.append(tdv)
    wbf_tok[0] = tdv
    ta = b.op("vector", lambda e: e.tensor_scalar(
        out=dtraw[:], in0=dtraw[:], scalar1=dtb_s[:, 0:1], scalar2=None,
        op0=ALU.add), inc=("sV", 1), wait=[tdv, t_consts])
    tb_ = b.op("scalar", lambda e: e.activation(
        out=dtt1[:], in_=dtraw[:], func=AF.Abs), inc=("sS", 1), wait=[ta])
    tb_ = b.op("scalar", lambda e: e.activation(
        out=dtt1[:], in_=dtt1[:], func=AF.Exp,
        scale=cpool[0:NH, CP_NEG1:CP_NEG1 + 1]), inc=("sS", 1), wait=[tb_])
    tb_ = b.op("scalar", lambda e: e.activation(
        out=dtt1[:], in_=dtt1[:], func=AF.Ln,
        bias=cpool[0:NH, CP_ONE:CP_ONE + 1]), inc=("sS", 1), wait=[tb_])
    tdt = b.op("vector", lambda e: e.scalar_tensor_tensor(
        out=dt_f[:], in0=dtraw[:], scalar=0.0, in1=dtt1[:], op0=ALU.max,
        op1=ALU.add), inc=("sV", 1), wait=[tb_])
    tap("dt", dt_f[:], tdt)
    tap("dtraw", dtraw[:], tdt)

    tA = b.op("scalar", lambda e: e.activation(
        out=acolt[:, 0:1], in_=alog_s[:], func=AF.Exp), inc=("sS", 1),
        wait=[t_consts])
    tA = b.op("scalar", lambda e: e.mul(
        out=acolt[:, 1:2], in_=acolt[:, 0:1], mul=-1.0), inc=("sS", 1),
        wait=[tA])
    tadt = b.op("vector", lambda e: e.tensor_scalar(
        out=adt_f[:], in0=dt_f[:], scalar1=acolt[:, 1:2], scalar2=None,
        op0=ALU.mult), inc=("sV", 1), wait=[tA, tdt])
    tap("adt", adt_f[:], tadt)
    tap("acol", acolt[:], tadt)

    tacs = None
    for c in range(NCHK):
        tacs = b.op("vector", lambda e, c=c: e.tensor_tensor_scan(
            out=acs_hm[:, c * CH:(c + 1) * CH],
            data0=adt_f[:, c * CH:(c + 1) * CH], data1=zeros32[:],
            initial=0.0, op0=ALU.add, op1=ALU.add),
            inc=("sV", 1) if c == NCHK - 1 else None, wait=[tadt])
    tap("acs", acs_hm[:], tacs)

    def tok_transpose(dst, src_fn, src_toks):
        last = None
        for g0 in range(0, NCHK, 16):
            gn = min(16, NCHK - g0)
            bank = pb[6]
            ws = bank.ww()
            tok = None
            for i in range(gn):
                c = g0 + i
                tok = b.op("tensor", lambda e, c=c, i=i: e.transpose(
                    out=pb[6].t[:, i * 32:(i + 1) * 32], in_=src_fn(c),
                    identity=identf[0:32, 0:32]),
                    inc=("sT", 1) if i == gn - 1 else None,
                    wait=(ws + src_toks + [t_consts]) if i == 0 else [])
            bank.w = tok
            last = b.op("vector", lambda e, g0=g0, gn=gn: e.tensor_copy(
                out=dst[:, g0 * 32:(g0 + gn) * 32], in_=pb[6].t[:, 0:gn * 32]),
                inc=("sV", 1), wait=bank.rw())
            bank.r.append(last)
        return last

    t_acstok = tok_transpose(acs_tok, lambda c: acs_hm[:, c * CH:(c + 1) * CH],
                             [tacs])
    t_dttok = tok_transpose(dt_tok, lambda c: dt_f[:, c * CH:(c + 1) * CH],
                            [tdt])
    tap("acstok", acs_tok[:], t_acstok)
    tap("dttok", dt_tok[:], t_dttok)

    t_dcy = None
    dtmp_rd = None
    for c in range(NCHK):
        t1 = b.op("vector", lambda e, c=c: e.tensor_scalar(
            out=dtmp[:], in0=acs_hm[:, c * CH:(c + 1) * CH],
            scalar1=acs_hm[:, (c + 1) * CH - 1:(c + 1) * CH], scalar2=-1.0,
            op0=ALU.subtract, op1=ALU.mult), inc=("sV", 1),
            wait=[tacs, dtmp_rd])
        t2 = b.op("scalar", lambda e: e.activation(
            out=dtmp[:], in_=dtmp[:], func=AF.Exp), inc=("sS", 1), wait=[t1])
        bank = pb[6]
        ws = bank.ww()
        tok = b.op("tensor", lambda e, c=c: e.transpose(
            out=pb[6].t[:, (c % 16) * 32:(c % 16 + 1) * 32], in_=dtmp[:],
            identity=identf[0:32, 0:32]), inc=("sT", 1), wait=ws + [t2])
        bank.w = tok
        dtmp_rd = tok
        t_dcy = b.op("vector", lambda e, c=c: e.tensor_copy(
            out=dcy_tok[:, c * 32:(c + 1) * 32],
            in_=pb[6].t[:, (c % 16) * 32:(c % 16 + 1) * 32]), inc=("sV", 1),
            wait=bank.rw())
        bank.r.append(t_dcy)
    tap("dcytok", dcy_tok[:], t_dcy)

    tea = b.op("vector", lambda e: e.tensor_copy(
        out=ea_all[:], in_=acs_hm[:, CH - 1::CH]), inc=("sV", 1), wait=[tacs])
    tea = b.op("scalar", lambda e: e.activation(
        out=ea_all[:], in_=ea_all[:], func=AF.Exp), inc=("sS", 1), wait=[tea])
    bank = pb[6]
    ws = bank.ww()
    tek = b.op("tensor", lambda e: e.transpose(
        out=pb[6].t[0:NCHK, 0:NH], in_=ea_all[:],
        identity=identf[0:NH, 0:NH]),
        inc=("sT", 1), wait=ws + [tea])
    bank.w = tek
    tev2 = b.op("vector", lambda e: e.tensor_copy(
        out=ea_t[:], in_=pb[6].t[0:NCHK, 0:NH]), inc=("sV", 1), wait=bank.rw())
    bank.r.append(tev2)
    t_earep = None
    ear_rd = [None, None]
    for c in range(NCHK):
        tr = b.op("gpsimd", lambda e, c=c: e.dma_start(
            out=earow[c % 2][:], in_=ea_t[c:c + 1, :]), inc=("sge", 16),
            wait=[tev2, ear_rd[c % 2]])
        bank = pb[6]
        ws = bank.ww()
        tk = b.op("tensor", lambda e, c=c: e.matmul(
            out=pb[6].t[:, (c % 16) * 32:(c % 16 + 1) * 32], lhsT=ones_f[:],
            rhs=earow[c % 2][:], start=True, stop=True, skip_group_check=True),
            inc=("sT", 1), wait=ws + [tr, t_init])
        bank.w = tk
        ear_rd[c % 2] = tk
        t_earep = b.op("vector", lambda e, c=c: e.tensor_copy(
            out=ea_rep[:, c * 32:(c + 1) * 32],
            in_=pb[6].t[:, (c % 16) * 32:(c % 16 + 1) * 32]), inc=("sV", 1),
            wait=bank.rw())
        bank.r.append(t_earep)
    tap("earep", ea_rep[:], t_earep)
    t_acsw = b.op("gpsimd", lambda e: e.dma_start(out=acs_scr, in_=acs_hm[:]),
                  inc=("sG", 16), wait=[tacs])

    if debug_stop == "p1":
        return finish(b, nc)
    b.barrier()
    p1.close()

    pY = contextlib.ExitStack()
    b.stacks.append(pY)
    yg = b.sb("yg", [128, DIC, T], BF16, pY, side="right")
    p2 = contextlib.ExitStack()
    b.stacks.append(p2)
    realloc_wbufs(p2, 1024)
    cbuf2 = b.sb("cbuf2", [128, T + 4], BF16, p2)
    cac2 = [b.sb(f"cac2_{i}", [128, T], BF16, p2) for i in range(2)]
    _sz = b.sb("sz0b", [128, T], BF16, p2)
    sz = [_sz, _sz]
    _xs = b.sb("xsfm0b", [128, T], BF16, p2)
    xs_fm = [_xs, _xs]
    xs_tok = b.sb("xs_tok", [128, T], BF16, p2)
    xdt = b.sb("xdt", [128, T], BF16, p2)
    xdtd = b.sb("xdtd", [128, T], BF16, p2)
    t1a = b.sb("t1a", [128, 512], F32, p2)
    t1b = [b.sb("t1b0", [128, 512], F32, p2)] * 2
    ep_g = [b.sb(f"epg{i}", [128, 512], BF16, p2) for i in range(2)]
    er_g = [b.sb(f"erg{i}", [128, 512], BF16, p2) for i in range(2)]
    mt_b = [b.sb(f"mtb{i}", [128, 512], BF16, p2) for i in range(3)]
    cp_b = [b.sb(f"cpb{i}", [128, 512], BF16, p2) for i in range(3)]
    s_run = [b.sb(f"srun{i}", [128, 128], F32, p2) for i in range(2)]
    s_bf = [b.sb(f"sbf{i}", [128, 128], BF16, p2) for i in range(2)]
    ytmp = b.sb("ytmp", [128, 512], F32, p2)
    arow = [rbB[0:1, :], rbB[32:33, :], rbB[64:65, :], rbC[32:33, :]]
    arow_bp = [0, 32, 64, 32]

    t_cb2 = b.op("vector", lambda e: e.memset(cbuf2[:, 0:4], 0.0), inc=("sV", 1))
    t_ssq0 = b.op("vector", lambda e: e.memset(row(rbC, 0), 0.0), inc=("sV", 1))

    acs3 = acs_tok[:].rearrange("p (c q) -> p c q", q=32)
    dt3 = dt_tok[:].rearrange("p (c q) -> p c q", q=32)
    dcy3 = dcy_tok[:].rearrange("p (c q) -> p c q", q=32)

    arow_tok = {}
    arow_rd = {}

    def dma_acs_row(h, slot):
        return b.op("gpsimd", lambda e, h=h, slot=slot: e.dma_start(
            out=arow[slot][:], in_=acs_scr[h:h + 1, :]), inc=(f"sg{slot}", 16),
            wait=[t_acsw, arow_rd.get(slot)])

    arow_tok[(0, 0)] = dma_acs_row(0, 0)
    arow_tok[(0, 1)] = dma_acs_row(1, 1)

    sz_rd = [None, None]
    xs_rd = [None, None]
    conv_s_war = bc_tok[33]
    ygsq_rd = None
    snapT = ("sT", 0)
    war_ep = [None, None]
    war_er = [None, None]
    war_t1b = [None, None]
    mt_war = [None, None, None]
    cp_war = [None, None, None]
    s_bf_rd = [[None, None], [None, None]]

    for jp in range(DIC):
        par = jp % 2
        if jp + 1 < DIC:
            npar = (jp + 1) % 2
            arow_tok[(jp + 1, 0)] = dma_acs_row(2 * (jp + 1), 2 * npar)
            arow_tok[(jp + 1, 1)] = dma_acs_row(2 * (jp + 1) + 1, 2 * npar + 1)

        bfz = jp % 2
        tcz = stream_tile(wzT[:, jp * 128:(jp + 1) * 128], jp % 2, bfz, DC)
        tsz = None
        for tt in range(NTT):
            bank = pb[2 + tt % 2]
            mm_hn(bank, bfz, tt, fw=[tcz])
            tsz = b.op("scalar", lambda e, tt=tt, par=par, bank=bank: e.activation(
                out=sz[par][:, tt * 512:(tt + 1) * 512], in_=bank.t[:, 0:512],
                func=AF.Silu), inc=("sS", 1), wait=bank.rw() + [sz_rd[par]])
            bank.r.append(tsz)
        wbf_tok[bfz] = tsz
        if jp == 0:
            tap("sz0", sz[0][:], tsz)

        bfx = (jp + 1) % 2
        tcx = stream_tile(wzT[:, (16 + jp) * 128:(17 + jp) * 128],
                          (jp + 1) % 2, bfx, DC)
        tev = None
        for tt in range(NTT):
            bank = pb[2 + tt % 2]
            mm_hn(bank, bfx, tt, fw=[tcx])
            tev = b.op("vector", lambda e, tt=tt, bank=bank: e.tensor_copy(
                out=cbuf2[:, 4 + tt * 512:4 + (tt + 1) * 512],
                in_=bank.t[:, 0:512]), inc=("sV", 1), wait=bank.rw())
            bank.r.append(tev)
        wbf_tok[bfx] = tev
        txs = conv_silu(xs_fm[par][:], jp, tev, xs_rd[par], cbuf2, cac2,
                        t_cb2, war2=conv_s_war)
        conv_s_war = txs
        if jp == 0:
            tap("xs0", xs_fm[0][:], txs)

        t_xstok = transpose_blocks(
            xs_tok, lambda c, par=par: xs_fm[par][:, c * 128:(c + 1) * 128],
            NCHK, txs)
        dtb_ap = dt3[:, :, 2 * jp:2 * jp + 2].unsqueeze(3) \
            .broadcast_to([128, NCHK, 2, 64])
        dcyb_ap = dcy3[:, :, 2 * jp:2 * jp + 2].unsqueeze(3) \
            .broadcast_to([128, NCHK, 2, 64])
        txdt = b.op("vector", lambda e, dtb_ap=dtb_ap: e.tensor_tensor(
            out=xdt[:].rearrange("p (c h q) -> p c h q", c=NCHK, h=2),
            in0=xs_tok[:].rearrange("p (c h q) -> p c h q", c=NCHK, h=2),
            in1=dtb_ap, op=ALU.mult), inc=("sV", 1),
            wait=[t_xstok, t_dttok, snapT])
        txdtd = b.op("vector", lambda e, dcyb_ap=dcyb_ap: e.tensor_tensor(
            out=xdtd[:].rearrange("p (c h q) -> p c h q", c=NCHK, h=2),
            in0=xdt[:].rearrange("p (c h q) -> p c h q", c=NCHK, h=2),
            in1=dcyb_ap, op=ALU.mult), inc=("sV", 1), wait=[t_dcy])

        NG = NCHK // 4
        prod = [(g, hh) for g in range(NG) for hh in range(2)]
        mt_tok = {}
        cp_tok = {}
        s_run_tok = [None, None]
        s_bf_tok = [None, None]

        def produce(k):
            g, hh = prod[k]
            h = 2 * jp + hh
            k3 = k % 3
            rtok = arow_tok[(jp, hh)]
            arw = arow[2 * par + hh]
            rbank = pb[4 + k % 2]
            ws = rbank.ww()
            tokR = None
            bp = arow_bp[2 * par + hh]
            lhR = onesF[bp:bp + 1, :]
            for i in range(4):
                c = 4 * g + i
                tokR = b.op("tensor", lambda e, c=c, i=i, arw=arw, lhR=lhR,
                            rbank=rbank: e.matmul(
                    out=rbank.t[:, i * 128:(i + 1) * 128], lhsT=lhR,
                    rhs=arw[:, c * CH:(c + 1) * CH], start=True, stop=True,
                    skip_group_check=True),
                    inc=("sT", 1) if i == 3 else None,
                    wait=(ws + [rtok, t_init]) if i == 0 else [])
            rbank.w = tokR
            arow_rd[2 * par + hh] = tokR
            colrep = acs3[:, 4 * g:4 * g + 4, h:h + 1] \
                .broadcast_to([128, 4, 128])
            mrep = masks[:].unsqueeze(1).broadcast_to([128, 4, 128])
            tt1 = b.op("vector", lambda e, colrep=colrep, rbank=rbank:
                       e.tensor_tensor(
                out=t1a[:].rearrange("p (c l) -> p c l", c=4),
                in0=rbank.t[:, 0:512].rearrange("p (c l) -> p c l", c=4),
                in1=colrep, op=ALU.subtract), inc=("sV", 1),
                wait=rbank.rw() + [t_acstok])
            rbank.r.append(tt1)
            k2 = k % 2
            tt1 = b.op("vector", lambda e, mrep=mrep, k2=k2: e.tensor_tensor(
                out=t1b[k2][:].rearrange("p (c l) -> p c l", c=4),
                in0=t1a[:].rearrange("p (c l) -> p c l", c=4),
                in1=mrep, op=ALU.add), inc=("sV", 1),
                wait=[t_consts, war_t1b[k2]])
            tep = b.op("scalar", lambda e, k2=k2: e.activation(
                out=ep_g[k2][:], in_=t1b[k2][:], func=AF.Exp),
                inc=("sS", 1), wait=[tt1, war_ep[k2]])
            war_t1b[k2] = tep
            ter = b.op("scalar", lambda e, k2=k2, rbank=rbank: e.activation(
                out=er_g[k2][:], in_=rbank.t[:, 0:512], func=AF.Exp),
                inc=("sS", 1), wait=rbank.rw() + [war_er[k2]])
            rbank.r.append(ter)
            tmt = b.op("vector", lambda e, g=g, k2=k2, k3=k3: e.tensor_tensor(
                out=mt_b[k3][:], in0=cbtm[:, g * 512:(g + 1) * 512],
                in1=ep_g[k2][:], op=ALU.mult), inc=("sV", 1),
                wait=[tep, t_cbtm, mt_war[k3]])
            war_ep[k2] = tmt
            tcp = b.op("vector", lambda e, g=g, k2=k2, k3=k3: e.tensor_tensor(
                out=cp_b[k3][:], in0=c_fm[:, g * 512:(g + 1) * 512],
                in1=er_g[k2][:], op=ALU.mult), inc=("sV", 1),
                wait=[ter, bc_tok[33], cp_war[k3]])
            war_er[k2] = tcp
            mt_tok[k] = tmt
            cp_tok[k] = tcp

        def consume(g):
            k0 = 2 * g
            ybank = pb[6]
            ws_y = ybank.ww()
            tok_y = None
            for i in range(4):
                c = 4 * g + i
                pc, pn = c % 2, (c + 1) % 2
                for hh in range(2):
                    h = 2 * jp + hh
                    k3 = (k0 + hh) % 3
                    tok_y = b.op("tensor", lambda e, c=c, i=i, hh=hh, k3=k3,
                                 ybank=ybank: e.matmul(
                        out=ybank.t[hh * 64:(hh + 1) * 64,
                                    i * 128:(i + 1) * 128],
                        lhsT=xdt[:, c * 128 + hh * 64:c * 128 + (hh + 1) * 64],
                        rhs=mt_b[k3][:, i * 128:(i + 1) * 128],
                        start=True, stop=(c == 0), skip_group_check=True),
                        inc=("sT", 1) if c == 0 else None,
                        wait=(ws_y if (i == 0 and hh == 0) else []) +
                        [mt_tok[k0 + hh], txdt])
                    if c > 0:
                        tok_y = b.op("tensor", lambda e, i=i, hh=hh, k3=k3,
                                     pc=pc, ybank=ybank: e.matmul(
                            out=ybank.t[hh * 64:(hh + 1) * 64,
                                        i * 128:(i + 1) * 128],
                            lhsT=s_bf[pc][:, hh * 64:(hh + 1) * 64],
                            rhs=cp_b[k3][:, i * 128:(i + 1) * 128],
                            start=False, stop=True, skip_group_check=True),
                            inc=("sT", 1),
                            wait=[cp_tok[k0 + hh], s_bf_tok[pc]])
                        s_bf_rd[pc][hh] = tok_y
                for hh in range(2):
                    h = 2 * jp + hh
                    sbank = pb[0 + c % 2]
                    ws_s = sbank.ww()
                    b.op("tensor", lambda e, c=c, hh=hh, sbank=sbank: e.matmul(
                        out=sbank.t[:, hh * 64:(hh + 1) * 64],
                        lhsT=b_tok[:, c * 128:(c + 1) * 128],
                        rhs=xdtd[:, c * 128 + hh * 64:c * 128 + (hh + 1) * 64],
                        start=True, stop=True, skip_group_check=True),
                        inc=("sT", 1),
                        wait=(ws_s + [txdtd, t_btok]) if hh == 0 else [])
                    sbank.w = b.tok("sT")
                for hh in range(2):
                    h = 2 * jp + hh
                    sbank = pb[0 + c % 2]
                    if c == 0:
                        tsu = b.op("vector", lambda e, hh=hh, pn=pn,
                                   sbank=sbank: e.tensor_copy(
                            out=s_run[pn][:, hh * 64:(hh + 1) * 64],
                            in_=sbank.t[:, hh * 64:(hh + 1) * 64]),
                            inc=("sV", 1), wait=sbank.rw())
                    else:
                        eacol = ea_rep[:, c * 32 + h:c * 32 + h + 1]
                        tsu = b.op("vector", lambda e, hh=hh, pc=pc, pn=pn,
                                   eacol=eacol, sbank=sbank:
                                   e.scalar_tensor_tensor(
                            out=s_run[pn][:, hh * 64:(hh + 1) * 64],
                            in0=s_run[pc][:, hh * 64:(hh + 1) * 64],
                            scalar=eacol,
                            in1=sbank.t[:, hh * 64:(hh + 1) * 64],
                            op0=ALU.mult, op1=ALU.add), inc=("sV", 1),
                            wait=sbank.rw() + [t_earep, s_run_tok[pn]])
                    sbank.r.append(tsu)
                    s_run_tok[pn] = tsu
                    tsb = b.op("scalar", lambda e, hh=hh, pn=pn: e.copy(
                        out=s_bf[pn][:, hh * 64:(hh + 1) * 64],
                        in_=s_run[pn][:, hh * 64:(hh + 1) * 64]),
                        inc=("sS", 1), wait=[tsu, s_bf_rd[pn][hh]])
                    s_bf_tok[pn] = tsb
            ybank.w = tok_y
            mt_war[k0 % 3] = tok_y
            mt_war[(k0 + 1) % 3] = tok_y
            cp_war[k0 % 3] = tok_y
            cp_war[(k0 + 1) % 3] = tok_y
            tye = b.op("vector", lambda e, g=g, jp=jp, par=par, ybank=ybank:
                       e.scalar_tensor_tensor(
                out=ytmp[:], in0=xs_fm[par][:, g * 512:(g + 1) * 512],
                scalar=drep_s[:, jp:jp + 1], in1=ybank.t[:, 0:512],
                op0=ALU.mult, op1=ALU.add), inc=("sV", 1),
                wait=ybank.rw() + [txs, t_consts])
            ybank.r.append(tye)
            tyg = b.op("vector", lambda e, g=g, jp=jp, par=par:
                       e.tensor_tensor(
                out=yg[:, jp, g * 512:(g + 1) * 512], in0=ytmp[:],
                in1=sz[par][:, g * 512:(g + 1) * 512], op=ALU.mult),
                inc=("sV", 1), wait=[tsz])
            xs_rd[par] = tye
            sz_rd[par] = tyg
            return tyg

        produce(0)
        produce(1)
        produce(2)
        tyg = None
        for g in range(NG):
            tyg = consume(g)
            if 2 * g + 3 < 2 * NG:
                produce(2 * g + 3)
            if 2 * g + 4 < 2 * NG:
                produce(2 * g + 4)
        arow_tok.pop((jp, 0))
        arow_tok.pop((jp, 1))
        t_ygj = b.tok("sV")

        for tt in range(NTT):
            tq = b.op("scalar", lambda e, jp=jp, tt=tt: e.activation(
                out=sqb[0][:], in_=yg[:, jp, tt * 512:(tt + 1) * 512],
                func=AF.Square), inc=("sS", 1), wait=[t_ygj, ygsq_rd])
            bank = pb[2 + tt % 2]
            ws = bank.ww()
            tokq = b.op("tensor", lambda e, bank=bank: e.matmul(
                out=bank.t[0:1, 0:512], lhsT=ones_b[:], rhs=sqb[0][:],
                start=True, stop=True, skip_group_check=True), inc=("sT", 1),
                wait=ws + [tq, t_init])
            bank.w = tokq
            ygsq_rd = tokq
            b.op("vector", lambda e, tt=tt, bank=bank: e.tensor_tensor(
                out=row(rbC, 0, tt * 512, (tt + 1) * 512),
                in0=row(rbC, 0, tt * 512, (tt + 1) * 512),
                in1=bank.t[0:1, 0:512], op=ALU.add), inc=("sV", 1),
                wait=bank.rw() + [t_ssq0])
            bank.r.append(b.tok("sV"))
        snapT = b.tok("sT")
    t_ssqg = b.tok("sV")
    tap("yg", yg[:], t_ssqg)

    if debug_stop == "p2":
        return finish(b, nc)

    b.barrier()
    p2.close()

    p2t = contextlib.ExitStack()
    b.stacks.append(p2t)
    pg2t = guard()
    realloc_wbufs(p2t, 1024)
    rgb_sb = b.sb("rgb_sb", [128, T], BF16, p2t)
    qdbuf = b.sb("qdbuf", [128, T], BF16, p2t)
    qre = [b.sb(f"qre{i}", [128, T], BF16, p2t) for i in range(2)]
    oab = [b.sb(f"oab{i}", [128, T], BF16, p2t) for i in range(2)]
    oare = [b.sb(f"oare{i}", [128, T], BF16, p2t) for i in range(2)]
    t2tmp = b.sb("t2tmp", [128, 512], F32, p2t)

    for tt in range(NTT):
        lo, hi = tt * 512, (tt + 1) * 512
        ti, _ = rows_span(None, rbC[0:1, lo:hi], 512, (rbB, 0, lo, hi), None,
                          CP_I2048, False, wt=[t_ssqg])
        tb0 = bcast_row(pb[0], row(rbB, 0, lo, hi), [ti])
        mark_row_read(id(rbB), 0, tb0)
        b.op("vector", lambda e, lo=lo, hi=hi: e.tensor_copy(
            out=rgb_sb[:, lo:hi], in_=pb[0].t[:, 0:512]), inc=("sV", 1),
            wait=pb[0].rw())
        pb[0].r.append(b.tok("sV"))
    t_rgb = b.tok("sV")
    tap("rgb", rgb_sb[:], t_rgb)

    qre_rd = [None, None]
    oab_rd = [None, None]
    t_qscr = b.tok("sQ")
    for j in range(DC):
        tqr = b.op("sync", lambda e, j=j: e.dma_start(
            out=qre[j % 2][:], in_=qrot_scr[j * 128:(j + 1) * 128, :]),
            inc=("sW", 16), wait=[qre_rd[j % 2]] +
            (pg2t + [t_qscr] if j < 2 else []))
        tc_half = [
            stream_tile(woT[(ks * 8) * 128:(ks * 8 + 8) * 128,
                            j * 128:(j + 1) * 128], ks, ks, DC,
                        scale_base=ks * 8)
            for ks in range(2)]
        if j == 0 and "wdump" in tap_out:
            b.op("scalar", lambda e: e.dma_start(
                out=tap_out["wdump"], in_=wbf[0][:, 0:1024]),
                inc=("sQ", 16), wait=tc_half)
        touta = None
        for tt in range(NTT):
            lo, hi = tt * 512, (tt + 1) * 512
            bank = pb[2 + tt % 2]
            ws = bank.ww()
            tok = None
            for kk in range(16):
                ks, c = kk // 8, kk % 8
                lh = wbf[ks][:, c * 128:(c + 1) * 128]
                tok = b.op("tensor", lambda e, lh=lh, ks=ks, c=c, lo=lo, hi=hi,
                           kk=kk, bank=bank: e.matmul(
                    out=bank.t[:, 0:512], lhsT=lh,
                    rhs=yg[:, ks * 8 + c, lo:hi], start=(kk == 0),
                    stop=(kk == 15)),
                    inc=("sT", 1) if kk == 15 else None,
                    wait=(ws + tc_half + [t_ssqg]) if kk == 0 else [])
            bank.w = tok
            if j == 0 and "pout" in tap_out:
                tpd = b.op("vector", lambda e, lo=lo, hi=hi, bank=bank:
                           e.tensor_copy(out=oare[1][:, lo:hi],
                                         in_=bank.t[:, 0:512]),
                           inc=("sV", 1), wait=bank.rw())
                bank.r.append(tpd)
                b.op("scalar", lambda e, lo=lo, hi=hi: e.dma_start(
                    out=tap_out["pout"][:, lo:hi],
                    in_=oare[1][:, lo:hi].bitcast(F32)[:, 0:(hi - lo) // 2]
                    if False else oare[1][:, lo:hi]), inc=("sQ", 16),
                    wait=[tpd])
            tv1 = b.op("vector", lambda e, lo=lo, hi=hi, bank=bank:
                       e.tensor_tensor(
                out=t2tmp[:], in0=bank.t[:, 0:512], in1=rgb_sb[:, lo:hi],
                op=ALU.mult), inc=("sV", 1), wait=bank.rw() + [t_rgb])
            bank.r.append(tv1)
            touta = b.op("vector", lambda e, j=j, lo=lo, hi=hi: e.tensor_tensor(
                out=oab[j % 2][:, lo:hi], in0=t2tmp[:],
                in1=qre[j % 2][:, lo:hi], op=ALU.add), inc=("sV", 1),
                wait=[tqr, oab_rd[j % 2]])
        if j == 0 and "qredump" in tap_out:
            tqd = b.op("vector", lambda e: e.tensor_copy(
                out=qdbuf[:], in_=qre[0][:]), inc=("sV", 1), wait=[touta])
            b.op("scalar", lambda e: e.dma_start(
                out=tap_out["qredump"], in_=qdbuf[:]), inc=("sQ", 16),
                wait=[tqd])
            touta = tqd
        qre_rd[j % 2] = touta
        if j == 0 and "qre0" in tap_out:
            b.op("scalar", lambda e: e.dma_start(
                out=tap_out["qre0"], in_=qre[0][:]), inc=("sQ", 16),
                wait=[touta])
        wbf_tok[0] = touta
        wbf_tok[1] = touta
        tdo = b.op("scalar", lambda e, j=j: e.dma_start(
            out=outa_scr[j * 128:(j + 1) * 128, :], in_=oab[j % 2][:]),
            inc=("sQ", 16), wait=[touta])
        if "outa" in tap_out:
            b.op("scalar", lambda e, j=j: e.dma_start(
                out=tap_out["outa"][:, j, :], in_=oab[j % 2][:]),
                inc=("sQ", 16), wait=[touta])
        oab_rd[j % 2] = tdo

    b.barrier()
    pY.close()
    pS.close()
    pH.close()
    p2u = contextlib.ExitStack()
    b.stacks.append(p2u)
    pg2u = guard()
    i2b = b.sb("i2b", [128, T], F32, p2u)
    m2b = b.sb("m2b", [128, T], F32, p2u)
    x1c = [b.sb(f"x1c{i}", [128, T], F32, p2u) for i in range(2)]
    x1bt = b.sb("x1bt", [128, 512], BF16, p2u)
    xstg2 = [b.sb(f"xstg2_{i}", [128, T], F32, p2u) for i in range(2)]

    tz2 = b.op("vector", lambda e: e.memset(row(rbC, 32), 0.0), inc=("sV", 1))
    tz2b = b.op("vector", lambda e: e.memset(row(rbC, 64), 0.0), inc=("sV", 1))
    oare_rd = [None, None]
    for c in range(DC):
        tor = b.op("sync", lambda e, c=c: e.dma_start(
            out=oare[c % 2][:], in_=outa_scr[c * 128:(c + 1) * 128, :]),
            inc=("sW", 16), wait=[oare_rd[c % 2]] +
            (pg2t + [b.tok("sQ")] if c < 2 else []))
        for tt in range(NTT):
            lo, hi = tt * 512, (tt + 1) * 512
            tsq = b.op("scalar", lambda e, c=c, lo=lo, hi=hi: e.activation(
                out=sqb[0][:], in_=oare[c % 2][:, lo:hi], func=AF.Square),
                inc=("sS", 1), wait=[tor, ygsq_rd])
            bank = pb[4 + tt % 2]
            ws = bank.ww()
            b.op("tensor", lambda e, c=c, lo=lo, hi=hi, bank=bank: e.matmul(
                out=bank.t[0:1, 0:512], lhsT=ones_b[:],
                rhs=oare[c % 2][:, lo:hi], start=True, stop=True,
                skip_group_check=True), inc=("sT", 1),
                wait=ws + [tor, t_init])
            bank.w = b.tok("sT")
            b.op("vector", lambda e, lo=lo, hi=hi, bank=bank: e.tensor_tensor(
                out=row(rbC, 32, lo, hi), in0=row(rbC, 32, lo, hi),
                in1=bank.t[0:1, 0:512], op=ALU.add), inc=("sV", 1),
                wait=bank.rw() + [tz2])
            bank.r.append(b.tok("sV"))
            bank2 = pb[6]
            ws2 = bank2.ww()
            tq2 = b.op("tensor", lambda e, bank2=bank2: e.matmul(
                out=bank2.t[0:1, 0:512], lhsT=ones_b[:], rhs=sqb[0][:],
                start=True, stop=True, skip_group_check=True), inc=("sT", 1),
                wait=ws2 + [tsq])
            bank2.w = tq2
            ygsq_rd = tq2
            tlast = b.op("vector", lambda e, lo=lo, hi=hi, bank2=bank2:
                         e.tensor_tensor(
                out=row(rbC, 64, lo, hi), in0=row(rbC, 64, lo, hi),
                in1=bank2.t[0:1, 0:512], op=ALU.add), inc=("sV", 1),
                wait=bank2.rw() + [tz2b])
            bank2.r.append(tlast)
        oare_rd[c % 2] = tlast

    for tt in range(NTT):
        lo, hi = tt * 512, (tt + 1) * 512
        ti, tmi = rows_span(rbC[32:33, lo:hi], rbC[64:65, lo:hi], 512,
                            (rbB, 32, lo, hi), (rbB, 64, lo, hi), CP_I1024,
                            True)
        tb0 = bcast_row(pb[0], row(rbB, 32, lo, hi), [ti])
        mark_row_read(id(rbB), 32, tb0)
        b.op("vector", lambda e, lo=lo, hi=hi: e.tensor_copy(
            out=i2b[:, lo:hi], in_=pb[0].t[:, 0:512]), inc=("sV", 1),
            wait=pb[0].rw())
        pb[0].r.append(b.tok("sV"))
        tb1 = bcast_row(pb[1], row(rbB, 64, lo, hi), [tmi])
        mark_row_read(id(rbB), 64, tb1)
        b.op("vector", lambda e, lo=lo, hi=hi: e.tensor_copy(
            out=m2b[:, lo:hi], in_=pb[1].t[:, 0:512]), inc=("sV", 1),
            wait=pb[1].rw())
        pb[1].r.append(b.tok("sV"))
    t_ln2b = b.tok("sV")

    tz3 = b.op("vector", lambda e: e.memset(row(rbC, 0), 0.0), inc=("sV", 1))
    tz3b = b.op("vector", lambda e: e.memset(row(rbC, 96), 0.0), inc=("sV", 1))
    xst2_rd = [None, None]
    x1_rd = [None, None]
    oare_rd2 = list(oare_rd)
    for c in range(DC):
        td = b.op("sync", lambda e, c=c: e.dma_start(
            out=xstg2[c % 2][:], in_=xT[c * 128:(c + 1) * 128, :]),
            inc=("sW", 16), wait=[xst2_rd[c % 2]] + (pg2u if c < 2 else []))
        tor = b.op("sync", lambda e, c=c: e.dma_start(
            out=oare[c % 2][:], in_=outa_scr[c * 128:(c + 1) * 128, :]),
            inc=("sW", 16), wait=[oare_rd2[c % 2]])
        if "oarall" in tap_out:
            b.op("scalar", lambda e, c=c: e.dma_start(
                out=tap_out["oarall"][:, c, :], in_=oare[c % 2][:]),
                inc=("sQ", 16), wait=[tor])
        tx1 = None
        for tt in range(NTT):
            lo, hi = tt * 512, (tt + 1) * 512
            b.op("vector", lambda e, c=c, lo=lo, hi=hi: e.tensor_tensor(
                out=t2tmp[:], in0=oare[c % 2][:, lo:hi], in1=i2b[:, lo:hi],
                op=ALU.mult), inc=("sV", 1), wait=[t_ln2b, x1_rd[c % 2], tor])
            b.op("vector", lambda e, lo=lo, hi=hi: e.tensor_tensor(
                out=t2tmp[:], in0=t2tmp[:], in1=m2b[:, lo:hi],
                op=ALU.subtract), inc=("sV", 1))
            tx1 = b.op("vector", lambda e, c=c, lo=lo, hi=hi: e.tensor_tensor(
                out=x1c[c % 2][:, lo:hi], in0=t2tmp[:],
                in1=xstg2[c % 2][:, lo:hi], op=ALU.add), inc=("sV", 1),
                wait=[td])
            tcb = b.op("vector", lambda e, c=c, lo=lo, hi=hi: e.tensor_copy(
                out=x1bt[:], in_=x1c[c % 2][:, lo:hi]), inc=("sV", 1))
            tsq = b.op("scalar", lambda e, c=c, lo=lo, hi=hi: e.activation(
                out=sqb[0][:], in_=x1c[c % 2][:, lo:hi], func=AF.Square),
                inc=("sS", 1), wait=[tx1, ygsq_rd])
            bank = pb[4 + tt % 2]
            ws = bank.ww()
            b.op("tensor", lambda e, bank=bank: e.matmul(
                out=bank.t[0:1, 0:512], lhsT=ones_b[:], rhs=x1bt[:],
                start=True, stop=True, skip_group_check=True), inc=("sT", 1),
                wait=ws + [tcb, t_init])
            bank.w = b.tok("sT")
            b.op("vector", lambda e, lo=lo, hi=hi, bank=bank: e.tensor_tensor(
                out=row(rbC, 0, lo, hi), in0=row(rbC, 0, lo, hi),
                in1=bank.t[0:1, 0:512], op=ALU.add), inc=("sV", 1),
                wait=bank.rw() + [tz3])
            bank.r.append(b.tok("sV"))
            bank2 = pb[6]
            ws2 = bank2.ww()
            tq2 = b.op("tensor", lambda e, bank2=bank2: e.matmul(
                out=bank2.t[0:1, 0:512], lhsT=ones_b[:], rhs=sqb[0][:],
                start=True, stop=True, skip_group_check=True), inc=("sT", 1),
                wait=ws2 + [tsq])
            bank2.w = tq2
            ygsq_rd = tq2
            tlast = b.op("vector", lambda e, lo=lo, hi=hi, bank2=bank2:
                         e.tensor_tensor(
                out=row(rbC, 96, lo, hi), in0=row(rbC, 96, lo, hi),
                in1=bank2.t[0:1, 0:512], op=ALU.add), inc=("sV", 1),
                wait=bank2.rw() + [tz3b])
            bank2.r.append(tlast)
        tdx = b.op("scalar", lambda e, c=c: e.dma_start(
            out=x1_scr[c * 128:(c + 1) * 128, :], in_=x1c[c % 2][:]),
            inc=("sQ", 16), wait=[tx1, tlast])
        if "x1" in tap_out:
            b.op("scalar", lambda e, c=c: e.dma_start(
                out=tap_out["x1"][:, c, :], in_=x1c[c % 2][:]),
                inc=("sQ", 16), wait=[tx1])
        x1_rd[c % 2] = tdx
        xst2_rd[c % 2] = tx1
        oare_rd2[c % 2] = tx1

    t_i3 = t_mi3 = None
    for tt in range(NTT):
        lo, hi = tt * 512, (tt + 1) * 512
        t_i3, t_mi3 = rows_span(rbC[0:1, lo:hi], rbC[96:97, lo:hi], 512,
                                (rbB, 32, lo, hi), (rbB, 64, lo, hi),
                                CP_I1024, True)

    if "qscr" in tap_out:
        tlate = b.op("sync", lambda e: e.dma_start(
            out=qdbuf[:], in_=qrot_scr[0:128, :]), inc=("sW", 16),
            wait=[b.tok("sV"), b.tok("sQ")])
        tld = b.op("vector", lambda e: e.tensor_copy(
            out=oare[0][:], in_=qdbuf[:]), inc=("sV", 1), wait=[tlate])
        b.op("scalar", lambda e: e.dma_start(
            out=tap_out["qscr"], in_=oare[0][:]), inc=("sQ", 16), wait=[tld])
    if debug_stop == "p2t":
        return finish(b, nc)

    b.barrier()
    p2u.close()
    p2t.close()

    if "canary" in tap_out:
        tcz1 = b.op("vector", lambda e: e.memset(canary[:, 0:4], 0.0),
                    inc=("sV", 1))
        tcz2 = b.op("scalar", lambda e: e.activation(
            out=canary[:, 4:8], in_=canary[:, 0:4], func=AF.Exp),
            inc=("sS", 1), wait=[tcz1])
        tap("canary", canary[:], tcz2)

    p3 = contextlib.ExitStack()
    b.stacks.append(p3)
    pg3 = guard()
    realloc_wbufs(p3, HF, nbf=3)
    selrw_s = b.sb("selrw_s", [NE, NE * 128], BF16, p3)
    b.op("sync", lambda e: e.dma_start(out=selrw_s[:], in_=selrw_d[0:NE]),
         inc=("sW", 16), wait=pg3)
    xf = b.sb("xf", [128, DC, TH], BF16, p3)
    i3b = b.sb("i3b", [128, TH], F32, p3)
    m3b = b.sb("m3b", [128, TH], F32, p3)
    xstg3 = [b.sb(f"xstg3_{i}", [128, TH], F32, p3) for i in range(2)]
    t3tmp = b.sb("t3tmp", [128, 512], F32, p3)
    ymoe = b.sb("ymoe", [128, DC, TH], F32, p3)
    he = b.sb("he", [128, HFC, TH], BF16, p3)
    hprod = b.sb("hprod", [128, TH], F32, p3)
    hsil = b.sb("hsil", [128, TH], BF16, p3)
    rwb_sb = b.sb("rwb_sb", [128, TH], BF16, p3)
    ga_t = b.sb("ga_t", [128, 512], F32, p3)
    gb_t = b.sb("gb_t", [128, 512], F32, p3)
    NTI = TH // 128
    glog = b.sb("glog", [128, NTI * NE], F32, p3)
    gsc1 = b.sb("gsc1", [128, NTI], F32, p3)
    gsc2 = b.sb("gsc2", [128, NTI * NE], F32, p3)
    gsc3 = b.sb("gsc3", [128, NTI * NE], F32, p3)
    rw_f = b.sb("rw_f", [128, NTI * NE], F32, p3)
    rw_b16 = b.sb("rw_b16", [128, NTI * NE], BF16, p3)
    rw_fm = b.sb("rw_fm", [NE, TH], BF16, p3)
    gw_s = b.sb("gw_s", [128, DC * NE], F32, p3)
    gw_b = b.sb("gw_b", [128, DC * NE], BF16, p3)
    gbr_s = b.sb("gbr_s", [128, NTI * NE], F32, p3)
    b2_s = b.sb("b2_s", [NE, D], F32, p3)
    b2_b = b.sb("b2_b", [NE, D], BF16, p3)
    bias1 = b.sb("bias1", [128, HFC], F32, p3)
    bias3 = b.sb("bias3", [128, HFC], F32, p3)
    sbias1 = b.sb("sbias1", [128, HFC], F32, p3)
    sbias3 = b.sb("sbias3", [128, HFC], F32, p3)
    sbias2 = b.sb("sbias2", [128, DC], F32, p3)
    NTTH = TH // 512

    b.op("sync", lambda e: e.dma_start(
        out=gw_s[:].rearrange("p (c m) -> p c m", c=DC),
        in_=gw_d.rearrange("(c p) m -> p c m", p=128)), inc=("sW", 16),
        wait=pg3)
    b.op("sync", lambda e: e.dma_start(out=gbr_s[:], in_=gb_d[:, 0:NTI * NE]),
         inc=("sW", 16))
    b.op("sync", lambda e: e.dma_start(out=b2_s[:], in_=b2_d[0:NE]), inc=("sW", 16))
    b.op("sync", lambda e: e.dma_start(out=sbias1[:], in_=sb1_d), inc=("sW", 16))
    b.op("sync", lambda e: e.dma_start(out=sbias3[:], in_=sb3_d), inc=("sW", 16))
    b.op("sync", lambda e: e.dma_start(out=sbias2[:], in_=sb2_d), inc=("sW", 16))
    t_gwl = b.tok("sW")
    tgw = b.op("vector", lambda e: e.tensor_copy(out=gw_b[:], in_=gw_s[:]),
               inc=("sV", 1), wait=[t_gwl])
    tb2c = b.op("vector", lambda e: e.tensor_copy(out=b2_b[:], in_=b2_s[:]),
                inc=("sV", 1), wait=[t_gwl])

    tz4 = b.op("vector", lambda e: e.memset(row(rbC, 32), 0.0), inc=("sV", 1))
    tz4b = b.op("vector", lambda e: e.memset(row(rbC, 64), 0.0), inc=("sV", 1))

    he_rd = None
    rwb_rd = None
    bias_rd = None
    hsil_rd = None
    gab_rd = None
    ymoe_scr_rd = None
    xst3_rd = [None, None]
    ygsq_rd4 = ygsq_rd

    for th in range(2):
        base = th * TH
        t_xf = None
        for c in range(DC):
            td = b.op("sync", lambda e, c=c, base=base: e.dma_start(
                out=xstg3[c % 2][:],
                in_=x1_scr[c * 128:(c + 1) * 128, base:base + TH]),
                inc=("sW", 16), wait=[xst3_rd[c % 2]] +
                (pg3 + [b.tok("sQ")] if (th == 0 and c < 2) else []))
            for tt in range(NTTH):
                lo, hi = tt * 512, (tt + 1) * 512
                tb0 = bcast_row(pb[0], row(rbB, 32, base + lo, base + hi),
                                [t_i3]) if c == 0 else None
                if c == 0:
                    mark_row_read(id(rbB), 32, tb0)
                    b.op("vector", lambda e, lo=lo, hi=hi: e.tensor_copy(
                        out=i3b[:, lo:hi], in_=pb[0].t[:, 0:512]),
                        inc=("sV", 1), wait=pb[0].rw())
                    pb[0].r.append(b.tok("sV"))
                    tb1 = bcast_row(pb[1], row(rbB, 64, base + lo, base + hi),
                                    [t_mi3])
                    mark_row_read(id(rbB), 64, tb1)
                    b.op("vector", lambda e, lo=lo, hi=hi: e.tensor_copy(
                        out=m3b[:, lo:hi], in_=pb[1].t[:, 0:512]),
                        inc=("sV", 1), wait=pb[1].rw())
                    pb[1].r.append(b.tok("sV"))
                b.op("vector", lambda e, c=c, lo=lo, hi=hi: e.tensor_tensor(
                    out=t3tmp[:], in0=xstg3[c % 2][:, lo:hi],
                    in1=i3b[:, lo:hi], op=ALU.mult), inc=("sV", 1),
                    wait=[td])
                t_xf = b.op("vector", lambda e, c=c, lo=lo, hi=hi:
                            e.tensor_tensor(
                    out=xf[:, c, lo:hi], in0=t3tmp[:], in1=m3b[:, lo:hi],
                    op=ALU.subtract), inc=("sV", 1))
            xst3_rd[c % 2] = t_xf
        t_xf = b.tok("sV")
        if th == 0:
            tap("xf", xf[:], t_xf)

        gbank = pb[4]
        ws = gbank.ww()
        tok = None
        for i in range(NTI):
            for c in range(DC):
                tok = b.op("tensor", lambda e, i=i, c=c, gbank=gbank: e.matmul(
                    out=gbank.t[:, i * NE:(i + 1) * NE],
                    lhsT=xf[:, c, i * 128:(i + 1) * 128],
                    rhs=gw_b[:, c * NE:(c + 1) * NE],
                    start=(c == 0), stop=(c == DC - 1), skip_group_check=True),
                    inc=("sT", 1) if c == DC - 1 else None,
                    wait=(ws + [tgw, t_xf]) if (i == 0 and c == 0) else [])
        gbank.w = tok
        tgl2 = b.op("vector", lambda e, gbank=gbank: e.tensor_copy(
            out=glog[:], in_=gbank.t[:, 0:NTI * NE]), inc=("sV", 1),
            wait=gbank.rw())
        gbank.r.append(tgl2)
        g3v = glog[:].rearrange("p (i e) -> p i e", e=NE)
        s3v = gsc2[:].rearrange("p (i e) -> p i e", e=NE)
        b3v = gsc3[:].rearrange("p (i e) -> p i e", e=NE)
        r3v = rw_f[:].rearrange("p (i e) -> p i e", e=NE)
        g1u = gsc1[:].unsqueeze(2)
        g1b = gsc1[:].unsqueeze(2).broadcast_to([128, NTI, NE])
        vdelay()
        tmx = b.op("vector", lambda e: e.tensor_reduce(
            out=g1u, in_=g3v, axis=mybir.AxisListType.X, op=ALU.max),
            inc=("sV", 1), wait=[tgl2])
        if th == 0:
            tap("glog", glog[:], tgl2)
            tap("gmx", gsc1[:], tmx)
        vdelay()
        tsb2 = b.op("vector", lambda e: e.tensor_tensor(
            out=s3v, in0=g3v, in1=g1b, op=ALU.subtract), inc=("sV", 1),
            wait=[tmx])
        tex = b.op("scalar", lambda e: e.activation(
            out=gsc2[:], in_=gsc2[:], func=AF.Exp), inc=("sS", 1), wait=[tsb2])
        if th == 0:
            tap("gexp", gsc2[:], tex)
        vdelay()
        tsm = b.op("vector", lambda e: e.tensor_reduce(
            out=g1u, in_=s3v, axis=mybir.AxisListType.X, op=ALU.add),
            inc=("sV", 1), wait=[tex])
        vdelay()
        trc = b.op("vector", lambda e: e.reciprocal(out=gsc1[:], in_=gsc1[:]),
                   inc=("sV", 1), wait=[tsm])
        vdelay()
        tsc = b.op("vector", lambda e: e.tensor_tensor(
            out=s3v, in0=s3v, in1=g1b, op=ALU.mult), inc=("sV", 1), wait=[trc])
        vdelay()
        tgb2 = b.op("vector", lambda e: e.tensor_tensor(
            out=gsc3[:], in0=gsc2[:], in1=gbr_s[:], op=ALU.add),
            inc=("sV", 1), wait=[tsc, t_gwl])
        vdelay()
        tm1 = b.op("vector", lambda e: e.tensor_reduce(
            out=g1u, in_=b3v, axis=mybir.AxisListType.X, op=ALU.max),
            inc=("sV", 1), wait=[tgb2])
        vdelay()
        tk1 = b.op("vector", lambda e: e.tensor_tensor(
            out=r3v, in0=b3v, in1=g1b, op=ALU.is_ge), inc=("sV", 1),
            wait=[tm1])
        vdelay()
        tb2m = b.op("vector", lambda e: e.scalar_tensor_tensor(
            out=gsc3[:], in0=rw_f[:], scalar=-1e30, in1=gsc3[:],
            op0=ALU.mult, op1=ALU.add), inc=("sV", 1), wait=[tk1])
        vdelay()
        tm2 = b.op("vector", lambda e: e.tensor_reduce(
            out=g1u, in_=b3v, axis=mybir.AxisListType.X, op=ALU.max),
            inc=("sV", 1), wait=[tb2m])
        vdelay()
        tk2 = b.op("vector", lambda e: e.tensor_tensor(
            out=b3v, in0=b3v, in1=g1b, op=ALU.is_ge), inc=("sV", 1),
            wait=[tm2])
        vdelay()
        tkm = b.op("vector", lambda e: e.tensor_tensor(
            out=rw_f[:], in0=rw_f[:], in1=gsc3[:], op=ALU.add), inc=("sV", 1),
            wait=[tk2])
        vdelay()
        trw = b.op("vector", lambda e: e.tensor_tensor(
            out=rw_f[:], in0=rw_f[:], in1=gsc2[:], op=ALU.mult),
            inc=("sV", 1))
        if th == 0:
            tap("rw", rw_f[:], trw)
        vdelay()
        trwb = b.op("vector", lambda e: e.tensor_copy(
            out=rw_b16[:], in_=rw_f[:]), inc=("sV", 1), wait=[trw])
        t_rwfm = None
        for g0 in range(0, NTI, 8):
            gn = min(8, NTI - g0)
            ws = pbT.ww()
            tok = None
            for i in range(gn):
                it = g0 + i
                tok = b.op("tensor", lambda e, it=it, i=i: e.transpose(
                    out=pbT.t[0:NE, i * 128:(i + 1) * 128],
                    in_=rw_b16[:, it * NE:(it + 1) * NE], identity=identb[:]),
                    inc=("sT", 1) if i == gn - 1 else None,
                    wait=(ws + [trwb, t_consts]) if i == 0 else [])
            pbT.w = tok
            t_rwfm = b.op("vector", lambda e, g0=g0, gn=gn: e.tensor_copy(
                out=rw_fm[0:NE, g0 * 128:(g0 + gn) * 128],
                in_=pbT.t[0:NE, 0:gn * 128]), inc=("sV", 1), wait=pbT.rw())
            pbT.r.append(t_rwfm)

        t_ymi = None
        for j in range(DC):
            for tt in range(NTTH):
                lo, hi = tt * 512, (tt + 1) * 512
                bank = pb[2 + tt % 2]
                ws = bank.ww()
                tok = b.op("tensor", lambda e, j=j, lo=lo, hi=hi, bank=bank:
                           e.matmul(
                    out=bank.t[:, 0:512],
                    lhsT=b2_b[0:NE, j * 128:(j + 1) * 128],
                    rhs=rw_fm[0:NE, lo:hi], start=True, stop=True),
                    inc=("sT", 1), wait=ws + [tb2c, t_rwfm])
                bank.w = tok
                t_ymi = b.op("vector", lambda e, j=j, lo=lo, hi=hi, bank=bank:
                             e.tensor_scalar(
                    out=ymoe[:, j, lo:hi], in0=bank.t[:, 0:512],
                    scalar1=sbias2[:, j:j + 1], scalar2=None, op0=ALU.add),
                    inc=("sV", 1), wait=bank.rw() + [t_gwl, ymoe_scr_rd])
                bank.r.append(t_ymi)
        ymoe_wr = b.tok("sV")

        for ex in range(NE + 1):
            shared = ex == NE
            if shared:
                trb = b.op("vector", lambda e: e.memset(rwb_sb[:], 1.0),
                           inc=("sV", 1), wait=[rwb_rd])
            else:
                trb = None
                for tt in range(NTTH):
                    lo, hi = tt * 512, (tt + 1) * 512
                    bank = pb[6]
                    ws = bank.ww()
                    tok = b.op("tensor", lambda e, ex=ex, lo=lo, hi=hi,
                               bank=bank: e.matmul(
                        out=bank.t[:, 0:512],
                        lhsT=selrw_s[0:NE, ex * 128:(ex + 1) * 128],
                        rhs=rw_fm[0:NE, lo:hi], start=True, stop=True),
                        inc=("sT", 1), wait=ws + [t_rwfm, t_consts])
                    bank.w = tok
                    trb = b.op("vector", lambda e, lo=lo, hi=hi, bank=bank:
                               e.tensor_copy(
                        out=rwb_sb[:, lo:hi], in_=bank.t[:, 0:512]),
                        inc=("sV", 1),
                        wait=bank.rw() + ([rwb_rd] if tt == 0 else []))
                    bank.r.append(trb)
            if shared:
                tbl = t_gwl
                bb1, bb3 = sbias1, sbias3
            else:
                b.op("sync", lambda e, ex=ex: e.dma_start(
                    out=bias1[:], in_=b1_d[ex]), inc=("sW", 16),
                    wait=[bias_rd] + (pg3 if ex == 0 else []))
                tbl = b.op("sync", lambda e, ex=ex: e.dma_start(
                    out=bias3[:], in_=b3_d[ex]), inc=("sW", 16))
                bb1, bb3 = bias1, bias3
            tsil_last = None
            for j in range(HFC):
                if shared:
                    a1 = sw1_d[:, j * 128:(j + 1) * 128]
                    a3 = sw3_d[:, j * 128:(j + 1) * 128]
                else:
                    a1 = w1e[ex][:, j * 128:(j + 1) * 128]
                    a3 = w3e[ex][:, j * 128:(j + 1) * 128]
                tc1 = stream_tile(a1, 0, 0, DC)
                tc3 = stream_tile(a3, 1, 1, DC)
                tpr = None
                for tt in range(NTTH):
                    lo, hi = tt * 512, (tt + 1) * 512
                    bankA = pb[2 if tt % 2 == 0 else 0]
                    bankB = pb[3 if tt % 2 == 0 else 1]
                    mm_acc(bankA,
                           lambda c: wbf[0][:, c * 128:(c + 1) * 128],
                           lambda c, lo=lo, hi=hi: xf[:, c, lo:hi], DC,
                           fw=[tc1, t_xf])
                    mm_acc(bankB,
                           lambda c: wbf[1][:, c * 128:(c + 1) * 128],
                           lambda c, lo=lo, hi=hi: xf[:, c, lo:hi], DC,
                           fw=[tc3, t_xf])
                    tga = b.op("scalar", lambda e, j=j, bb1=bb1, bankA=bankA:
                               e.add(
                        out=ga_t[:], in_=bankA.t[:, 0:512],
                        add=bb1[:, j:j + 1]), inc=("sS", 1),
                        wait=bankA.rw() + [tbl, gab_rd])
                    bankA.r.append(tga)
                    tgb_ = b.op("vector", lambda e, j=j, bb3=bb3, bankB=bankB:
                                e.tensor_scalar(
                        out=gb_t[:], in0=bankB.t[:, 0:512],
                        scalar1=bb3[:, j:j + 1], scalar2=None, op0=ALU.add),
                        inc=("sV", 1), wait=bankB.rw() + [tbl, gab_rd])
                    bankB.r.append(tgb_)
                    tpr = b.op("gpsimd", lambda e, lo=lo, hi=hi: e.tensor_tensor(
                        out=hprod[:, lo:hi], in0=ga_t[:], in1=gb_t[:],
                        op=ALU.mult), inc=("sP", 1),
                        wait=[tga, tgb_, hsil_rd])
                    gab_rd = tpr
                wbf_tok[0] = tpr
                wbf_tok[1] = tpr
                tsl = b.op("scalar", lambda e, j=j: e.activation(
                    out=hsil[:], in_=hprod[:], func=AF.Silu), inc=("sS", 1),
                    wait=[tpr, hsil_rd])
                hsil_rd = tsl
                tsil_last = b.op("vector", lambda e, j=j: e.tensor_tensor(
                    out=he[:, j, :], in0=hsil[:], in1=rwb_sb[:], op=ALU.mult),
                    inc=("sV", 1), wait=[tsl, trb, he_rd])
                hsil_rd = tsil_last
            rwb_rd = tsil_last
            bias_rd = tsil_last
            t_he = b.tok("sV")
            if ex == 0 and th == 0:
                tap("he", he[:], t_he)
            t_acc = None
            for j2 in range(DC):
                a2 = sw2_d[:, j2 * 128:(j2 + 1) * 128] if shared else \
                    w2e[ex][:, j2 * 128:(j2 + 1) * 128]
                tc2 = stream_tile(a2, j2 % 2, 2, HFC)
                for tt in range(NTTH):
                    lo, hi = tt * 512, (tt + 1) * 512
                    bank = pb[4 + tt % 2]
                    mm_acc(bank,
                           lambda c: wbf[2][:, c * 128:(c + 1) * 128],
                           lambda c, lo=lo, hi=hi: he[:, c, lo:hi], HFC,
                           fw=[tc2, t_he])
                    t_acc = b.op("vector", lambda e, j2=j2, lo=lo, hi=hi,
                                 bank=bank: e.tensor_tensor(
                        out=ymoe[:, j2, lo:hi], in0=ymoe[:, j2, lo:hi],
                        in1=bank.t[:, 0:512], op=ALU.add), inc=("sV", 1),
                        wait=bank.rw() + [ymoe_wr])
                    bank.r.append(t_acc)
                wbf_tok[2] = t_acc
            he_rd = t_acc
            ymoe_wr = t_acc
        t_ym = b.tok("sV")
        if th == 0:
            tap("ymoe", ymoe[:], t_ym)

        for c in range(DC):
            tlast = None
            for tt in range(NTTH):
                lo, hi = tt * 512, (tt + 1) * 512
                tcb = b.op("vector", lambda e, c=c, lo=lo, hi=hi: e.tensor_copy(
                    out=sqb[1][:], in_=ymoe[:, c, lo:hi]), inc=("sV", 1),
                    wait=[t_ym, sq_rd[1]])
                tsq = b.op("scalar", lambda e, c=c, lo=lo, hi=hi: e.activation(
                    out=sqb[0][:], in_=ymoe[:, c, lo:hi], func=AF.Square),
                    inc=("sS", 1), wait=[t_ym, ygsq_rd4])
                bank = pb[4 + tt % 2]
                ws = bank.ww()
                b.op("tensor", lambda e, bank=bank: e.matmul(
                    out=bank.t[0:1, 0:512], lhsT=ones_b[:], rhs=sqb[1][:],
                    start=True, stop=True, skip_group_check=True),
                    inc=("sT", 1), wait=ws + [tcb, t_init])
                bank.w = b.tok("sT")
                sq_rd[1] = bank.w
                b.op("vector", lambda e, base=base, lo=lo, hi=hi, bank=bank:
                     e.tensor_tensor(
                    out=row(rbC, 32, base + lo, base + hi),
                    in0=row(rbC, 32, base + lo, base + hi),
                    in1=bank.t[0:1, 0:512], op=ALU.add), inc=("sV", 1),
                    wait=bank.rw() + [tz4])
                bank.r.append(b.tok("sV"))
                bank2 = pb[6]
                ws2 = bank2.ww()
                tq2 = b.op("tensor", lambda e, bank2=bank2: e.matmul(
                    out=bank2.t[0:1, 0:512], lhsT=ones_b[:], rhs=sqb[0][:],
                    start=True, stop=True, skip_group_check=True),
                    inc=("sT", 1), wait=ws2 + [tsq])
                bank2.w = tq2
                ygsq_rd4 = tq2
                tlast = b.op("vector", lambda e, base=base, lo=lo, hi=hi,
                             bank2=bank2: e.tensor_tensor(
                    out=row(rbC, 64, base + lo, base + hi),
                    in0=row(rbC, 64, base + lo, base + hi),
                    in1=bank2.t[0:1, 0:512], op=ALU.add), inc=("sV", 1),
                    wait=bank2.rw() + [tz4b])
                bank2.r.append(tlast)
            b.op("scalar", lambda e, c=c, base=base: e.dma_start(
                out=ymoe_scr[c * 128:(c + 1) * 128, base:base + TH],
                in_=ymoe[:, c, :]), inc=("sQ", 16), wait=[t_ym, tlast])
        ymoe_scr_rd = b.tok("sQ")

    t_i4 = t_mi4 = None
    for tt in range(NTT):
        lo, hi = tt * 512, (tt + 1) * 512
        t_i4, t_mi4 = rows_span(rbC[32:33, lo:hi], rbC[64:65, lo:hi], 512,
                                (rbB, 32, lo, hi), (rbB, 64, lo, hi),
                                CP_I1024, True)

    ymstr = [ymoe[:, 0, :], ymoe[:, 1, :]]
    x1str = [ymoe[:, 2, :], ymoe[:, 3, :]]
    obuf = [ymoe[:, 4, :], ymoe[:, 5, :]]
    t_ymall = b.tok("sQ")
    str_rd = {}
    for c in range(DC):
        for th in range(2):
            base = th * TH
            k = th
            td = b.op("sync", lambda e, c=c, base=base, k=k: e.dma_start(
                out=x1str[k],
                in_=x1_scr[c * 128:(c + 1) * 128, base:base + TH]),
                inc=("sW", 16),
                wait=[str_rd.get(("x", k)), t_ymall] +
                (guard() if c == 0 else []))
            tym = b.op("sync", lambda e, c=c, base=base, k=k: e.dma_start(
                out=ymstr[k],
                in_=ymoe_scr[c * 128:(c + 1) * 128, base:base + TH]),
                inc=("sW", 16),
                wait=[str_rd.get(("y", k)), t_ymall] +
                (guard() if c == 0 else []))
            tob = None
            for tt in range(NTTH):
                lo, hi = tt * 512, (tt + 1) * 512
                bcast_row(pb[0], row(rbB, 32, base + lo, base + hi), [t_i4])
                bcast_row(pb[1], row(rbB, 64, base + lo, base + hi), [t_mi4])
                tv1 = b.op("vector", lambda e, k=k, lo=lo, hi=hi:
                           e.tensor_tensor(
                    out=t3tmp[:], in0=ymstr[k][:, lo:hi],
                    in1=pb[0].t[:, 0:512], op=ALU.mult), inc=("sV", 1),
                    wait=pb[0].rw() + [tym])
                pb[0].r.append(tv1)
                tv2 = b.op("vector", lambda e, lo=lo, hi=hi: e.tensor_tensor(
                    out=t3tmp[:], in0=t3tmp[:], in1=pb[1].t[:, 0:512],
                    op=ALU.subtract), inc=("sV", 1), wait=pb[1].rw())
                pb[1].r.append(tv2)
                tob = b.op("vector", lambda e, k=k, c=c, lo=lo, hi=hi:
                           e.tensor_tensor(
                    out=obuf[c % 2][:, lo:hi], in0=t3tmp[:],
                    in1=x1str[k][:, lo:hi], op=ALU.add), inc=("sV", 1),
                    wait=[td, str_rd.get(("o", c % 2))])
            str_rd[("x", k)] = tob
            str_rd[("y", k)] = tob
            tdo = b.op("scalar", lambda e, c=c, base=base: e.dma_start(
                out=out_fm[c * 128:(c + 1) * 128, base:base + TH],
                in_=obuf[c % 2][:]), inc=("sO", 16), wait=[tob])
            str_rd[("o", c % 2)] = tdo

    return finish(b, nc)


def finish(b, nc):
    for st in reversed(b.stacks):
        st.close()
    for s in ("sO", "sQ", "sG"):
        h, v, _ = b.sems[s]
        if v > 0:
            b.ops["scalar"].append(lambda e, h=h, v=v: e.wait_ge(h, v))
    block = b.es.enter_context(nc.Block())
    b.finalize(block)
    b.es.close()
    return nc


def host_prep(inputs, T=2048):
    f32 = np.float32
    g = lambda k: np.asarray(inputs[k])
    cm = {}
    cm["wqT"] = np.ascontiguousarray(g("L0_attn_W_in")[:1024].T)
    cm["wzT"] = np.ascontiguousarray(g("L0_attn_W_zxbcdt").T)
    cm["woT"] = np.ascontiguousarray(g("L0_attn_W_out").T)
    cm["convw_d"] = np.ascontiguousarray(
        g("L0_attn_conv_w").reshape(18, 128, 4).transpose(1, 0, 2))
    cm["convb_d"] = np.ascontiguousarray(g("L0_attn_conv_b").reshape(18, 128).T)
    cm["dtb_d"] = np.zeros((128, 1), f32)
    cm["dtb_d"][:NH, 0] = g("L0_attn_dt_bias")
    cm["alog_d"] = np.zeros((128, 1), f32)
    cm["alog_d"][:NH, 0] = g("L0_attn_A_log")
    cm["drep_d"] = np.ascontiguousarray(
        np.repeat(g("L0_attn_D_param"), 64).reshape(16, 128).T)
    cm["npw_d"] = np.ascontiguousarray(g("L0_attn_norm_pre_w").reshape(DC, 128).T)
    cm["ngw_d"] = np.ascontiguousarray(
        g("L0_attn_norm_gated_w").reshape(DIC, 128).T)
    cm["gw_d"] = np.ascontiguousarray(g("L0_ffn_gate_w").T)
    cm["gb_d"] = np.ascontiguousarray(np.tile(g("L0_ffn_gate_b"), (128, 16)))
    cm["w1e"] = g("L0_ffn_w1")
    cm["w3e"] = g("L0_ffn_w3")
    cm["w2e"] = g("L0_ffn_w2")
    cm["b1_d"] = np.ascontiguousarray(
        g("L0_ffn_b1").reshape(NE, HFC, 128).transpose(0, 2, 1))
    cm["b3_d"] = np.ascontiguousarray(
        g("L0_ffn_b3").reshape(NE, HFC, 128).transpose(0, 2, 1))
    cm["b2_d"] = np.zeros((128, 1024), f32)
    cm["b2_d"][:NE] = g("L0_ffn_b2")
    cm["sw1_d"] = g("L0_ffn_sw1")
    cm["sw3_d"] = g("L0_ffn_sw3")
    cm["sw2_d"] = g("L0_ffn_sw2")
    cm["sb1_d"] = np.ascontiguousarray(g("L0_ffn_sb1").reshape(HFC, 128).T)
    cm["sb3_d"] = np.ascontiguousarray(g("L0_ffn_sb3").reshape(HFC, 128).T)
    cm["sb2_d"] = np.ascontiguousarray(g("L0_ffn_sb2").reshape(DC, 128).T)
    cm["identb_d"] = np.eye(128, dtype=f32).astype(BF)
    cm["identf_d"] = np.eye(128, dtype=f32)
    inv_freq = (1.0 / 10000.0 ** (np.arange(0, 64, 2, dtype=np.float64) / 64.0))
    t_ = np.arange(T, dtype=np.float64)
    freqs = t_[:, None] * inv_freq[None, :]
    cos = np.cos(freqs).T.astype(f32)
    sin = np.sin(freqs).T.astype(f32)
    ropeC = np.zeros((128, T), f32)
    ropeS = np.zeros((128, T), f32)
    for blk in range(2):
        base = blk * 64
        ropeC[base:base + 32] = cos
        ropeC[base + 32:base + 64] = cos
        ropeS[base:base + 32] = -sin
        ropeS[base + 32:base + 64] = sin
    cm["ropeC_d"] = ropeC
    cm["ropeS_d"] = ropeS
    mneg = np.zeros((128, 128), f32)
    mneg[np.arange(128)[:, None] > np.arange(128)[None, :]] = -1e30
    cm["maskneg_d"] = mneg.astype(BF)
    selrw = np.zeros((128, NE * 128), f32)
    for e in range(NE):
        selrw[e, e * 128:(e + 1) * 128] = 1.0
    cm["selrw_d"] = selrw.astype(BF)
    for k in list(cm):
        v = cm[k]
        if v.dtype == np.float64:
            cm[k] = v.astype(f32)
    x = g("L0_attn_x")
    in_maps = []
    for i in range(8):
        m = dict(cm)
        m["xT"] = np.ascontiguousarray(x[i][:T].T)
        in_maps.append(m)
    return in_maps


def kernel(**inputs):
    in_maps = host_prep(inputs)
    nc = build_core()
    res = run_bass_kernel_spmd(nc, in_maps, core_ids=list(range(8)))
    outs = [np.ascontiguousarray(r["out_fm"].T) for r in res.results]
    return np.stack(outs, axis=0).astype(np.float32)


# version 10 (final)
# speedup vs baseline: 1.3343x; 1.3343x over previous
import contextlib
import numpy as np
import ml_dtypes

import concourse.bass as bass
import concourse.mybir as mybir
from concourse.bass_utils import run_bass_kernel_spmd

F32 = mybir.dt.float32
BF16 = mybir.dt.bfloat16
AF = mybir.ActivationFunctionType
ALU = mybir.AluOpType

D = 1024
DC = 8
DI = 2048
DIC = 16
NH = 32
DZ = 4384
HF = 1408
HFC = 11
NE = 8
EPS = 1e-5
CH = 128

ENGINES = ("sync", "scalar", "vector", "tensor", "gpsimd")
BF = ml_dtypes.bfloat16


class Builder:
    def __init__(self, nc):
        self.nc = nc
        self.ops = {e: [] for e in ENGINES}
        self.sems = {}
        self.waited = {e: {} for e in ENGINES}
        self.es = contextlib.ExitStack()
        self.stacks = []

    def sem(self, name, eng):
        h = self.es.enter_context(self.nc.semaphore(name))
        self.sems[name] = [h, 0, eng]

    UNSAFE_SEMS = ("sV", "sS", "sP", "sT")

    def op(self, eng, fn, inc=None, wait=()):
        need_delay = False
        for w in wait:
            if w is None:
                continue
            s, v = w[0], w[1]
            safe = len(w) > 2 and w[2] == "safe"
            if self.sems[s][2] == eng:
                continue
            if self.waited[eng].get(s, -1) >= v:
                continue
            self.waited[eng][s] = v
            h = self.sems[s][0]
            self.ops[eng].append(lambda e, h=h, v=v: e.wait_ge(h, v))
            if s in self.UNSAFE_SEMS and not safe:
                need_delay = True
        if need_delay:
            self.ops[eng].append(lambda e: e.nop(cycle_cnt=400))
        if inc is None:
            self.ops[eng].append(fn)
            return None
        s, k = inc
        rec = self.sems[s]
        rec[1] += k
        h = rec[0]
        self.ops[eng].append(lambda e, fn=fn, h=h, k=k: fn(e).then_inc(h, k))
        return (s, rec[1])

    def tok(self, s):
        return (s, self.sems[s][1])

    def barrier(self, engines=("scalar", "vector", "tensor", "gpsimd")):
        toks = []
        for e in engines:
            s = "bar_" + e
            rec = self.sems[s]
            rec[1] += 1
            h, v = rec[0], rec[1]
            self.ops[e].append(lambda eng, h=h: eng.sem_inc(h, 1))
            toks.append((s, v))
        for e in engines:
            for (s, v) in toks:
                if self.sems[s][2] == e:
                    continue
                self.waited[e][s] = v
                h = self.sems[s][0]
                self.ops[e].append(lambda eng, h=h, v=v: eng.wait_ge(h, v))

    def sb(self, name, shape, dtype, es=None, side=None):
        kw = {"side": side} if side else {}
        return (es or self.es).enter_context(
            self.nc.sbuf_tensor(name, shape, dtype, **kw))

    def psum(self, name, shape, dtype):
        return self.es.enter_context(self.nc.psum_tensor(name, shape, dtype))

    def finalize(self, block):
        for eng in ENGINES:
            opl = self.ops[eng]

            def make(opl):
                def _(e):
                    for f in opl:
                        f(e)
                return _
            getattr(block, eng)(make(opl))


class Bank:
    def __init__(self, t):
        self.t = t
        self.w = None
        self.r = []

    def ww(self):
        out = [x for x in self.r if x is not None]
        if self.w is not None:
            out.append(self.w)
        self.r = []
        return out

    def rw(self):
        return [self.w] if self.w is not None else []


def build_core(T=2048, taps=(), debug_stop=None):
    NTT = T // 512
    NCHK = T // CH
    TH = T // 2
    taps = set(taps)
    nc = bass.Bass("TRN2", target_bir_lowering=False)
    b = Builder(nc)

    def din(name, shape, dtype=F32):
        return nc.dram_tensor(name, shape, dtype, kind="ExternalInput").ap()

    xT = din("xT", [D, T])
    wqT = din("wqT", [D, D])
    wzT = din("wzT", [D, DZ])
    woT = din("woT", [DI, D])
    convw_d = din("convw_d", [128, 18, 4])
    convb_d = din("convb_d", [128, 18])
    dtb_d = din("dtb_d", [128, 1])
    alog_d = din("alog_d", [128, 1])
    drep_d = din("drep_d", [128, 16])
    npw_d = din("npw_d", [128, DC])
    ngw_d = din("ngw_d", [128, DIC])
    gw_d = din("gw_d", [D, NE])
    gb_d = din("gb_d", [128, 16 * NE])
    w1e = din("w1e", [NE, D, HF])
    w3e = din("w3e", [NE, D, HF])
    w2e = din("w2e", [NE, HF, D])
    b1_d = din("b1_d", [NE, 128, HFC])
    b3_d = din("b3_d", [NE, 128, HFC])
    b2_d = din("b2_d", [128, D])
    sw1_d = din("sw1_d", [D, HF])
    sw3_d = din("sw3_d", [D, HF])
    sw2_d = din("sw2_d", [HF, D])
    sb1_d = din("sb1_d", [128, HFC])
    sb3_d = din("sb3_d", [128, HFC])
    sb2_d = din("sb2_d", [128, DC])
    identb_d = din("identb_d", [128, 128], BF16)
    identf_d = din("identf_d", [128, 128], F32)
    ropeC_d = din("ropeC_d", [128, T])
    ropeS_d = din("ropeS_d", [128, T])
    maskneg_d = din("maskneg_d", [128, 128], BF16)
    selrw_d = din("selrw_d", [128, NE * 128], BF16)

    out_fm = nc.dram_tensor("out_fm", [D, T], F32, kind="ExternalOutput").ap()
    qrot_scr = nc.dram_tensor("qrot_scr", [D, T], BF16, kind="Internal").ap()
    x1_scr = nc.dram_tensor("x1_scr", [D, T], F32, kind="Internal").ap()
    outa_scr = nc.dram_tensor("outa_scr", [D, T], BF16, kind="Internal").ap()
    ymoe_scr = nc.dram_tensor("ymoe_scr", [D, T], F32, kind="Internal").ap()
    acs_scr = nc.dram_tensor("acs_scr", [NH, T], F32, kind="Internal").ap()

    tap_out = {}

    def mktap(name, shape, dt):
        if name in taps:
            tap_out[name] = nc.dram_tensor("dbg_" + name, shape, dt,
                                           kind="ExternalOutput").ap()

    mktap("xn", [128, DC, T], BF16)
    mktap("rows", [128, T], F32)
    mktap("qrot", [128, DC, T], BF16)
    mktap("hn", [128, DC, T], BF16)
    mktap("dt", [NH, T], F32)
    mktap("dtraw", [NH, T], F32)
    mktap("adt", [NH, T], F32)
    mktap("acol", [NH, 2], F32)
    mktap("convraw", [128, T], BF16)
    mktap("acs", [NH, T], F32)
    mktap("bfm", [128, T], BF16)
    mktap("cfm", [128, T], BF16)
    mktap("cbtm", [128, T], BF16)
    mktap("btok", [128, T], BF16)
    mktap("acstok", [128, NCHK * 32], F32)
    mktap("dttok", [128, NCHK * 32], BF16)
    mktap("dcytok", [128, NCHK * 32], BF16)
    mktap("earep", [128, NCHK * 32], F32)
    mktap("sz0", [128, T], BF16)
    mktap("xs0", [128, T], BF16)
    mktap("yg", [128, DIC, T], BF16)
    mktap("outa", [128, DC, T], BF16)
    mktap("rgb", [128, T], BF16)
    mktap("pout", [128, T], BF16)
    mktap("wdump", [128, 1024], BF16)
    mktap("oarall", [128, DC, T], BF16)
    mktap("qredump", [128, T], BF16)
    mktap("qscr", [128, T], BF16)
    mktap("qre0", [128, T], BF16)
    mktap("x1", [128, DC, T], F32)
    mktap("xf", [128, DC, TH], BF16)
    mktap("rw", [128, (TH // 128) * NE], F32)
    mktap("glog", [128, (TH // 128) * NE], F32)
    mktap("gexp", [128, (TH // 128) * NE], F32)
    mktap("gmx", [128, TH // 128], F32)
    mktap("canary", [128, 8], F32)
    mktap("he", [128, HFC, TH], BF16)
    mktap("ymoe", [128, DC, TH], F32)

    def tap(name, ap, wtok):
        if name in tap_out:
            b.op("scalar", lambda e, ap=ap, name=name: e.dma_start(
                out=tap_out[name], in_=ap), inc=("sQ", 16),
                wait=[wtok] if wtok else [])

    b.sem("sW", None)
    b.sem("sQ", None)
    b.sem("sG", None)
    for i in range(4):
        b.sem(f"sg{i}", None)
    b.sem("sge", None)
    b.sem("sO", None)
    b.sem("sV", "vector")
    b.sem("sS", "scalar")
    b.sem("sT", "tensor")
    b.sem("sP", "gpsimd")
    for e in ("scalar", "vector", "tensor", "gpsimd"):
        b.sem("bar_" + e, e)

    pb = [Bank(b.psum(f"pb{i}", [128, 512], F32)) for i in range(7)]
    pbT = Bank(b.psum("pbT", [128, 1024], BF16))

    cpool = b.sb("cpool", [128, 8], F32)
    CP_ONE, CP_NEG1, CP_NEGH, CP_EPS, CP_I1024, CP_I2048 = range(6)
    ones_b = b.sb("ones_b", [128, 1], BF16)
    ones_f = b.sb("ones_f", [1, 128], F32)
    onesF = b.sb("onesF", [128, 128], F32)
    identb = b.sb("identb", [128, 128], BF16)
    identf = b.sb("identf", [128, 128], F32)
    masks = b.sb("masks", [128, 128], BF16)
    rbB = b.sb("rbB", [128, T], F32)
    rbC = b.sb("rbC", [128, T], F32)
    rsc = b.sb("rsc", [1, 2048], F32)
    convw_s = b.sb("convw_s", [128, 72], F32)
    convb_s = b.sb("convb_s", [128, 18], F32)
    dtb_s = b.sb("dtb_s", [NH, 1], F32)
    alog_s = b.sb("alog_s", [NH, 1], F32)
    drep_s = b.sb("drep_s", [128, 16], F32)
    npw_s = b.sb("npw_s", [128, DC], F32)
    ngw_s = b.sb("ngw_s", [128, DIC], F32)
    canary = b.sb("canary", [128, 8], F32)
    wstg = [None, None]
    wbf = [None, None, None, None, None, None]
    sqb = [b.sb(f"sqb{i}", [128, 512], BF16) for i in range(4)]

    def realloc_wbufs(es, width, nbf=2):
        gv, gt = b.tok("sV"), b.tok("sT")
        for i in range(2):
            wstg[i] = b.sb(f"wstg{i}_{width}_{b.sems['sV'][1]}_{b.sems['sT'][1]}",
                           [128, width], F32, es)
            wstg_tok[i] = gv
        for i in range(6):
            wbf[i] = None
            wbf_tok[i] = gt
        for i in range(nbf):
            wbf[i] = b.sb(f"wbf{i}_{width}_{b.sems['sV'][1]}_{b.sems['sT'][1]}",
                          [128, width], BF16, es)

    for i, v in enumerate([1.0, -1.0, -0.5, EPS, 1.0 / 1024, 1.0 / 2048]):
        b.op("vector", lambda e, i=i, v=v: e.memset(cpool[:, i:i + 1], v))
    b.op("vector", lambda e: e.memset(ones_b[:], 1.0))
    b.op("vector", lambda e: e.memset(ones_f[:], 1.0))
    t_init = b.op("vector", lambda e: e.memset(onesF[:], 1.0), inc=("sV", 1))

    for dst, src in [(identb, identb_d), (identf, identf_d), (masks, maskneg_d),
                     (convb_s, convb_d), (dtb_s, dtb_d[0:NH]), (alog_s, alog_d[0:NH]),
                     (drep_s, drep_d), (npw_s, npw_d), (ngw_s, ngw_d)]:
        b.op("sync", lambda e, d=dst, s=src: e.dma_start(out=d[:], in_=s),
             inc=("sW", 16))
    b.op("sync", lambda e: e.dma_start(
        out=convw_s[:].rearrange("p (j k) -> p j k", j=18), in_=convw_d),
        inc=("sW", 16))
    t_consts = b.tok("sW")

    def row(rb, slot, lo=0, hi=T):
        return rb[slot:slot + 1, lo:hi]

    wstg_tok = [None, None]
    wbf_tok = [None, None, None, None, None, None]

    def stream_tile(a, sg, bf, kc, mj=128, scale_base=None):
        stg, wb = wstg[sg], wbf[bf]
        td = b.op("sync", lambda e, stg=stg, kc=kc, mj=mj, a=a: e.dma_start(
            out=stg[:, :kc * mj].rearrange("p (c m) -> p c m", c=kc),
            in_=a.rearrange("(c p) m -> p c m", p=128)),
            inc=("sW", 16), wait=[wstg_tok[sg]])
        if scale_base is None:
            tc = b.op("vector", lambda e, stg=stg, wb=wb, kc=kc, mj=mj:
                      e.tensor_copy(
                out=wb[:, :kc * mj], in_=stg[:, :kc * mj]),
                inc=("sV", 1), wait=[td, wbf_tok[bf]])
            wstg_tok[sg] = tc
            return relay(tc)
        else:
            tc = None
            for c in range(kc):
                tc = b.op("vector", lambda e, stg=stg, wb=wb, c=c, mj=mj:
                          e.tensor_scalar(
                    out=wb[:, c * mj:(c + 1) * mj],
                    in0=stg[:, c * mj:(c + 1) * mj],
                    scalar1=ngw_s[:, scale_base + c:scale_base + c + 1],
                    scalar2=None, op0=ALU.mult), inc=("sV", 1),
                    wait=[td, t_consts, wbf_tok[bf]] if c == 0 else [])
        wstg_tok[sg] = tc
        return relay(tc)

    def guard():
        return [b.tok("sV"), b.tok("sT"), b.tok("sS"), b.tok("sP")]

    def vdelay():
        b.op("vector", lambda e: e.nop(cycle_cnt=400))

    def relay(tok):
        t = b.op("gpsimd", lambda e: e.engine_nop() if hasattr(e, "engine_nop")
                 else e.nop(), inc=("sP", 1), wait=[tok])
        return (t[0], t[1], "safe")

    def mm_acc(bank, lhs_fn, rhs_fn, nk, mj=128, fw=()):
        ws = bank.ww()
        tok = None
        for c in range(nk):
            lh, rh = lhs_fn(c), rhs_fn(c)
            tok = b.op("tensor", lambda e, lh=lh, rh=rh, c=c, mj=mj, bank=bank:
                       e.matmul(
                out=bank.t[0:mj, 0:512], lhsT=lh, rhs=rh,
                start=(c == 0), stop=(c == nk - 1)),
                inc=("sT", 1) if c == nk - 1 else None,
                wait=(ws + [w for w in fw]) if c == 0 else [])
        bank.w = tok
        return tok

    def bcast_row(bank, src_row, wt):
        ws = bank.ww()
        bp = src_row.base_partition()
        lh = onesF[bp:bp + 1, :]
        t = b.op("tensor", lambda e, src_row=src_row, lh=lh: e.matmul(
            out=bank.t[:, 0:512], lhsT=lh, rhs=src_row,
            start=True, stop=True), inc=("sT", 1),
            wait=ws + [w for w in wt] + [t_init])
        bank.w = t
        return t

    sq_rd = [None] * 4

    def stats_tt(src3, nch, tt, sum_bank, ssq_bank, src_tok):
        ws = sum_bank.ww()
        wq = ssq_bank.ww()
        tok_s = tok_q = None
        for c0 in range(0, nch, 4):
            cb = min(4, nch - c0)
            tsq = None
            for ci in range(cb):
                c = c0 + ci
                sq = sqb[c % 4]
                tsq = b.op("vector", lambda e, c=c, tt=tt, sq=sq: e.tensor_tensor(
                    out=sq[:], in0=src3[:, c, tt * 512:(tt + 1) * 512],
                    in1=src3[:, c, tt * 512:(tt + 1) * 512], op=ALU.mult),
                    inc=("sV", 1), wait=[src_tok, sq_rd[c % 4]])
            for ci in range(cb):
                c = c0 + ci
                sq = sqb[c % 4]
                tok_s = b.op("tensor", lambda e, c=c, tt=tt: e.matmul(
                    out=sum_bank.t[0:1, 0:512], lhsT=ones_b[:],
                    rhs=src3[:, c, tt * 512:(tt + 1) * 512],
                    start=(c == 0), stop=(c == nch - 1), skip_group_check=True),
                    inc=("sT", 1) if c == nch - 1 else None,
                    wait=(ws + [src_tok, t_init]) if c == 0 else [])
                tok_q = b.op("tensor", lambda e, c=c, sq=sq: e.matmul(
                    out=ssq_bank.t[0:1, 0:512], lhsT=ones_b[:], rhs=sq[:],
                    start=(c == 0), stop=(c == nch - 1), skip_group_check=True),
                    inc=("sT", 1),
                    wait=(wq + [t_init, tsq]) if c == 0 else [tsq])
                sq_rd[c % 4] = tok_q
        sum_bank.w = tok_s
        ssq_bank.w = tok_q
        return tok_s, tok_q

    rb_war = {}
    RV = rsc[0:1, 0:512]
    RM2 = rsc[0:1, 512:1024]
    RM = rsc[0:1, 1024:1536]
    RI = rsc[0:1, 1536:2048]

    def rows_span(src_sum, src_ssq, w, inv_dst, minv_dst, scale_idx,
                  want_mean, wt=(), keep_ri=False):
        tm = None
        if want_mean:
            tm = b.op("vector", lambda e: e.tensor_scalar(
                out=RM[:, 0:w], in0=src_sum,
                scalar1=cpool[0:1, scale_idx:scale_idx + 1], scalar2=None,
                op0=ALU.mult), inc=("sV", 1), wait=list(wt))
        tv = b.op("vector", lambda e: e.tensor_scalar(
            out=RV[:, 0:w], in0=src_ssq,
            scalar1=cpool[0:1, scale_idx:scale_idx + 1], scalar2=None,
            op0=ALU.mult), inc=("sV", 1),
            wait=list(wt) + [rb_war.pop("RV", None)])
        if want_mean:
            b.op("vector", lambda e: e.tensor_tensor(
                out=RM2[:, 0:w], in0=RM[:, 0:w], in1=RM[:, 0:w],
                op=ALU.mult), inc=("sV", 1), wait=[rb_war.pop("RM2", None)])
            tv = b.op("vector", lambda e: e.tensor_tensor(
                out=RV[:, 0:w], in0=RV[:, 0:w], in1=RM2[:, 0:w],
                op=ALU.subtract), inc=("sV", 1))
        tl = b.op("scalar", lambda e: e.activation(
            out=RM2[:, 0:w], in_=RV[:, 0:w], func=AF.Ln,
            bias=cpool[0:1, CP_EPS:CP_EPS + 1], scale=1.0), inc=("sS", 1),
            wait=[tv])
        rb_war["RV"] = tl
        ti = b.op("scalar", lambda e: e.activation(
            out=RI[:, 0:w], in_=RM2[:, 0:w], func=AF.Exp,
            scale=cpool[0:1, CP_NEGH:CP_NEGH + 1]), inc=("sS", 1),
            wait=[rb_war.pop("RI", None)])
        rb_war["RM2"] = ti
        rb_, slot, lo, hi = inv_dst
        tic = b.op("vector", lambda e, rb_=rb_, slot=slot, lo=lo, hi=hi:
                   e.tensor_copy(out=rb_[slot:slot + 1, lo:hi],
                                 in_=RI[:, 0:w]), inc=("sV", 1),
            wait=[ti, rb_war.pop((id(rb_), slot), None)])
        tmi = None
        if want_mean:
            rb2, slot2, lo2, hi2 = minv_dst
            tmi = b.op("vector", lambda e, rb2=rb2, slot2=slot2, lo2=lo2,
                       hi2=hi2: e.tensor_tensor(
                out=rb2[slot2:slot2 + 1, lo2:hi2], in0=RM[:, 0:w],
                in1=RI[:, 0:w], op=ALU.mult), inc=("sV", 1),
                wait=[ti, rb_war.pop((id(rb2), slot2), None)])
        rb_war["RI"] = tmi if tmi is not None else tic
        return tic, tmi

    def mark_row_read(rb_id, slot, tok):
        rb_war[(rb_id, slot)] = tok

    pH = contextlib.ExitStack()
    b.stacks.append(pH)
    hn = b.sb("hn", [128, DC, T], BF16, pH, side="right")
    p0 = contextlib.ExitStack()
    b.stacks.append(p0)
    ropC = b.sb("ropC", [128, T], F32, p0)
    ropS = b.sb("ropS", [128, T], F32, p0)
    xn = b.sb("xn", [128, DC, T], BF16, p0)
    qrot = b.sb("qrot", [128, DC, T], BF16, p0)
    xstg = [b.sb(f"xstg{i}", [128, T], F32, p0) for i in range(2)]
    xtmp = b.sb("xtmp", [128, 512], F32, p0)
    rtA = b.sb("rtA", [128, 512], F32, p0)
    rtB = b.sb("rtB", [128, 512], F32, p0)
    realloc_wbufs(p0, 1024)
    xbf = xn

    b.op("sync", lambda e: e.dma_start(out=ropC[:], in_=ropeC_d), inc=("sW", 16))
    t_ropes = b.op("sync", lambda e: e.dma_start(out=ropS[:], in_=ropeS_d),
                   inc=("sW", 16))

    xstg_tok = [None, None]
    for c in range(DC):
        td = b.op("sync", lambda e, c=c: e.dma_start(
            out=xstg[c % 2][:], in_=xT[c * 128:(c + 1) * 128, :]),
            inc=("sW", 16), wait=[xstg_tok[c % 2]])
        tc = b.op("vector", lambda e, c=c: e.tensor_copy(
            out=xbf[:, c, :], in_=xstg[c % 2][:]), inc=("sV", 1), wait=[td])
        xstg_tok[c % 2] = tc
    t_xbf = b.tok("sV")

    for tt in range(NTT):
        lo, hi = tt * 512, (tt + 1) * 512
        stats_tt(xbf, DC, tt, pb[4], pb[5], t_xbf)
        ti, tmi = rows_span(pb[4].t[0:1, 0:512], pb[5].t[0:1, 0:512], 512,
                            (rbB, 0, lo, hi), (rbB, 32, lo, hi), CP_I1024,
                            True, wt=pb[4].rw() + pb[5].rw())
        pb[4].r.append(ti)
        pb[5].r.append(ti)
        tb0 = bcast_row(pb[0], row(rbB, 0, lo, hi), [ti])
        mark_row_read(id(rbB), 0, tb0)
        tb1 = bcast_row(pb[1], row(rbB, 32, lo, hi), [tmi])
        mark_row_read(id(rbB), 32, tb1)
        for c in range(DC):
            b.op("vector", lambda e, c=c, lo=lo, hi=hi: e.tensor_tensor(
                out=xtmp[:], in0=xbf[:, c, lo:hi], in1=pb[0].t[:, 0:512],
                op=ALU.mult), inc=("sV", 1), wait=pb[0].rw())
            pb[0].r.append(b.tok("sV"))
            b.op("vector", lambda e, c=c, lo=lo, hi=hi: e.tensor_tensor(
                out=xn[:, c, lo:hi], in0=xtmp[:], in1=pb[1].t[:, 0:512],
                op=ALU.subtract), inc=("sV", 1), wait=pb[1].rw())
            pb[1].r.append(b.tok("sV"))
    t_xn = b.tok("sV")
    tap("xn", xn[:], t_xn)
    tap("rows", rbB[:], t_xn)

    for j in range(DC):
        sg, bf = j % 2, j % 2
        tc = stream_tile(wqT[:, j * 128:(j + 1) * 128], sg, bf, DC)
        tq = None
        for tt in range(NTT):
            lo, hi = tt * 512, (tt + 1) * 512
            bank = pb[2 + tt % 2]
            mm_acc(bank, lambda c, bf=bf: wbf[bf][:, c * 128:(c + 1) * 128],
                   lambda c, lo=lo, hi=hi: xn[:, c, lo:hi], DC,
                   fw=[tc, t_xn])
            b.op("vector", lambda e, lo=lo, hi=hi, bank=bank: e.tensor_tensor(
                out=rtA[:], in0=bank.t[:, 0:512], in1=ropC[:, lo:hi],
                op=ALU.mult), inc=("sV", 1), wait=bank.rw() + [t_ropes])
            bank.r.append(b.tok("sV"))
            for blk in range(4):
                b.op("vector", lambda e, blk=blk, lo=lo, hi=hi, bank=bank:
                     e.tensor_tensor(
                    out=rtB[blk * 32:(blk + 1) * 32, :],
                    in0=bank.t[(blk ^ 1) * 32:((blk ^ 1) + 1) * 32, 0:512],
                    in1=ropS[blk * 32:(blk + 1) * 32, lo:hi], op=ALU.mult),
                    inc=("sV", 1) if blk == 3 else None)
            bank.r.append(b.tok("sV"))
            tq = b.op("vector", lambda e, j=j, lo=lo, hi=hi: e.tensor_tensor(
                out=qrot[:, j, lo:hi], in0=rtA[:], in1=rtB[:], op=ALU.add),
                inc=("sV", 1))
        wbf_tok[bf] = tq
        b.op("scalar", lambda e, j=j: e.dma_start(
            out=qrot_scr[j * 128:(j + 1) * 128, :], in_=qrot[:, j, :]),
            inc=("sQ", 16), wait=[tq])
    t_qrot = b.tok("sV")
    tap("qrot", qrot[:], t_qrot)

    for tt in range(NTT):
        lo, hi = tt * 512, (tt + 1) * 512
        stats_tt(qrot, DC, tt, pb[4], pb[5], t_qrot)
        ti, _ = rows_span(None, pb[5].t[0:1, 0:512], 512,
                          (rbB, 64, lo, hi), None, CP_I1024, False,
                          wt=pb[5].rw())
        pb[5].r.append(ti)
        pb[4].r.append(ti)
        tb0 = bcast_row(pb[0], row(rbB, 64, lo, hi), [ti])
        mark_row_read(id(rbB), 64, tb0)
        for c in range(DC):
            b.op("vector", lambda e, c=c, lo=lo, hi=hi: e.tensor_tensor(
                out=xtmp[:], in0=qrot[:, c, lo:hi], in1=pb[0].t[:, 0:512],
                op=ALU.mult), inc=("sV", 1), wait=pb[0].rw())
            pb[0].r.append(b.tok("sV"))
            b.op("vector", lambda e, c=c, lo=lo, hi=hi: e.tensor_scalar(
                out=hn[:, c, lo:hi], in0=xtmp[:], scalar1=npw_s[:, c:c + 1],
                scalar2=None, op0=ALU.mult), inc=("sV", 1), wait=[t_consts])
    t_hn = b.tok("sV")
    tap("hn", hn[:], t_hn)

    if debug_stop == "p0":
        return finish(b, nc)
    b.barrier()
    p0.close()

    pS = contextlib.ExitStack()
    b.stacks.append(pS)
    acs_tok = b.sb("acs_tok", [128, NCHK * 32], F32, pS, side="right")
    dt_tok = b.sb("dt_tok", [128, NCHK * 32], BF16, pS, side="right")
    dcy_tok = b.sb("dcy_tok", [128, NCHK * 32], BF16, pS, side="right")
    ea_rep = b.sb("ea_rep", [128, NCHK * 32], F32, pS, side="right")
    b_tok = b.sb("b_tok", [128, T], BF16, pS, side="right")
    c_fm = b.sb("c_fm", [128, T], BF16, pS, side="right")
    cbtm = b.sb("cbtm", [128, T], BF16, pS, side="right")

    p1 = contextlib.ExitStack()
    b.stacks.append(p1)
    acs_hm = b.sb("acs_hm", [NH, T], F32, p1)
    realloc_wbufs(p1, 1024)
    convbuf = b.sb("convbuf", [128, T + 4], BF16, p1)
    convac = [b.sb(f"convac{i}", [128, T], BF16, p1) for i in range(2)]
    bfm = b.sb("bfm", [128, T], BF16, p1)
    dt_f = b.sb("dt_f", [NH, T], F32, p1)
    adt_f = b.sb("adt_f", [NH, T], F32, p1)
    dtraw = b.sb("dtraw", [NH, T], F32, p1)
    dtt1 = b.sb("dtt1", [NH, T], F32, p1)
    zeros32 = b.sb("zeros32", [NH, CH], F32, p1)
    acolt = b.sb("acolt", [NH, 2], F32, p1)
    ea_all = b.sb("ea_all", [NH, NCHK], F32, p1)
    ea_t = b.sb("ea_t", [NCHK, NH], F32, p1)
    earow = [b.sb(f"earow{i}", [1, NH], F32, p1) for i in range(2)]
    dtmp = b.sb("dtmp", [NH, CH], F32, p1)

    b.op("vector", lambda e: e.memset(zeros32[:], 0.0))
    t_cb0 = b.op("vector", lambda e: e.memset(convbuf[:, 0:4], 0.0),
                 inc=("sV", 1))

    def mm_hn(bank, bf, tt, mj=128, fw=()):
        return mm_acc(bank,
                      lambda c, bf=bf, mj=mj: wbf[bf][:, c * mj:(c + 1) * mj],
                      lambda c, tt=tt: hn[:, c, tt * 512:(tt + 1) * 512],
                      DC, mj=mj, fw=list(fw) + [t_hn])

    def conv_silu(dst_ap, chblk, ev_tok, war_tok, cbuf, cac, cb0, war2=None):
        tcv = None
        for k in range(4):
            vk = cbuf[:, 1 + k:1 + k + T]
            wk = convw_s[:, chblk * 4 + k:chblk * 4 + k + 1]
            if k == 0:
                tcv = b.op("vector", lambda e, vk=vk, wk=wk, chblk=chblk:
                           e.tensor_scalar(
                    out=cac[0][:], in0=vk, scalar1=wk,
                    scalar2=convb_s[:, chblk:chblk + 1], op0=ALU.mult,
                    op1=ALU.add), inc=("sV", 1),
                    wait=[ev_tok, t_consts, cb0, war_tok])
            else:
                tcv = b.op("vector", lambda e, vk=vk, wk=wk, k=k:
                           e.scalar_tensor_tensor(
                    out=cac[k % 2][:], in0=vk, scalar=wk,
                    in1=cac[(k - 1) % 2][:], op0=ALU.mult, op1=ALU.add),
                    inc=("sV", 1), wait=[war2] if k == 1 else [])
        return b.op("scalar", lambda e, dst_ap=dst_ap: e.activation(
            out=dst_ap, in_=cac[1][:], func=AF.Silu), inc=("sS", 1),
            wait=[tcv, war_tok])

    bc_tok = {}
    for idx, jt in enumerate((32, 33)):
        sg, bf = idx % 2, idx % 2
        tc = stream_tile(wzT[:, jt * 128:(jt + 1) * 128], sg, bf, DC)
        tev = None
        for tt in range(NTT):
            bank = pb[2 + tt % 2]
            mm_hn(bank, bf, tt, fw=[tc])
            tev = b.op("vector", lambda e, tt=tt, bank=bank: e.tensor_copy(
                out=convbuf[:, 4 + tt * 512:4 + (tt + 1) * 512],
                in_=bank.t[:, 0:512]), inc=("sV", 1), wait=bank.rw())
            bank.r.append(tev)
        wbf_tok[bf] = tev
        dst = bfm if jt == 32 else c_fm
        bc_tok[jt] = conv_silu(dst[:], 16 + idx, tev, None, convbuf, convac,
                               t_cb0, war2=bc_tok.get(32) if jt == 33 else None)
        if jt == 32:
            tap("convraw", convac[1][:], bc_tok[32])
    tap("bfm", bfm[:], bc_tok[32])
    tap("cfm", c_fm[:], bc_tok[33])

    def transpose_blocks(dst_sb, src_fn, nblk, src_tok, blkw=128):
        last = None
        per_bank = 1024 // blkw
        for g0 in range(0, nblk, per_bank):
            gn = min(per_bank, nblk - g0)
            ws = pbT.ww()
            tok = None
            for i in range(gn):
                c = g0 + i
                tok = b.op("tensor", lambda e, c=c, i=i, blkw=blkw: e.transpose(
                    out=pbT.t[:, i * blkw:(i + 1) * blkw], in_=src_fn(c),
                    identity=identb[:]),
                    inc=("sT", 1) if i == gn - 1 else None,
                    wait=(ws + [src_tok, t_consts]) if i == 0 else [])
            pbT.w = tok
            last = b.op("vector", lambda e, g0=g0, gn=gn, blkw=blkw: e.tensor_copy(
                out=dst_sb[:, g0 * blkw:(g0 + gn) * blkw],
                in_=pbT.t[:, 0:gn * blkw]), inc=("sV", 1), wait=pbT.rw())
            pbT.r.append(last)
        return last

    t_btok = transpose_blocks(b_tok, lambda c: bfm[:, c * 128:(c + 1) * 128],
                              NCHK, bc_tok[32])
    tap("btok", b_tok[:], t_btok)

    t_cbtm = None
    for g0 in range(0, NCHK, 4):
        bank = pb[2 + (g0 // 4) % 2]
        ws = bank.ww()
        tok = None
        for i in range(4):
            c = g0 + i
            tok = b.op("tensor", lambda e, c=c, i=i, bank=bank: e.matmul(
                out=bank.t[:, i * 128:(i + 1) * 128],
                lhsT=bfm[:, c * 128:(c + 1) * 128],
                rhs=c_fm[:, c * 128:(c + 1) * 128], start=True, stop=True,
                skip_group_check=True),
                inc=("sT", 1) if i == 3 else None,
                wait=(ws + [bc_tok[32], bc_tok[33]]) if i == 0 else [])
        bank.w = tok
        t_cbtm = b.op("vector", lambda e, g0=g0, bank=bank: e.tensor_copy(
            out=cbtm[:, g0 * 128:(g0 + 4) * 128], in_=bank.t[:, 0:512]),
            inc=("sV", 1), wait=bank.rw())
        bank.r.append(t_cbtm)
    tap("cbtm", cbtm[:], t_cbtm)

    tc34 = stream_tile(wzT[:, 34 * 128:34 * 128 + 32], 0, 0, DC, mj=32)
    tdv = None
    for tt in range(NTT):
        bank = pb[2 + tt % 2]
        mm_hn(bank, 0, tt, mj=32, fw=[tc34])
        tdv = b.op("vector", lambda e, tt=tt, bank=bank: e.tensor_copy(
            out=dtraw[:, tt * 512:(tt + 1) * 512], in_=bank.t[0:NH, 0:512]),
            inc=("sV", 1), wait=bank.rw())
        bank.r.append(tdv)
    wbf_tok[0] = tdv
    ta = b.op("vector", lambda e: e.tensor_scalar(
        out=dtraw[:], in0=dtraw[:], scalar1=dtb_s[:, 0:1], scalar2=None,
        op0=ALU.add), inc=("sV", 1), wait=[tdv, t_consts])
    tb_ = b.op("scalar", lambda e: e.activation(
        out=dtt1[:], in_=dtraw[:], func=AF.Abs), inc=("sS", 1), wait=[ta])
    tb_ = b.op("scalar", lambda e: e.activation(
        out=dtt1[:], in_=dtt1[:], func=AF.Exp,
        scale=cpool[0:NH, CP_NEG1:CP_NEG1 + 1]), inc=("sS", 1), wait=[tb_])
    tb_ = b.op("scalar", lambda e: e.activation(
        out=dtt1[:], in_=dtt1[:], func=AF.Ln,
        bias=cpool[0:NH, CP_ONE:CP_ONE + 1]), inc=("sS", 1), wait=[tb_])
    tdt = b.op("vector", lambda e: e.scalar_tensor_tensor(
        out=dt_f[:], in0=dtraw[:], scalar=0.0, in1=dtt1[:], op0=ALU.max,
        op1=ALU.add), inc=("sV", 1), wait=[tb_])
    tap("dt", dt_f[:], tdt)
    tap("dtraw", dtraw[:], tdt)

    tA = b.op("scalar", lambda e: e.activation(
        out=acolt[:, 0:1], in_=alog_s[:], func=AF.Exp), inc=("sS", 1),
        wait=[t_consts])
    tA = b.op("scalar", lambda e: e.mul(
        out=acolt[:, 1:2], in_=acolt[:, 0:1], mul=-1.0), inc=("sS", 1),
        wait=[tA])
    tadt = b.op("vector", lambda e: e.tensor_scalar(
        out=adt_f[:], in0=dt_f[:], scalar1=acolt[:, 1:2], scalar2=None,
        op0=ALU.mult), inc=("sV", 1), wait=[tA, tdt])
    tap("adt", adt_f[:], tadt)
    tap("acol", acolt[:], tadt)

    tacs = None
    for c in range(NCHK):
        tacs = b.op("vector", lambda e, c=c: e.tensor_tensor_scan(
            out=acs_hm[:, c * CH:(c + 1) * CH],
            data0=adt_f[:, c * CH:(c + 1) * CH], data1=zeros32[:],
            initial=0.0, op0=ALU.add, op1=ALU.add),
            inc=("sV", 1) if c == NCHK - 1 else None, wait=[tadt])
    tap("acs", acs_hm[:], tacs)

    def tok_transpose(dst, src_fn, src_toks):
        last = None
        for g0 in range(0, NCHK, 16):
            gn = min(16, NCHK - g0)
            bank = pb[6]
            ws = bank.ww()
            tok = None
            for i in range(gn):
                c = g0 + i
                tok = b.op("tensor", lambda e, c=c, i=i: e.transpose(
                    out=pb[6].t[:, i * 32:(i + 1) * 32], in_=src_fn(c),
                    identity=identf[0:32, 0:32]),
                    inc=("sT", 1) if i == gn - 1 else None,
                    wait=(ws + src_toks + [t_consts]) if i == 0 else [])
            bank.w = tok
            last = b.op("vector", lambda e, g0=g0, gn=gn: e.tensor_copy(
                out=dst[:, g0 * 32:(g0 + gn) * 32], in_=pb[6].t[:, 0:gn * 32]),
                inc=("sV", 1), wait=bank.rw())
            bank.r.append(last)
        return last

    t_acstok = tok_transpose(acs_tok, lambda c: acs_hm[:, c * CH:(c + 1) * CH],
                             [tacs])
    t_dttok = tok_transpose(dt_tok, lambda c: dt_f[:, c * CH:(c + 1) * CH],
                            [tdt])
    tap("acstok", acs_tok[:], t_acstok)
    tap("dttok", dt_tok[:], t_dttok)

    t_dcy = None
    dtmp_rd = None
    for c in range(NCHK):
        t1 = b.op("vector", lambda e, c=c: e.tensor_scalar(
            out=dtmp[:], in0=acs_hm[:, c * CH:(c + 1) * CH],
            scalar1=acs_hm[:, (c + 1) * CH - 1:(c + 1) * CH], scalar2=-1.0,
            op0=ALU.subtract, op1=ALU.mult), inc=("sV", 1),
            wait=[tacs, dtmp_rd])
        t2 = b.op("scalar", lambda e: e.activation(
            out=dtmp[:], in_=dtmp[:], func=AF.Exp), inc=("sS", 1), wait=[t1])
        bank = pb[6]
        ws = bank.ww()
        tok = b.op("tensor", lambda e, c=c: e.transpose(
            out=pb[6].t[:, (c % 16) * 32:(c % 16 + 1) * 32], in_=dtmp[:],
            identity=identf[0:32, 0:32]), inc=("sT", 1), wait=ws + [t2])
        bank.w = tok
        dtmp_rd = tok
        t_dcy = b.op("vector", lambda e, c=c: e.tensor_copy(
            out=dcy_tok[:, c * 32:(c + 1) * 32],
            in_=pb[6].t[:, (c % 16) * 32:(c % 16 + 1) * 32]), inc=("sV", 1),
            wait=bank.rw())
        bank.r.append(t_dcy)
    tap("dcytok", dcy_tok[:], t_dcy)

    tea = b.op("vector", lambda e: e.tensor_copy(
        out=ea_all[:], in_=acs_hm[:, CH - 1::CH]), inc=("sV", 1), wait=[tacs])
    tea = b.op("scalar", lambda e: e.activation(
        out=ea_all[:], in_=ea_all[:], func=AF.Exp), inc=("sS", 1), wait=[tea])
    bank = pb[6]
    ws = bank.ww()
    tek = b.op("tensor", lambda e: e.transpose(
        out=pb[6].t[0:NCHK, 0:NH], in_=ea_all[:],
        identity=identf[0:NH, 0:NH]),
        inc=("sT", 1), wait=ws + [tea])
    bank.w = tek
    tev2 = b.op("vector", lambda e: e.tensor_copy(
        out=ea_t[:], in_=pb[6].t[0:NCHK, 0:NH]), inc=("sV", 1), wait=bank.rw())
    bank.r.append(tev2)
    t_earep = None
    ear_rd = [None, None]
    for c in range(NCHK):
        tr = b.op("gpsimd", lambda e, c=c: e.dma_start(
            out=earow[c % 2][:], in_=ea_t[c:c + 1, :]), inc=("sge", 16),
            wait=[tev2, ear_rd[c % 2]])
        bank = pb[6]
        ws = bank.ww()
        tk = b.op("tensor", lambda e, c=c: e.matmul(
            out=pb[6].t[:, (c % 16) * 32:(c % 16 + 1) * 32], lhsT=ones_f[:],
            rhs=earow[c % 2][:], start=True, stop=True, skip_group_check=True),
            inc=("sT", 1), wait=ws + [tr, t_init])
        bank.w = tk
        ear_rd[c % 2] = tk
        t_earep = b.op("vector", lambda e, c=c: e.tensor_copy(
            out=ea_rep[:, c * 32:(c + 1) * 32],
            in_=pb[6].t[:, (c % 16) * 32:(c % 16 + 1) * 32]), inc=("sV", 1),
            wait=bank.rw())
        bank.r.append(t_earep)
    tap("earep", ea_rep[:], t_earep)
    t_acsw = b.op("gpsimd", lambda e: e.dma_start(out=acs_scr, in_=acs_hm[:]),
                  inc=("sG", 16), wait=[tacs])

    if debug_stop == "p1":
        return finish(b, nc)
    b.barrier()
    p1.close()

    pY = contextlib.ExitStack()
    b.stacks.append(pY)
    yg = b.sb("yg", [128, DIC, T], BF16, pY, side="right")
    p2 = contextlib.ExitStack()
    b.stacks.append(p2)
    realloc_wbufs(p2, 1024)
    cbuf2 = b.sb("cbuf2", [128, T + 4], BF16, p2)
    cac2 = [b.sb(f"cac2_{i}", [128, T], BF16, p2) for i in range(2)]
    _sz = b.sb("sz0b", [128, T], BF16, p2)
    sz = [_sz, _sz]
    _xs = b.sb("xsfm0b", [128, T], BF16, p2)
    xs_fm = [_xs, _xs]
    xs_tok = b.sb("xs_tok", [128, T], BF16, p2)
    xdt = b.sb("xdt", [128, T], BF16, p2)
    xdtd = b.sb("xdtd", [128, T], BF16, p2)
    t1a = b.sb("t1a", [128, 512], F32, p2)
    t1b = [b.sb("t1b0", [128, 512], F32, p2)] * 2
    ep_g = [b.sb(f"epg{i}", [128, 512], BF16, p2) for i in range(2)]
    er_g = [b.sb(f"erg{i}", [128, 512], BF16, p2) for i in range(2)]
    mt_b = [b.sb(f"mtb{i}", [128, 512], BF16, p2) for i in range(3)]
    cp_b = [b.sb(f"cpb{i}", [128, 512], BF16, p2) for i in range(3)]
    s_run = [b.sb(f"srun{i}", [128, 128], F32, p2) for i in range(2)]
    s_bf = [b.sb(f"sbf{i}", [128, 128], BF16, p2) for i in range(2)]
    ytmp = b.sb("ytmp", [128, 512], F32, p2)
    arow = [rbB[0:1, :], rbB[32:33, :], rbB[64:65, :], rbC[32:33, :]]
    arow_bp = [0, 32, 64, 32]

    t_cb2 = b.op("vector", lambda e: e.memset(cbuf2[:, 0:4], 0.0), inc=("sV", 1))
    t_ssq0 = b.op("vector", lambda e: e.memset(row(rbC, 0), 0.0), inc=("sV", 1))

    acs3 = acs_tok[:].rearrange("p (c q) -> p c q", q=32)
    dt3 = dt_tok[:].rearrange("p (c q) -> p c q", q=32)
    dcy3 = dcy_tok[:].rearrange("p (c q) -> p c q", q=32)

    arow_tok = {}
    arow_rd = {}

    def dma_acs_row(h, slot):
        return b.op("gpsimd", lambda e, h=h, slot=slot: e.dma_start(
            out=arow[slot][:], in_=acs_scr[h:h + 1, :]), inc=(f"sg{slot}", 16),
            wait=[t_acsw, arow_rd.get(slot)])

    arow_tok[(0, 0)] = dma_acs_row(0, 0)
    arow_tok[(0, 1)] = dma_acs_row(1, 1)

    sz_rd = [None, None]
    xs_rd = [None, None]
    conv_s_war = bc_tok[33]
    ygsq_rd = None
    snapT = ("sT", 0)
    war_ep = [None, None]
    war_er = [None, None]
    war_t1b = [None, None]
    mt_war = [None, None, None]
    cp_war = [None, None, None]
    s_bf_rd = [[None, None], [None, None]]

    for jp in range(DIC):
        par = jp % 2
        if jp + 1 < DIC:
            npar = (jp + 1) % 2
            arow_tok[(jp + 1, 0)] = dma_acs_row(2 * (jp + 1), 2 * npar)
            arow_tok[(jp + 1, 1)] = dma_acs_row(2 * (jp + 1) + 1, 2 * npar + 1)

        bfz = jp % 2
        tcz = stream_tile(wzT[:, jp * 128:(jp + 1) * 128], jp % 2, bfz, DC)
        tsz = None
        for tt in range(NTT):
            bank = pb[2 + tt % 2]
            mm_hn(bank, bfz, tt, fw=[tcz])
            tsz = b.op("scalar", lambda e, tt=tt, par=par, bank=bank: e.activation(
                out=sz[par][:, tt * 512:(tt + 1) * 512], in_=bank.t[:, 0:512],
                func=AF.Silu), inc=("sS", 1), wait=bank.rw() + [sz_rd[par]])
            bank.r.append(tsz)
        wbf_tok[bfz] = tsz
        if jp == 0:
            tap("sz0", sz[0][:], tsz)

        bfx = (jp + 1) % 2
        tcx = stream_tile(wzT[:, (16 + jp) * 128:(17 + jp) * 128],
                          (jp + 1) % 2, bfx, DC)
        tev = None
        for tt in range(NTT):
            bank = pb[2 + tt % 2]
            mm_hn(bank, bfx, tt, fw=[tcx])
            tev = b.op("vector", lambda e, tt=tt, bank=bank: e.tensor_copy(
                out=cbuf2[:, 4 + tt * 512:4 + (tt + 1) * 512],
                in_=bank.t[:, 0:512]), inc=("sV", 1), wait=bank.rw())
            bank.r.append(tev)
        wbf_tok[bfx] = tev
        txs = conv_silu(xs_fm[par][:], jp, tev, xs_rd[par], cbuf2, cac2,
                        t_cb2, war2=conv_s_war)
        conv_s_war = txs
        if jp == 0:
            tap("xs0", xs_fm[0][:], txs)

        t_xstok = transpose_blocks(
            xs_tok, lambda c, par=par: xs_fm[par][:, c * 128:(c + 1) * 128],
            NCHK, txs)
        dtb_ap = dt3[:, :, 2 * jp:2 * jp + 2].unsqueeze(3) \
            .broadcast_to([128, NCHK, 2, 64])
        dcyb_ap = dcy3[:, :, 2 * jp:2 * jp + 2].unsqueeze(3) \
            .broadcast_to([128, NCHK, 2, 64])
        txdt = b.op("vector", lambda e, dtb_ap=dtb_ap: e.tensor_tensor(
            out=xdt[:].rearrange("p (c h q) -> p c h q", c=NCHK, h=2),
            in0=xs_tok[:].rearrange("p (c h q) -> p c h q", c=NCHK, h=2),
            in1=dtb_ap, op=ALU.mult), inc=("sV", 1),
            wait=[t_xstok, t_dttok, snapT])
        txdtd = b.op("vector", lambda e, dcyb_ap=dcyb_ap: e.tensor_tensor(
            out=xdtd[:].rearrange("p (c h q) -> p c h q", c=NCHK, h=2),
            in0=xdt[:].rearrange("p (c h q) -> p c h q", c=NCHK, h=2),
            in1=dcyb_ap, op=ALU.mult), inc=("sV", 1), wait=[t_dcy])

        NG = NCHK // 4
        prod = [(g, hh) for g in range(NG) for hh in range(2)]
        mt_tok = {}
        cp_tok = {}
        s_run_tok = [None, None]
        s_bf_tok = [None, None]

        def produce(k):
            g, hh = prod[k]
            h = 2 * jp + hh
            k3 = k % 3
            rtok = arow_tok[(jp, hh)]
            arw = arow[2 * par + hh]
            rbank = pb[4 + k % 2]
            ws = rbank.ww()
            tokR = None
            bp = arow_bp[2 * par + hh]
            lhR = onesF[bp:bp + 1, :]
            for i in range(4):
                c = 4 * g + i
                tokR = b.op("tensor", lambda e, c=c, i=i, arw=arw, lhR=lhR,
                            rbank=rbank: e.matmul(
                    out=rbank.t[:, i * 128:(i + 1) * 128], lhsT=lhR,
                    rhs=arw[:, c * CH:(c + 1) * CH], start=True, stop=True,
                    skip_group_check=True),
                    inc=("sT", 1) if i == 3 else None,
                    wait=(ws + [rtok, t_init]) if i == 0 else [])
            rbank.w = tokR
            arow_rd[2 * par + hh] = tokR
            colrep = acs3[:, 4 * g:4 * g + 4, h:h + 1] \
                .broadcast_to([128, 4, 128])
            mrep = masks[:].unsqueeze(1).broadcast_to([128, 4, 128])
            tt1 = b.op("vector", lambda e, colrep=colrep, rbank=rbank:
                       e.tensor_tensor(
                out=t1a[:].rearrange("p (c l) -> p c l", c=4),
                in0=rbank.t[:, 0:512].rearrange("p (c l) -> p c l", c=4),
                in1=colrep, op=ALU.subtract), inc=("sV", 1),
                wait=rbank.rw() + [t_acstok])
            rbank.r.append(tt1)
            k2 = k % 2
            tt1 = b.op("vector", lambda e, mrep=mrep, k2=k2: e.tensor_tensor(
                out=t1b[k2][:].rearrange("p (c l) -> p c l", c=4),
                in0=t1a[:].rearrange("p (c l) -> p c l", c=4),
                in1=mrep, op=ALU.add), inc=("sV", 1),
                wait=[t_consts, war_t1b[k2]])
            tep = b.op("scalar", lambda e, k2=k2: e.activation(
                out=ep_g[k2][:], in_=t1b[k2][:], func=AF.Exp),
                inc=("sS", 1), wait=[tt1, war_ep[k2]])
            war_t1b[k2] = tep
            ter = b.op("scalar", lambda e, k2=k2, rbank=rbank: e.activation(
                out=er_g[k2][:], in_=rbank.t[:, 0:512], func=AF.Exp),
                inc=("sS", 1), wait=rbank.rw() + [war_er[k2]])
            rbank.r.append(ter)
            tmt = b.op("vector", lambda e, g=g, k2=k2, k3=k3: e.tensor_tensor(
                out=mt_b[k3][:], in0=cbtm[:, g * 512:(g + 1) * 512],
                in1=ep_g[k2][:], op=ALU.mult), inc=("sV", 1),
                wait=[tep, t_cbtm, mt_war[k3]])
            war_ep[k2] = tmt
            tcp = b.op("vector", lambda e, g=g, k2=k2, k3=k3: e.tensor_tensor(
                out=cp_b[k3][:], in0=c_fm[:, g * 512:(g + 1) * 512],
                in1=er_g[k2][:], op=ALU.mult), inc=("sV", 1),
                wait=[ter, bc_tok[33], cp_war[k3]])
            war_er[k2] = tcp
            mt_tok[k] = tmt
            cp_tok[k] = tcp

        def consume(g):
            k0 = 2 * g
            ybank = pb[6]
            ws_y = ybank.ww()
            tok_y = None
            for i in range(4):
                c = 4 * g + i
                pc, pn = c % 2, (c + 1) % 2
                for hh in range(2):
                    h = 2 * jp + hh
                    k3 = (k0 + hh) % 3
                    tok_y = b.op("tensor", lambda e, c=c, i=i, hh=hh, k3=k3,
                                 ybank=ybank: e.matmul(
                        out=ybank.t[hh * 64:(hh + 1) * 64,
                                    i * 128:(i + 1) * 128],
                        lhsT=xdt[:, c * 128 + hh * 64:c * 128 + (hh + 1) * 64],
                        rhs=mt_b[k3][:, i * 128:(i + 1) * 128],
                        start=True, stop=(c == 0), skip_group_check=True),
                        inc=("sT", 1) if c == 0 else None,
                        wait=(ws_y if (i == 0 and hh == 0) else []) +
                        [mt_tok[k0 + hh], txdt])
                    if c > 0:
                        tok_y = b.op("tensor", lambda e, i=i, hh=hh, k3=k3,
                                     pc=pc, ybank=ybank: e.matmul(
                            out=ybank.t[hh * 64:(hh + 1) * 64,
                                        i * 128:(i + 1) * 128],
                            lhsT=s_bf[pc][:, hh * 64:(hh + 1) * 64],
                            rhs=cp_b[k3][:, i * 128:(i + 1) * 128],
                            start=False, stop=True, skip_group_check=True),
                            inc=("sT", 1),
                            wait=[cp_tok[k0 + hh], s_bf_tok[pc]])
                        s_bf_rd[pc][hh] = tok_y
                for hh in range(2):
                    h = 2 * jp + hh
                    sbank = pb[2 + c % 2]
                    ws_s = sbank.ww()
                    b.op("tensor", lambda e, c=c, hh=hh, sbank=sbank: e.matmul(
                        out=sbank.t[:, hh * 64:(hh + 1) * 64],
                        lhsT=b_tok[:, c * 128:(c + 1) * 128],
                        rhs=xdtd[:, c * 128 + hh * 64:c * 128 + (hh + 1) * 64],
                        start=True, stop=True, skip_group_check=True),
                        inc=("sT", 1),
                        wait=(ws_s + [txdtd, t_btok]) if hh == 0 else [])
                    sbank.w = b.tok("sT")
                for hh in range(2):
                    h = 2 * jp + hh
                    sbank = pb[2 + c % 2]
                    if c == 0:
                        tsu = b.op("vector", lambda e, hh=hh, pn=pn,
                                   sbank=sbank: e.tensor_copy(
                            out=s_run[pn][:, hh * 64:(hh + 1) * 64],
                            in_=sbank.t[:, hh * 64:(hh + 1) * 64]),
                            inc=("sV", 1), wait=sbank.rw())
                    else:
                        eacol = ea_rep[:, c * 32 + h:c * 32 + h + 1]
                        tsu = b.op("vector", lambda e, hh=hh, pc=pc, pn=pn,
                                   eacol=eacol, sbank=sbank:
                                   e.scalar_tensor_tensor(
                            out=s_run[pn][:, hh * 64:(hh + 1) * 64],
                            in0=s_run[pc][:, hh * 64:(hh + 1) * 64],
                            scalar=eacol,
                            in1=sbank.t[:, hh * 64:(hh + 1) * 64],
                            op0=ALU.mult, op1=ALU.add), inc=("sV", 1),
                            wait=sbank.rw() + [t_earep, s_run_tok[pn]])
                    sbank.r.append(tsu)
                    s_run_tok[pn] = tsu
                    tsb = b.op("scalar", lambda e, hh=hh, pn=pn: e.copy(
                        out=s_bf[pn][:, hh * 64:(hh + 1) * 64],
                        in_=s_run[pn][:, hh * 64:(hh + 1) * 64]),
                        inc=("sS", 1), wait=[tsu, s_bf_rd[pn][hh]])
                    s_bf_tok[pn] = tsb
            ybank.w = tok_y
            mt_war[k0 % 3] = tok_y
            mt_war[(k0 + 1) % 3] = tok_y
            cp_war[k0 % 3] = tok_y
            cp_war[(k0 + 1) % 3] = tok_y
            tye = b.op("vector", lambda e, g=g, jp=jp, par=par, ybank=ybank:
                       e.scalar_tensor_tensor(
                out=ytmp[:], in0=xs_fm[par][:, g * 512:(g + 1) * 512],
                scalar=drep_s[:, jp:jp + 1], in1=ybank.t[:, 0:512],
                op0=ALU.mult, op1=ALU.add), inc=("sV", 1),
                wait=ybank.rw() + [txs, t_consts])
            ybank.r.append(tye)
            tyg = b.op("vector", lambda e, g=g, jp=jp, par=par:
                       e.tensor_tensor(
                out=yg[:, jp, g * 512:(g + 1) * 512], in0=ytmp[:],
                in1=sz[par][:, g * 512:(g + 1) * 512], op=ALU.mult),
                inc=("sV", 1), wait=[tsz])
            xs_rd[par] = tye
            sz_rd[par] = tyg
            return tyg

        produce(0)
        produce(1)
        produce(2)
        tyg = None
        for g in range(NG):
            tyg = consume(g)
            if 2 * g + 3 < 2 * NG:
                produce(2 * g + 3)
            if 2 * g + 4 < 2 * NG:
                produce(2 * g + 4)
        arow_tok.pop((jp, 0))
        arow_tok.pop((jp, 1))
        t_ygj = b.tok("sV")

        for tt in range(NTT):
            tq = b.op("scalar", lambda e, jp=jp, tt=tt: e.activation(
                out=sqb[0][:], in_=yg[:, jp, tt * 512:(tt + 1) * 512],
                func=AF.Square), inc=("sS", 1), wait=[t_ygj, ygsq_rd])
            bank = pb[2 + tt % 2]
            ws = bank.ww()
            tokq = b.op("tensor", lambda e, bank=bank: e.matmul(
                out=bank.t[0:1, 0:512], lhsT=ones_b[:], rhs=sqb[0][:],
                start=True, stop=True, skip_group_check=True), inc=("sT", 1),
                wait=ws + [tq, t_init])
            bank.w = tokq
            ygsq_rd = tokq
            b.op("vector", lambda e, tt=tt, bank=bank: e.tensor_tensor(
                out=row(rbC, 0, tt * 512, (tt + 1) * 512),
                in0=row(rbC, 0, tt * 512, (tt + 1) * 512),
                in1=bank.t[0:1, 0:512], op=ALU.add), inc=("sV", 1),
                wait=bank.rw() + [t_ssq0])
            bank.r.append(b.tok("sV"))
        snapT = b.tok("sT")
    t_ssqg = b.tok("sV")
    tap("yg", yg[:], t_ssqg)

    if debug_stop == "p2":
        return finish(b, nc)

    b.barrier()
    p2.close()

    p2t = contextlib.ExitStack()
    b.stacks.append(p2t)
    pg2t = guard()
    realloc_wbufs(p2t, 1024)
    rgb_sb = b.sb("rgb_sb", [128, T], BF16, p2t)
    qdbuf = b.sb("qdbuf", [128, T], BF16, p2t)
    qre = [b.sb(f"qre{i}", [128, T], BF16, p2t) for i in range(2)]
    oab = [b.sb(f"oab{i}", [128, T], BF16, p2t) for i in range(2)]
    oare = [b.sb(f"oare{i}", [128, T], BF16, p2t) for i in range(2)]
    t2tmp = b.sb("t2tmp", [128, 512], F32, p2t)

    for tt in range(NTT):
        lo, hi = tt * 512, (tt + 1) * 512
        ti, _ = rows_span(None, rbC[0:1, lo:hi], 512, (rbB, 0, lo, hi), None,
                          CP_I2048, False, wt=[t_ssqg])
        tb0 = bcast_row(pb[0], row(rbB, 0, lo, hi), [ti])
        mark_row_read(id(rbB), 0, tb0)
        b.op("vector", lambda e, lo=lo, hi=hi: e.tensor_copy(
            out=rgb_sb[:, lo:hi], in_=pb[0].t[:, 0:512]), inc=("sV", 1),
            wait=pb[0].rw())
        pb[0].r.append(b.tok("sV"))
    t_rgb = b.tok("sV")
    tap("rgb", rgb_sb[:], t_rgb)

    qre_rd = [None, None]
    oab_rd = [None, None]
    t_qscr = b.tok("sQ")
    for j in range(DC):
        tqr = b.op("sync", lambda e, j=j: e.dma_start(
            out=qre[j % 2][:], in_=qrot_scr[j * 128:(j + 1) * 128, :]),
            inc=("sW", 16), wait=[qre_rd[j % 2]] +
            (pg2t + [t_qscr] if j < 2 else []))
        tc_half = [
            stream_tile(woT[(ks * 8) * 128:(ks * 8 + 8) * 128,
                            j * 128:(j + 1) * 128], ks, ks, DC,
                        scale_base=ks * 8)
            for ks in range(2)]
        if j == 0 and "wdump" in tap_out:
            b.op("scalar", lambda e: e.dma_start(
                out=tap_out["wdump"], in_=wbf[0][:, 0:1024]),
                inc=("sQ", 16), wait=tc_half)
        touta = None
        for tt in range(NTT):
            lo, hi = tt * 512, (tt + 1) * 512
            bank = pb[2 + tt % 2]
            ws = bank.ww()
            tok = None
            for kk in range(16):
                ks, c = kk // 8, kk % 8
                lh = wbf[ks][:, c * 128:(c + 1) * 128]
                tok = b.op("tensor", lambda e, lh=lh, ks=ks, c=c, lo=lo, hi=hi,
                           kk=kk, bank=bank: e.matmul(
                    out=bank.t[:, 0:512], lhsT=lh,
                    rhs=yg[:, ks * 8 + c, lo:hi], start=(kk == 0),
                    stop=(kk == 15)),
                    inc=("sT", 1) if kk == 15 else None,
                    wait=(ws + tc_half + [t_ssqg]) if kk == 0 else [])
            bank.w = tok
            if j == 0 and "pout" in tap_out:
                tpd = b.op("vector", lambda e, lo=lo, hi=hi, bank=bank:
                           e.tensor_copy(out=oare[1][:, lo:hi],
                                         in_=bank.t[:, 0:512]),
                           inc=("sV", 1), wait=bank.rw())
                bank.r.append(tpd)
                b.op("scalar", lambda e, lo=lo, hi=hi: e.dma_start(
                    out=tap_out["pout"][:, lo:hi],
                    in_=oare[1][:, lo:hi].bitcast(F32)[:, 0:(hi - lo) // 2]
                    if False else oare[1][:, lo:hi]), inc=("sQ", 16),
                    wait=[tpd])
            tv1 = b.op("vector", lambda e, lo=lo, hi=hi, bank=bank:
                       e.tensor_tensor(
                out=t2tmp[:], in0=bank.t[:, 0:512], in1=rgb_sb[:, lo:hi],
                op=ALU.mult), inc=("sV", 1), wait=bank.rw() + [t_rgb])
            bank.r.append(tv1)
            touta = b.op("vector", lambda e, j=j, lo=lo, hi=hi: e.tensor_tensor(
                out=oab[j % 2][:, lo:hi], in0=t2tmp[:],
                in1=qre[j % 2][:, lo:hi], op=ALU.add), inc=("sV", 1),
                wait=[tqr, oab_rd[j % 2]])
        if j == 0 and "qredump" in tap_out:
            tqd = b.op("vector", lambda e: e.tensor_copy(
                out=qdbuf[:], in_=qre[0][:]), inc=("sV", 1), wait=[touta])
            b.op("scalar", lambda e: e.dma_start(
                out=tap_out["qredump"], in_=qdbuf[:]), inc=("sQ", 16),
                wait=[tqd])
            touta = tqd
        qre_rd[j % 2] = touta
        if j == 0 and "qre0" in tap_out:
            b.op("scalar", lambda e: e.dma_start(
                out=tap_out["qre0"], in_=qre[0][:]), inc=("sQ", 16),
                wait=[touta])
        wbf_tok[0] = touta
        wbf_tok[1] = touta
        tdo = b.op("scalar", lambda e, j=j: e.dma_start(
            out=outa_scr[j * 128:(j + 1) * 128, :], in_=oab[j % 2][:]),
            inc=("sQ", 16), wait=[touta])
        if "outa" in tap_out:
            b.op("scalar", lambda e, j=j: e.dma_start(
                out=tap_out["outa"][:, j, :], in_=oab[j % 2][:]),
                inc=("sQ", 16), wait=[touta])
        oab_rd[j % 2] = tdo

    b.barrier()
    pY.close()
    pS.close()
    pH.close()
    p2u = contextlib.ExitStack()
    b.stacks.append(p2u)
    pg2u = guard()
    i2b = b.sb("i2b", [128, T], F32, p2u)
    m2b = b.sb("m2b", [128, T], F32, p2u)
    x1c = [b.sb(f"x1c{i}", [128, T], F32, p2u) for i in range(2)]
    x1bt = b.sb("x1bt", [128, 512], BF16, p2u)
    xstg2 = [b.sb(f"xstg2_{i}", [128, T], F32, p2u) for i in range(2)]

    tz2 = b.op("vector", lambda e: e.memset(row(rbC, 32), 0.0), inc=("sV", 1))
    tz2b = b.op("vector", lambda e: e.memset(row(rbC, 64), 0.0), inc=("sV", 1))
    oare_rd = [None, None]
    for c in range(DC):
        tor = b.op("sync", lambda e, c=c: e.dma_start(
            out=oare[c % 2][:], in_=outa_scr[c * 128:(c + 1) * 128, :]),
            inc=("sW", 16), wait=[oare_rd[c % 2]] +
            (pg2t + [b.tok("sQ")] if c < 2 else []))
        for tt in range(NTT):
            lo, hi = tt * 512, (tt + 1) * 512
            tsq = b.op("scalar", lambda e, c=c, lo=lo, hi=hi: e.activation(
                out=sqb[0][:], in_=oare[c % 2][:, lo:hi], func=AF.Square),
                inc=("sS", 1), wait=[tor, ygsq_rd])
            bank = pb[4 + tt % 2]
            ws = bank.ww()
            b.op("tensor", lambda e, c=c, lo=lo, hi=hi, bank=bank: e.matmul(
                out=bank.t[0:1, 0:512], lhsT=ones_b[:],
                rhs=oare[c % 2][:, lo:hi], start=True, stop=True,
                skip_group_check=True), inc=("sT", 1),
                wait=ws + [tor, t_init])
            bank.w = b.tok("sT")
            b.op("vector", lambda e, lo=lo, hi=hi, bank=bank: e.tensor_tensor(
                out=row(rbC, 32, lo, hi), in0=row(rbC, 32, lo, hi),
                in1=bank.t[0:1, 0:512], op=ALU.add), inc=("sV", 1),
                wait=bank.rw() + [tz2])
            bank.r.append(b.tok("sV"))
            bank2 = pb[6]
            ws2 = bank2.ww()
            tq2 = b.op("tensor", lambda e, bank2=bank2: e.matmul(
                out=bank2.t[0:1, 0:512], lhsT=ones_b[:], rhs=sqb[0][:],
                start=True, stop=True, skip_group_check=True), inc=("sT", 1),
                wait=ws2 + [tsq])
            bank2.w = tq2
            ygsq_rd = tq2
            tlast = b.op("vector", lambda e, lo=lo, hi=hi, bank2=bank2:
                         e.tensor_tensor(
                out=row(rbC, 64, lo, hi), in0=row(rbC, 64, lo, hi),
                in1=bank2.t[0:1, 0:512], op=ALU.add), inc=("sV", 1),
                wait=bank2.rw() + [tz2b])
            bank2.r.append(tlast)
        oare_rd[c % 2] = tlast

    for tt in range(NTT):
        lo, hi = tt * 512, (tt + 1) * 512
        ti, tmi = rows_span(rbC[32:33, lo:hi], rbC[64:65, lo:hi], 512,
                            (rbB, 32, lo, hi), (rbB, 64, lo, hi), CP_I1024,
                            True)
        tb0 = bcast_row(pb[0], row(rbB, 32, lo, hi), [ti])
        mark_row_read(id(rbB), 32, tb0)
        b.op("vector", lambda e, lo=lo, hi=hi: e.tensor_copy(
            out=i2b[:, lo:hi], in_=pb[0].t[:, 0:512]), inc=("sV", 1),
            wait=pb[0].rw())
        pb[0].r.append(b.tok("sV"))
        tb1 = bcast_row(pb[1], row(rbB, 64, lo, hi), [tmi])
        mark_row_read(id(rbB), 64, tb1)
        b.op("vector", lambda e, lo=lo, hi=hi: e.tensor_copy(
            out=m2b[:, lo:hi], in_=pb[1].t[:, 0:512]), inc=("sV", 1),
            wait=pb[1].rw())
        pb[1].r.append(b.tok("sV"))
    t_ln2b = b.tok("sV")

    tz3 = b.op("vector", lambda e: e.memset(row(rbC, 0), 0.0), inc=("sV", 1))
    tz3b = b.op("vector", lambda e: e.memset(row(rbC, 96), 0.0), inc=("sV", 1))
    xst2_rd = [None, None]
    x1_rd = [None, None]
    oare_rd2 = list(oare_rd)
    for c in range(DC):
        td = b.op("sync", lambda e, c=c: e.dma_start(
            out=xstg2[c % 2][:], in_=xT[c * 128:(c + 1) * 128, :]),
            inc=("sW", 16), wait=[xst2_rd[c % 2]] + (pg2u if c < 2 else []))
        tor = b.op("sync", lambda e, c=c: e.dma_start(
            out=oare[c % 2][:], in_=outa_scr[c * 128:(c + 1) * 128, :]),
            inc=("sW", 16), wait=[oare_rd2[c % 2]])
        if "oarall" in tap_out:
            b.op("scalar", lambda e, c=c: e.dma_start(
                out=tap_out["oarall"][:, c, :], in_=oare[c % 2][:]),
                inc=("sQ", 16), wait=[tor])
        tx1 = None
        for tt in range(NTT):
            lo, hi = tt * 512, (tt + 1) * 512
            b.op("vector", lambda e, c=c, lo=lo, hi=hi: e.tensor_tensor(
                out=t2tmp[:], in0=oare[c % 2][:, lo:hi], in1=i2b[:, lo:hi],
                op=ALU.mult), inc=("sV", 1), wait=[t_ln2b, x1_rd[c % 2], tor])
            b.op("vector", lambda e, lo=lo, hi=hi: e.tensor_tensor(
                out=t2tmp[:], in0=t2tmp[:], in1=m2b[:, lo:hi],
                op=ALU.subtract), inc=("sV", 1))
            tx1 = b.op("vector", lambda e, c=c, lo=lo, hi=hi: e.tensor_tensor(
                out=x1c[c % 2][:, lo:hi], in0=t2tmp[:],
                in1=xstg2[c % 2][:, lo:hi], op=ALU.add), inc=("sV", 1),
                wait=[td])
            tcb = b.op("vector", lambda e, c=c, lo=lo, hi=hi: e.tensor_copy(
                out=x1bt[:], in_=x1c[c % 2][:, lo:hi]), inc=("sV", 1))
            tsq = b.op("scalar", lambda e, c=c, lo=lo, hi=hi: e.activation(
                out=sqb[0][:], in_=x1c[c % 2][:, lo:hi], func=AF.Square),
                inc=("sS", 1), wait=[tx1, ygsq_rd])
            bank = pb[4 + tt % 2]
            ws = bank.ww()
            b.op("tensor", lambda e, bank=bank: e.matmul(
                out=bank.t[0:1, 0:512], lhsT=ones_b[:], rhs=x1bt[:],
                start=True, stop=True, skip_group_check=True), inc=("sT", 1),
                wait=ws + [tcb, t_init])
            bank.w = b.tok("sT")
            b.op("vector", lambda e, lo=lo, hi=hi, bank=bank: e.tensor_tensor(
                out=row(rbC, 0, lo, hi), in0=row(rbC, 0, lo, hi),
                in1=bank.t[0:1, 0:512], op=ALU.add), inc=("sV", 1),
                wait=bank.rw() + [tz3])
            bank.r.append(b.tok("sV"))
            bank2 = pb[6]
            ws2 = bank2.ww()
            tq2 = b.op("tensor", lambda e, bank2=bank2: e.matmul(
                out=bank2.t[0:1, 0:512], lhsT=ones_b[:], rhs=sqb[0][:],
                start=True, stop=True, skip_group_check=True), inc=("sT", 1),
                wait=ws2 + [tsq])
            bank2.w = tq2
            ygsq_rd = tq2
            tlast = b.op("vector", lambda e, lo=lo, hi=hi, bank2=bank2:
                         e.tensor_tensor(
                out=row(rbC, 96, lo, hi), in0=row(rbC, 96, lo, hi),
                in1=bank2.t[0:1, 0:512], op=ALU.add), inc=("sV", 1),
                wait=bank2.rw() + [tz3b])
            bank2.r.append(tlast)
        tdx = b.op("scalar", lambda e, c=c: e.dma_start(
            out=x1_scr[c * 128:(c + 1) * 128, :], in_=x1c[c % 2][:]),
            inc=("sQ", 16), wait=[tx1, tlast])
        if "x1" in tap_out:
            b.op("scalar", lambda e, c=c: e.dma_start(
                out=tap_out["x1"][:, c, :], in_=x1c[c % 2][:]),
                inc=("sQ", 16), wait=[tx1])
        x1_rd[c % 2] = tdx
        xst2_rd[c % 2] = tx1
        oare_rd2[c % 2] = tx1

    t_i3 = t_mi3 = None
    for tt in range(NTT):
        lo, hi = tt * 512, (tt + 1) * 512
        t_i3, t_mi3 = rows_span(rbC[0:1, lo:hi], rbC[96:97, lo:hi], 512,
                                (rbB, 32, lo, hi), (rbB, 64, lo, hi),
                                CP_I1024, True)

    if "qscr" in tap_out:
        tlate = b.op("sync", lambda e: e.dma_start(
            out=qdbuf[:], in_=qrot_scr[0:128, :]), inc=("sW", 16),
            wait=[b.tok("sV"), b.tok("sQ")])
        tld = b.op("vector", lambda e: e.tensor_copy(
            out=oare[0][:], in_=qdbuf[:]), inc=("sV", 1), wait=[tlate])
        b.op("scalar", lambda e: e.dma_start(
            out=tap_out["qscr"], in_=oare[0][:]), inc=("sQ", 16), wait=[tld])
    if debug_stop == "p2t":
        return finish(b, nc)

    b.barrier()
    p2u.close()
    p2t.close()

    if "canary" in tap_out:
        tcz1 = b.op("vector", lambda e: e.memset(canary[:, 0:4], 0.0),
                    inc=("sV", 1))
        tcz2 = b.op("scalar", lambda e: e.activation(
            out=canary[:, 4:8], in_=canary[:, 0:4], func=AF.Exp),
            inc=("sS", 1), wait=[tcz1])
        tap("canary", canary[:], tcz2)

    p3 = contextlib.ExitStack()
    b.stacks.append(p3)
    pg3 = guard()
    realloc_wbufs(p3, HF, nbf=6)
    selrw_s = b.sb("selrw_s", [NE, NE * 128], BF16, p3)
    b.op("sync", lambda e: e.dma_start(out=selrw_s[:], in_=selrw_d[0:NE]),
         inc=("sW", 16), wait=pg3)
    xf = b.sb("xf", [128, DC, TH], BF16, p3)
    i3b = b.sb("i3b", [128, TH], F32, p3)
    m3b = b.sb("m3b", [128, TH], F32, p3)
    xstg3 = [b.sb(f"xstg3_{i}", [128, TH], F32, p3) for i in range(2)]
    t3tmp = b.sb("t3tmp", [128, 512], F32, p3)
    ymoe = b.sb("ymoe", [128, DC, TH], F32, p3)
    he = b.sb("he", [128, HFC, TH], BF16, p3)
    hprod = b.sb("hprod", [128, TH], F32, p3)
    hsil = b.sb("hsil", [128, TH], BF16, p3)
    rwb_sb = b.sb("rwb_sb", [128, TH], BF16, p3)
    ga_t = b.sb("ga_t", [128, 512], F32, p3)
    gb_t = b.sb("gb_t", [128, 512], F32, p3)
    NTI = TH // 128
    glog = b.sb("glog", [128, NTI * NE], F32, p3)
    gsc1 = b.sb("gsc1", [128, NTI], F32, p3)
    gsc2 = b.sb("gsc2", [128, NTI * NE], F32, p3)
    gsc3 = b.sb("gsc3", [128, NTI * NE], F32, p3)
    rw_f = b.sb("rw_f", [128, NTI * NE], F32, p3)
    rw_b16 = b.sb("rw_b16", [128, NTI * NE], BF16, p3)
    rw_fm = b.sb("rw_fm", [NE, TH], BF16, p3)
    gw_s = b.sb("gw_s", [128, DC * NE], F32, p3)
    gw_b = b.sb("gw_b", [128, DC * NE], BF16, p3)
    gbr_s = b.sb("gbr_s", [128, NTI * NE], F32, p3)
    b2_s = b.sb("b2_s", [NE, D], F32, p3)
    b2_b = b.sb("b2_b", [NE, D], BF16, p3)
    bias1 = b.sb("bias1", [128, HFC], F32, p3)
    bias3 = b.sb("bias3", [128, HFC], F32, p3)
    sbias1 = b.sb("sbias1", [128, HFC], F32, p3)
    sbias3 = b.sb("sbias3", [128, HFC], F32, p3)
    sbias2 = b.sb("sbias2", [128, DC], F32, p3)
    NTTH = TH // 512

    b.op("sync", lambda e: e.dma_start(
        out=gw_s[:].rearrange("p (c m) -> p c m", c=DC),
        in_=gw_d.rearrange("(c p) m -> p c m", p=128)), inc=("sW", 16),
        wait=pg3)
    b.op("sync", lambda e: e.dma_start(out=gbr_s[:], in_=gb_d[:, 0:NTI * NE]),
         inc=("sW", 16))
    b.op("sync", lambda e: e.dma_start(out=b2_s[:], in_=b2_d[0:NE]), inc=("sW", 16))
    b.op("sync", lambda e: e.dma_start(out=sbias1[:], in_=sb1_d), inc=("sW", 16))
    b.op("sync", lambda e: e.dma_start(out=sbias3[:], in_=sb3_d), inc=("sW", 16))
    b.op("sync", lambda e: e.dma_start(out=sbias2[:], in_=sb2_d), inc=("sW", 16))
    t_gwl = b.tok("sW")
    tgw = b.op("vector", lambda e: e.tensor_copy(out=gw_b[:], in_=gw_s[:]),
               inc=("sV", 1), wait=[t_gwl])
    tb2c = b.op("vector", lambda e: e.tensor_copy(out=b2_b[:], in_=b2_s[:]),
                inc=("sV", 1), wait=[t_gwl])

    tz4 = b.op("vector", lambda e: e.memset(row(rbC, 32), 0.0), inc=("sV", 1))
    tz4b = b.op("vector", lambda e: e.memset(row(rbC, 64), 0.0), inc=("sV", 1))

    he_rd = None
    rwb_rd = None
    bias_rd = None
    wflip13 = [0]
    wflip2 = [0]
    hsil_rd = None
    gab_rd = None
    ymoe_scr_rd = None
    xst3_rd = [None, None]
    ygsq_rd4 = ygsq_rd

    for th in range(2):
        base = th * TH
        t_xf = None
        for c in range(DC):
            td = b.op("sync", lambda e, c=c, base=base: e.dma_start(
                out=xstg3[c % 2][:],
                in_=x1_scr[c * 128:(c + 1) * 128, base:base + TH]),
                inc=("sW", 16), wait=[xst3_rd[c % 2]] +
                (pg3 + [b.tok("sQ")] if (th == 0 and c < 2) else []))
            for tt in range(NTTH):
                lo, hi = tt * 512, (tt + 1) * 512
                tb0 = bcast_row(pb[0], row(rbB, 32, base + lo, base + hi),
                                [t_i3]) if c == 0 else None
                if c == 0:
                    mark_row_read(id(rbB), 32, tb0)
                    b.op("vector", lambda e, lo=lo, hi=hi: e.tensor_copy(
                        out=i3b[:, lo:hi], in_=pb[0].t[:, 0:512]),
                        inc=("sV", 1), wait=pb[0].rw())
                    pb[0].r.append(b.tok("sV"))
                    tb1 = bcast_row(pb[1], row(rbB, 64, base + lo, base + hi),
                                    [t_mi3])
                    mark_row_read(id(rbB), 64, tb1)
                    b.op("vector", lambda e, lo=lo, hi=hi: e.tensor_copy(
                        out=m3b[:, lo:hi], in_=pb[1].t[:, 0:512]),
                        inc=("sV", 1), wait=pb[1].rw())
                    pb[1].r.append(b.tok("sV"))
                b.op("vector", lambda e, c=c, lo=lo, hi=hi: e.tensor_tensor(
                    out=t3tmp[:], in0=xstg3[c % 2][:, lo:hi],
                    in1=i3b[:, lo:hi], op=ALU.mult), inc=("sV", 1),
                    wait=[td])
                t_xf = b.op("vector", lambda e, c=c, lo=lo, hi=hi:
                            e.tensor_tensor(
                    out=xf[:, c, lo:hi], in0=t3tmp[:], in1=m3b[:, lo:hi],
                    op=ALU.subtract), inc=("sV", 1))
            xst3_rd[c % 2] = t_xf
        t_xf = b.tok("sV")
        if th == 0:
            tap("xf", xf[:], t_xf)

        gbank = pb[4]
        ws = gbank.ww()
        tok = None
        for i in range(NTI):
            for c in range(DC):
                tok = b.op("tensor", lambda e, i=i, c=c, gbank=gbank: e.matmul(
                    out=gbank.t[:, i * NE:(i + 1) * NE],
                    lhsT=xf[:, c, i * 128:(i + 1) * 128],
                    rhs=gw_b[:, c * NE:(c + 1) * NE],
                    start=(c == 0), stop=(c == DC - 1), skip_group_check=True),
                    inc=("sT", 1) if c == DC - 1 else None,
                    wait=(ws + [tgw, t_xf]) if (i == 0 and c == 0) else [])
        gbank.w = tok
        tgl2 = b.op("vector", lambda e, gbank=gbank: e.tensor_copy(
            out=glog[:], in_=gbank.t[:, 0:NTI * NE]), inc=("sV", 1),
            wait=gbank.rw())
        gbank.r.append(tgl2)
        g3v = glog[:].rearrange("p (i e) -> p i e", e=NE)
        s3v = gsc2[:].rearrange("p (i e) -> p i e", e=NE)
        b3v = gsc3[:].rearrange("p (i e) -> p i e", e=NE)
        r3v = rw_f[:].rearrange("p (i e) -> p i e", e=NE)
        g1u = gsc1[:].unsqueeze(2)
        g1b = gsc1[:].unsqueeze(2).broadcast_to([128, NTI, NE])
        vdelay()
        tmx = b.op("vector", lambda e: e.tensor_reduce(
            out=g1u, in_=g3v, axis=mybir.AxisListType.X, op=ALU.max),
            inc=("sV", 1), wait=[tgl2])
        if th == 0:
            tap("glog", glog[:], tgl2)
            tap("gmx", gsc1[:], tmx)
        vdelay()
        tsb2 = b.op("vector", lambda e: e.tensor_tensor(
            out=s3v, in0=g3v, in1=g1b, op=ALU.subtract), inc=("sV", 1),
            wait=[tmx])
        tex = b.op("scalar", lambda e: e.activation(
            out=gsc2[:], in_=gsc2[:], func=AF.Exp), inc=("sS", 1), wait=[tsb2])
        if th == 0:
            tap("gexp", gsc2[:], tex)
        vdelay()
        tsm = b.op("vector", lambda e: e.tensor_reduce(
            out=g1u, in_=s3v, axis=mybir.AxisListType.X, op=ALU.add),
            inc=("sV", 1), wait=[tex])
        vdelay()
        trc = b.op("vector", lambda e: e.reciprocal(out=gsc1[:], in_=gsc1[:]),
                   inc=("sV", 1), wait=[tsm])
        vdelay()
        tsc = b.op("vector", lambda e: e.tensor_tensor(
            out=s3v, in0=s3v, in1=g1b, op=ALU.mult), inc=("sV", 1), wait=[trc])
        vdelay()
        tgb2 = b.op("vector", lambda e: e.tensor_tensor(
            out=gsc3[:], in0=gsc2[:], in1=gbr_s[:], op=ALU.add),
            inc=("sV", 1), wait=[tsc, t_gwl])
        vdelay()
        tm1 = b.op("vector", lambda e: e.tensor_reduce(
            out=g1u, in_=b3v, axis=mybir.AxisListType.X, op=ALU.max),
            inc=("sV", 1), wait=[tgb2])
        vdelay()
        tk1 = b.op("vector", lambda e: e.tensor_tensor(
            out=r3v, in0=b3v, in1=g1b, op=ALU.is_ge), inc=("sV", 1),
            wait=[tm1])
        vdelay()
        tb2m = b.op("vector", lambda e: e.scalar_tensor_tensor(
            out=gsc3[:], in0=rw_f[:], scalar=-1e30, in1=gsc3[:],
            op0=ALU.mult, op1=ALU.add), inc=("sV", 1), wait=[tk1])
        vdelay()
        tm2 = b.op("vector", lambda e: e.tensor_reduce(
            out=g1u, in_=b3v, axis=mybir.AxisListType.X, op=ALU.max),
            inc=("sV", 1), wait=[tb2m])
        vdelay()
        tk2 = b.op("vector", lambda e: e.tensor_tensor(
            out=b3v, in0=b3v, in1=g1b, op=ALU.is_ge), inc=("sV", 1),
            wait=[tm2])
        vdelay()
        tkm = b.op("vector", lambda e: e.tensor_tensor(
            out=rw_f[:], in0=rw_f[:], in1=gsc3[:], op=ALU.add), inc=("sV", 1),
            wait=[tk2])
        vdelay()
        trw = b.op("vector", lambda e: e.tensor_tensor(
            out=rw_f[:], in0=rw_f[:], in1=gsc2[:], op=ALU.mult),
            inc=("sV", 1))
        if th == 0:
            tap("rw", rw_f[:], trw)
        vdelay()
        trwb = b.op("vector", lambda e: e.tensor_copy(
            out=rw_b16[:], in_=rw_f[:]), inc=("sV", 1), wait=[trw])
        t_rwfm = None
        for g0 in range(0, NTI, 8):
            gn = min(8, NTI - g0)
            ws = pbT.ww()
            tok = None
            for i in range(gn):
                it = g0 + i
                tok = b.op("tensor", lambda e, it=it, i=i: e.transpose(
                    out=pbT.t[0:NE, i * 128:(i + 1) * 128],
                    in_=rw_b16[:, it * NE:(it + 1) * NE], identity=identb[:]),
                    inc=("sT", 1) if i == gn - 1 else None,
                    wait=(ws + [trwb, t_consts]) if i == 0 else [])
            pbT.w = tok
            t_rwfm = b.op("vector", lambda e, g0=g0, gn=gn: e.tensor_copy(
                out=rw_fm[0:NE, g0 * 128:(g0 + gn) * 128],
                in_=pbT.t[0:NE, 0:gn * 128]), inc=("sV", 1), wait=pbT.rw())
            pbT.r.append(t_rwfm)

        t_ymi = None
        for j in range(DC):
            for tt in range(NTTH):
                lo, hi = tt * 512, (tt + 1) * 512
                bank = pb[2 + tt % 2]
                ws = bank.ww()
                tok = b.op("tensor", lambda e, j=j, lo=lo, hi=hi, bank=bank:
                           e.matmul(
                    out=bank.t[:, 0:512],
                    lhsT=b2_b[0:NE, j * 128:(j + 1) * 128],
                    rhs=rw_fm[0:NE, lo:hi], start=True, stop=True),
                    inc=("sT", 1), wait=ws + [tb2c, t_rwfm])
                bank.w = tok
                t_ymi = b.op("vector", lambda e, j=j, lo=lo, hi=hi, bank=bank:
                             e.tensor_scalar(
                    out=ymoe[:, j, lo:hi], in0=bank.t[:, 0:512],
                    scalar1=sbias2[:, j:j + 1], scalar2=None, op0=ALU.add),
                    inc=("sV", 1), wait=bank.rw() + [t_gwl, ymoe_scr_rd])
                bank.r.append(t_ymi)
        ymoe_wr = b.tok("sV")

        for ex in range(NE + 1):
            shared = ex == NE
            if shared:
                trb = b.op("vector", lambda e: e.memset(rwb_sb[:], 1.0),
                           inc=("sV", 1), wait=[rwb_rd])
            else:
                trb = None
                for tt in range(NTTH):
                    lo, hi = tt * 512, (tt + 1) * 512
                    bank = pb[6]
                    ws = bank.ww()
                    tok = b.op("tensor", lambda e, ex=ex, lo=lo, hi=hi,
                               bank=bank: e.matmul(
                        out=bank.t[:, 0:512],
                        lhsT=selrw_s[0:NE, ex * 128:(ex + 1) * 128],
                        rhs=rw_fm[0:NE, lo:hi], start=True, stop=True),
                        inc=("sT", 1), wait=ws + [t_rwfm, t_consts])
                    bank.w = tok
                    trb = b.op("vector", lambda e, lo=lo, hi=hi, bank=bank:
                               e.tensor_copy(
                        out=rwb_sb[:, lo:hi], in_=bank.t[:, 0:512]),
                        inc=("sV", 1),
                        wait=bank.rw() + ([rwb_rd] if tt == 0 else []))
                    bank.r.append(trb)
            if shared:
                tbl = t_gwl
                bb1, bb3 = sbias1, sbias3
            else:
                b.op("sync", lambda e, ex=ex: e.dma_start(
                    out=bias1[:], in_=b1_d[ex]), inc=("sW", 16),
                    wait=[bias_rd] + (pg3 if ex == 0 else []))
                tbl = b.op("sync", lambda e, ex=ex: e.dma_start(
                    out=bias3[:], in_=b3_d[ex]), inc=("sW", 16))
                bb1, bb3 = bias1, bias3
            def w13(j, shared=shared, ex=ex):
                if shared:
                    return (sw1_d[:, j * 128:(j + 1) * 128],
                            sw3_d[:, j * 128:(j + 1) * 128])
                return (w1e[ex][:, j * 128:(j + 1) * 128],
                        w3e[ex][:, j * 128:(j + 1) * 128])

            pf13 = {}
            a1, a3 = w13(0)
            pr0 = wflip13[0] % 2
            pf13[0] = (stream_tile(a1, 0, 3 * pr0, DC),
                       stream_tile(a3, 1, 1 + 3 * pr0, DC), pr0)
            wflip13[0] += 1

            def w2t(j2, shared=shared, ex=ex):
                return sw2_d[:, j2 * 128:(j2 + 1) * 128] if shared else \
                    w2e[ex][:, j2 * 128:(j2 + 1) * 128]

            pf2 = {}
            pr0w2 = wflip2[0] % 2
            pf2[0] = (stream_tile(w2t(0), 0, 2 + 3 * pr0w2, HFC), pr0w2)
            wflip2[0] += 1
            tsil_last = None
            for j in range(HFC):
                if j + 1 < HFC:
                    a1n, a3n = w13(j + 1)
                    prn = wflip13[0] % 2
                    pf13[j + 1] = (stream_tile(a1n, 0, 3 * prn, DC),
                                   stream_tile(a3n, 1, 1 + 3 * prn, DC), prn)
                    wflip13[0] += 1
                tc1, tc3, prj = pf13.pop(j)
                sA, sB = 3 * prj, 1 + 3 * prj
                tpr = None
                for tt in range(NTTH):
                    lo, hi = tt * 512, (tt + 1) * 512
                    bankA, bankB = pb[2], pb[3]
                    mm_acc(bankA,
                           lambda c, sA=sA: wbf[sA][:, c * 128:(c + 1) * 128],
                           lambda c, lo=lo, hi=hi: xf[:, c, lo:hi], DC,
                           fw=[tc1, t_xf])
                    mm_acc(bankB,
                           lambda c, sB=sB: wbf[sB][:, c * 128:(c + 1) * 128],
                           lambda c, lo=lo, hi=hi: xf[:, c, lo:hi], DC,
                           fw=[tc3, t_xf])
                    tga = b.op("scalar", lambda e, j=j, bb1=bb1, bankA=bankA:
                               e.add(
                        out=ga_t[:], in_=bankA.t[:, 0:512],
                        add=bb1[:, j:j + 1]), inc=("sS", 1),
                        wait=bankA.rw() + [tbl, gab_rd])
                    bankA.r.append(tga)
                    tgb_ = b.op("vector", lambda e, j=j, bb3=bb3, bankB=bankB:
                                e.tensor_scalar(
                        out=gb_t[:], in0=bankB.t[:, 0:512],
                        scalar1=bb3[:, j:j + 1], scalar2=None, op0=ALU.add),
                        inc=("sV", 1), wait=bankB.rw() + [tbl, gab_rd])
                    bankB.r.append(tgb_)
                    tpr = b.op("gpsimd", lambda e, lo=lo, hi=hi: e.tensor_tensor(
                        out=hprod[:, lo:hi], in0=ga_t[:], in1=gb_t[:],
                        op=ALU.mult), inc=("sP", 1),
                        wait=[tga, tgb_, hsil_rd])
                    gab_rd = tpr
                wbf_tok[sA] = tpr
                wbf_tok[sB] = tpr
                tsl = b.op("scalar", lambda e, j=j: e.activation(
                    out=hsil[:], in_=hprod[:], func=AF.Silu), inc=("sS", 1),
                    wait=[tpr, hsil_rd])
                hsil_rd = tsl
                tsil_last = b.op("vector", lambda e, j=j: e.tensor_tensor(
                    out=he[:, j, :], in0=hsil[:], in1=rwb_sb[:], op=ALU.mult),
                    inc=("sV", 1), wait=[tsl, trb, he_rd])
                hsil_rd = tsil_last
            rwb_rd = tsil_last
            bias_rd = tsil_last
            t_he = b.tok("sV")
            if ex == 0 and th == 0:
                tap("he", he[:], t_he)
            t_acc = None
            for j2 in range(DC):
                if j2 + 1 < DC:
                    prn2 = wflip2[0] % 2
                    pf2[j2 + 1] = (stream_tile(w2t(j2 + 1), (j2 + 1) % 2,
                                               2 + 3 * prn2, HFC), prn2)
                    wflip2[0] += 1
                tc2, prj2 = pf2.pop(j2)
                s2 = 2 + 3 * prj2
                for tt in range(NTTH):
                    lo, hi = tt * 512, (tt + 1) * 512
                    bank = pb[4 + tt % 2]
                    mm_acc(bank,
                           lambda c, s2=s2: wbf[s2][:, c * 128:(c + 1) * 128],
                           lambda c, lo=lo, hi=hi: he[:, c, lo:hi], HFC,
                           fw=[tc2, t_he])
                    t_acc = b.op("vector", lambda e, j2=j2, lo=lo, hi=hi,
                                 bank=bank: e.tensor_tensor(
                        out=ymoe[:, j2, lo:hi], in0=ymoe[:, j2, lo:hi],
                        in1=bank.t[:, 0:512], op=ALU.add), inc=("sV", 1),
                        wait=bank.rw() + [ymoe_wr])
                    bank.r.append(t_acc)
                wbf_tok[s2] = t_acc
            he_rd = t_acc
            ymoe_wr = t_acc
        t_ym = b.tok("sV")
        if th == 0:
            tap("ymoe", ymoe[:], t_ym)

        for c in range(DC):
            tlast = None
            for tt in range(NTTH):
                lo, hi = tt * 512, (tt + 1) * 512
                tcb = b.op("vector", lambda e, c=c, lo=lo, hi=hi: e.tensor_copy(
                    out=sqb[1][:], in_=ymoe[:, c, lo:hi]), inc=("sV", 1),
                    wait=[t_ym, sq_rd[1]])
                tsq = b.op("scalar", lambda e, c=c, lo=lo, hi=hi: e.activation(
                    out=sqb[0][:], in_=ymoe[:, c, lo:hi], func=AF.Square),
                    inc=("sS", 1), wait=[t_ym, ygsq_rd4])
                bank = pb[4 + tt % 2]
                ws = bank.ww()
                b.op("tensor", lambda e, bank=bank: e.matmul(
                    out=bank.t[0:1, 0:512], lhsT=ones_b[:], rhs=sqb[1][:],
                    start=True, stop=True, skip_group_check=True),
                    inc=("sT", 1), wait=ws + [tcb, t_init])
                bank.w = b.tok("sT")
                sq_rd[1] = bank.w
                b.op("vector", lambda e, base=base, lo=lo, hi=hi, bank=bank:
                     e.tensor_tensor(
                    out=row(rbC, 32, base + lo, base + hi),
                    in0=row(rbC, 32, base + lo, base + hi),
                    in1=bank.t[0:1, 0:512], op=ALU.add), inc=("sV", 1),
                    wait=bank.rw() + [tz4])
                bank.r.append(b.tok("sV"))
                bank2 = pb[6]
                ws2 = bank2.ww()
                tq2 = b.op("tensor", lambda e, bank2=bank2: e.matmul(
                    out=bank2.t[0:1, 0:512], lhsT=ones_b[:], rhs=sqb[0][:],
                    start=True, stop=True, skip_group_check=True),
                    inc=("sT", 1), wait=ws2 + [tsq])
                bank2.w = tq2
                ygsq_rd4 = tq2
                tlast = b.op("vector", lambda e, base=base, lo=lo, hi=hi,
                             bank2=bank2: e.tensor_tensor(
                    out=row(rbC, 64, base + lo, base + hi),
                    in0=row(rbC, 64, base + lo, base + hi),
                    in1=bank2.t[0:1, 0:512], op=ALU.add), inc=("sV", 1),
                    wait=bank2.rw() + [tz4b])
                bank2.r.append(tlast)
            b.op("scalar", lambda e, c=c, base=base: e.dma_start(
                out=ymoe_scr[c * 128:(c + 1) * 128, base:base + TH],
                in_=ymoe[:, c, :]), inc=("sQ", 16), wait=[t_ym, tlast])
        ymoe_scr_rd = b.tok("sQ")

    t_i4 = t_mi4 = None
    for tt in range(NTT):
        lo, hi = tt * 512, (tt + 1) * 512
        t_i4, t_mi4 = rows_span(rbC[32:33, lo:hi], rbC[64:65, lo:hi], 512,
                                (rbB, 32, lo, hi), (rbB, 64, lo, hi),
                                CP_I1024, True)

    ymstr = [ymoe[:, 0, :], ymoe[:, 1, :]]
    x1str = [ymoe[:, 2, :], ymoe[:, 3, :]]
    obuf = [ymoe[:, 4, :], ymoe[:, 5, :]]
    t_ymall = b.tok("sQ")
    str_rd = {}
    for c in range(DC):
        for th in range(2):
            base = th * TH
            k = th
            td = b.op("sync", lambda e, c=c, base=base, k=k: e.dma_start(
                out=x1str[k],
                in_=x1_scr[c * 128:(c + 1) * 128, base:base + TH]),
                inc=("sW", 16),
                wait=[str_rd.get(("x", k)), t_ymall] +
                (guard() if c == 0 else []))
            tym = b.op("sync", lambda e, c=c, base=base, k=k: e.dma_start(
                out=ymstr[k],
                in_=ymoe_scr[c * 128:(c + 1) * 128, base:base + TH]),
                inc=("sW", 16),
                wait=[str_rd.get(("y", k)), t_ymall] +
                (guard() if c == 0 else []))
            tob = None
            for tt in range(NTTH):
                lo, hi = tt * 512, (tt + 1) * 512
                bcast_row(pb[0], row(rbB, 32, base + lo, base + hi), [t_i4])
                bcast_row(pb[1], row(rbB, 64, base + lo, base + hi), [t_mi4])
                tv1 = b.op("vector", lambda e, k=k, lo=lo, hi=hi:
                           e.tensor_tensor(
                    out=t3tmp[:], in0=ymstr[k][:, lo:hi],
                    in1=pb[0].t[:, 0:512], op=ALU.mult), inc=("sV", 1),
                    wait=pb[0].rw() + [tym])
                pb[0].r.append(tv1)
                tv2 = b.op("vector", lambda e, lo=lo, hi=hi: e.tensor_tensor(
                    out=t3tmp[:], in0=t3tmp[:], in1=pb[1].t[:, 0:512],
                    op=ALU.subtract), inc=("sV", 1), wait=pb[1].rw())
                pb[1].r.append(tv2)
                tob = b.op("vector", lambda e, k=k, c=c, lo=lo, hi=hi:
                           e.tensor_tensor(
                    out=obuf[c % 2][:, lo:hi], in0=t3tmp[:],
                    in1=x1str[k][:, lo:hi], op=ALU.add), inc=("sV", 1),
                    wait=[td, str_rd.get(("o", c % 2))])
            str_rd[("x", k)] = tob
            str_rd[("y", k)] = tob
            tdo = b.op("scalar", lambda e, c=c, base=base: e.dma_start(
                out=out_fm[c * 128:(c + 1) * 128, base:base + TH],
                in_=obuf[c % 2][:]), inc=("sO", 16), wait=[tob])
            str_rd[("o", c % 2)] = tdo

    return finish(b, nc)


def finish(b, nc):
    for st in reversed(b.stacks):
        st.close()
    for s in ("sO", "sQ", "sG"):
        h, v, _ = b.sems[s]
        if v > 0:
            b.ops["scalar"].append(lambda e, h=h, v=v: e.wait_ge(h, v))
    block = b.es.enter_context(nc.Block())
    b.finalize(block)
    b.es.close()
    return nc


def host_prep(inputs, T=2048):
    f32 = np.float32
    g = lambda k: np.asarray(inputs[k])
    cm = {}
    cm["wqT"] = np.ascontiguousarray(g("L0_attn_W_in")[:1024].T)
    cm["wzT"] = np.ascontiguousarray(g("L0_attn_W_zxbcdt").T)
    cm["woT"] = np.ascontiguousarray(g("L0_attn_W_out").T)
    cm["convw_d"] = np.ascontiguousarray(
        g("L0_attn_conv_w").reshape(18, 128, 4).transpose(1, 0, 2))
    cm["convb_d"] = np.ascontiguousarray(g("L0_attn_conv_b").reshape(18, 128).T)
    cm["dtb_d"] = np.zeros((128, 1), f32)
    cm["dtb_d"][:NH, 0] = g("L0_attn_dt_bias")
    cm["alog_d"] = np.zeros((128, 1), f32)
    cm["alog_d"][:NH, 0] = g("L0_attn_A_log")
    cm["drep_d"] = np.ascontiguousarray(
        np.repeat(g("L0_attn_D_param"), 64).reshape(16, 128).T)
    cm["npw_d"] = np.ascontiguousarray(g("L0_attn_norm_pre_w").reshape(DC, 128).T)
    cm["ngw_d"] = np.ascontiguousarray(
        g("L0_attn_norm_gated_w").reshape(DIC, 128).T)
    cm["gw_d"] = np.ascontiguousarray(g("L0_ffn_gate_w").T)
    cm["gb_d"] = np.ascontiguousarray(np.tile(g("L0_ffn_gate_b"), (128, 16)))
    cm["w1e"] = g("L0_ffn_w1")
    cm["w3e"] = g("L0_ffn_w3")
    cm["w2e"] = g("L0_ffn_w2")
    cm["b1_d"] = np.ascontiguousarray(
        g("L0_ffn_b1").reshape(NE, HFC, 128).transpose(0, 2, 1))
    cm["b3_d"] = np.ascontiguousarray(
        g("L0_ffn_b3").reshape(NE, HFC, 128).transpose(0, 2, 1))
    cm["b2_d"] = np.zeros((128, 1024), f32)
    cm["b2_d"][:NE] = g("L0_ffn_b2")
    cm["sw1_d"] = g("L0_ffn_sw1")
    cm["sw3_d"] = g("L0_ffn_sw3")
    cm["sw2_d"] = g("L0_ffn_sw2")
    cm["sb1_d"] = np.ascontiguousarray(g("L0_ffn_sb1").reshape(HFC, 128).T)
    cm["sb3_d"] = np.ascontiguousarray(g("L0_ffn_sb3").reshape(HFC, 128).T)
    cm["sb2_d"] = np.ascontiguousarray(g("L0_ffn_sb2").reshape(DC, 128).T)
    cm["identb_d"] = np.eye(128, dtype=f32).astype(BF)
    cm["identf_d"] = np.eye(128, dtype=f32)
    inv_freq = (1.0 / 10000.0 ** (np.arange(0, 64, 2, dtype=np.float64) / 64.0))
    t_ = np.arange(T, dtype=np.float64)
    freqs = t_[:, None] * inv_freq[None, :]
    cos = np.cos(freqs).T.astype(f32)
    sin = np.sin(freqs).T.astype(f32)
    ropeC = np.zeros((128, T), f32)
    ropeS = np.zeros((128, T), f32)
    for blk in range(2):
        base = blk * 64
        ropeC[base:base + 32] = cos
        ropeC[base + 32:base + 64] = cos
        ropeS[base:base + 32] = -sin
        ropeS[base + 32:base + 64] = sin
    cm["ropeC_d"] = ropeC
    cm["ropeS_d"] = ropeS
    mneg = np.zeros((128, 128), f32)
    mneg[np.arange(128)[:, None] > np.arange(128)[None, :]] = -1e30
    cm["maskneg_d"] = mneg.astype(BF)
    selrw = np.zeros((128, NE * 128), f32)
    for e in range(NE):
        selrw[e, e * 128:(e + 1) * 128] = 1.0
    cm["selrw_d"] = selrw.astype(BF)
    for k in list(cm):
        v = cm[k]
        if v.dtype == np.float64:
            cm[k] = v.astype(f32)
    x = g("L0_attn_x")
    in_maps = []
    for i in range(8):
        m = dict(cm)
        m["xT"] = np.ascontiguousarray(x[i][:T].T)
        in_maps.append(m)
    return in_maps


def kernel(**inputs):
    in_maps = host_prep(inputs)
    nc = build_core()
    res = run_bass_kernel_spmd(nc, in_maps, core_ids=list(range(8)))
    outs = [np.ascontiguousarray(r["out_fm"].T) for r in res.results]
    return np.stack(outs, axis=0).astype(np.float32)
